# version 06
# baseline (speedup 1.0000x reference)
import numpy as np
import ml_dtypes
from contextlib import ExitStack

import concourse.bass as bass
import concourse.mybir as mybir
from concourse.bass_utils import run_bass_kernel_spmd

F32 = mybir.dt.float32
BF16 = mybir.dt.bfloat16
I32 = mybir.dt.int32
AF = mybir.ActivationFunctionType
ALU = mybir.AluOpType
BF16NP = ml_dtypes.bfloat16

D = 2048
DC = D // 128
S = 8192
NCORES = 8
HEADS = 16
HD = 128
NEG = -1e30

ENGINES = ("sync", "scalar", "vector", "tensor", "gpsimd")


class _Holder:
    __slots__ = ("incs",)

    def __init__(self):
        self.incs = []

    def then_inc(self, sem, amt=None):
        self.incs.append((sem, amt))
        return self


class _EngRec:

    def __init__(self, prog, name):
        self._prog = prog
        self._name = name

    def __getattr__(self, attr):
        prog, name = self._prog, self._name

        def call(*args, **kwargs):
            h = _Holder()

            def fn(e, _attr=attr, _args=args, _kwargs=kwargs, _h=h):
                inst = getattr(e, _attr)(*_args, **_kwargs)
                for (sem, amt) in _h.incs:
                    if amt is None:
                        inst.then_inc(sem)
                    else:
                        inst.then_inc(sem, amt)

            prog.fns[name].append(fn)
            return h

        return call


class Prog:
    def __init__(self):
        self.fns = {e: [] for e in ENGINES}
        self.sync = _EngRec(self, "sync")
        self.scalar = _EngRec(self, "scalar")
        self.vector = _EngRec(self, "vector")
        self.tensor = _EngRec(self, "tensor")
        self.gpsimd = _EngRec(self, "gpsimd")

    def eng(self, name):
        return getattr(self, name)

    def replay(self, block):
        @block.sync
        def _(e):
            for fn in self.fns["sync"]:
                fn(e)

        @block.scalar
        def _(e):
            for fn in self.fns["scalar"]:
                fn(e)

        @block.vector
        def _(e):
            for fn in self.fns["vector"]:
                fn(e)

        @block.tensor
        def _(e):
            for fn in self.fns["tensor"]:
                fn(e)

        @block.gpsimd
        def _(e):
            for fn in self.fns["gpsimd"]:
                fn(e)


class Sem:

    __slots__ = ("h", "name", "_cnt", "base")

    def __init__(self, handle, cnt, name=""):
        self.h = handle
        self._cnt = cnt
        self.base = cnt[0]
        self.name = name

    @property
    def n(self):
        return self._cnt[0]

    @n.setter
    def n(self, v):
        self._cnt[0] = v


import re as _re


class Ctx:

    def __init__(self, nc, stack):
        self.nc = nc
        self.stack = stack
        self.p = Prog()
        self._semcnt = 0
        self._sempool = {}

    def sem(self, name=None):
        name = name or f"s{self._semcnt}"
        self._semcnt += 1
        key = _re.sub(r"\d+", "", name)
        if key not in self._sempool:
            self._sempool[key] = (
                self.stack.enter_context(self.nc.semaphore(key)), [0])
        h, cnt = self._sempool[key]
        return Sem(h, cnt, key)

    def sbuf(self, name, shape, dtype):
        return self.stack.enter_context(self.nc.sbuf_tensor(name, shape, dtype))

    def psum(self, name, shape, dtype=F32):
        return self.stack.enter_context(self.nc.psum_tensor(name, shape, dtype))

    def dma(self, qname, out, in_, sem=None, **kw):
        i = self.p.eng(qname).dma_start(out=out, in_=in_, **kw)
        if sem is not None:
            i.then_inc(sem.h, 16)
            sem.n += 16
        return i

    def wait(self, qname, sem, val=None):
        self.p.eng(qname).wait_ge(sem.h, sem.n if val is None else val)


def bcast_col(ap_col, nfree):
    return bass.AP(ap_col.tensor, ap_col.offset, [list(ap_col.ap[0]), [0, nfree]])


def chunks512(t0, n):
    out = []
    while n > 0:
        sz = min(512, n)
        out.append((t0, sz))
        t0 += sz
        n -= sz
    return out


def host_tile_lhsT(A):
    K, M = A.shape
    assert K % 128 == 0 and M % 128 == 0
    H = np.ascontiguousarray(
        A.reshape(K // 128, 128, M // 128, 128).transpose(2, 1, 0, 3)
        .reshape(M // 128, 128, (K // 128) * 128)
    ).astype(BF16NP)
    return H


def f32c(x):
    return np.ascontiguousarray(x, dtype=np.float32)


class Pool:

    def __init__(self, c: Ctx, ln=True):
        self.c = c
        self.ps_a = [c.psum("ps_a0", [128, 512], F32), c.psum("ps_a1", [128, 512], F32)]
        self.ps_b = [c.psum("ps_b0", [128, 512], F32), c.psum("ps_b1", [128, 512], F32)]
        self.ps_s1 = c.psum("ps_s1", [128, 512], F32)
        self.ps_s2 = c.psum("ps_s2", [128, 512], F32)
        self.ps_c = [c.psum("ps_c0", [128, 512], F32), c.psum("ps_c1", [128, 512], F32)]
        self.ones_col = c.sbuf("ones_col", [128, 1], F32)
        self.ones_col_bf = c.sbuf("ones_col_bf", [128, 1], BF16)
        self.ones128 = c.sbuf("ones128", [128, 128], F32)
        self.ones128_bf = c.sbuf("ones128_bf", [128, 128], BF16)
        self.ones_row = c.sbuf("ones_row", [1, 128], F32)
        self.eps1 = c.sbuf("eps1", [1, 1], F32)
        self.ln = ln
        if ln:
            self.ones_row_t = c.sbuf("ones_row_t", [1, 512], F32)
            self.stat = c.sbuf("stat", [1, 4 * 512], F32)
            self.sq = c.sbuf("sq", [128, 2 * 512], F32)
            self.ab = c.sbuf("ab", [128, 2 * 512], F32)
        self.s_cst = c.sem("cst")

    def emit_consts(self):
        c = self.c
        p = c.p
        p.gpsimd.memset(self.ones_col[:], 1.0)
        p.gpsimd.memset(self.ones_col_bf[:], 1.0)
        p.gpsimd.memset(self.ones128[:], 1.0)
        p.gpsimd.memset(self.ones128_bf[:], 1.0)
        p.gpsimd.memset(self.eps1[:], 1e-5)
        i = p.gpsimd.memset(self.ones_row[:], 1.0)
        if self.ln:
            i = p.gpsimd.memset(self.ones_row_t[:], 1.0)
        i.then_inc(self.s_cst.h, 1)
        self.s_cst.n += 1
        for e in ("scalar", "vector", "tensor"):
            c.wait(e, self.s_cst)


def emit_ln(c: Ctx, pool: Pool, name, x_sb, xT, xt0, xn_sb, xnT, xnt0, nT,
            s_x=None, eps=1e-5, out_dtype_note="bf16"):
    p = c.p
    s_sq = c.sem("LNsq")
    s_mm = c.sem("LNmm")
    s_st = c.sem("LNst")
    s_bc = c.sem("LNbc")
    s_ab = c.sem("LNab")
    s_ap = c.sem("LNap")

    if s_x is not None:
        for e in ("scalar", "tensor", "vector"):
            c.wait(e, s_x)

    for (t0, tsz) in chunks512(0, nT):
        mm_base = s_mm.n
        for dc in range(DC):
            slot = dc % 2
            xsl = x_sb[:, dc * xT + xt0 + t0: dc * xT + xt0 + t0 + tsz]
            if dc >= 2:
                p.scalar.wait_ge(s_mm.h, mm_base + dc - 1)
            p.scalar.activation(out=pool.sq[:, slot * 512: slot * 512 + tsz],
                                in_=xsl, func=AF.Square).then_inc(s_sq.h, 1)
            s_sq.n += 1
            p.tensor.wait_ge(s_sq.h, s_sq.n)
            p.tensor.matmul(pool.ps_s1[:, 0:tsz], pool.ones128[:, :], xsl,
                            start=(dc == 0), stop=(dc == DC - 1))
            p.tensor.matmul(pool.ps_s2[:, 0:tsz], pool.ones128[:, :],
                            pool.sq[:, slot * 512: slot * 512 + tsz],
                            start=(dc == 0), stop=(dc == DC - 1)).then_inc(s_mm.h, 1)
            s_mm.n += 1
        mean = pool.stat[0:1, 0:tsz]
        ex2 = pool.stat[0:1, 512:512 + tsz]
        a_v = pool.stat[0:1, 1024:1024 + tsz]
        b_v = pool.stat[0:1, 1536:1536 + tsz]
        p.vector.wait_ge(s_mm.h, s_mm.n)
        p.vector.tensor_scalar_mul(out=mean, in0=pool.ps_s1[0:1, 0:tsz], scalar1=1.0 / D)
        p.vector.tensor_scalar_mul(out=ex2, in0=pool.ps_s2[0:1, 0:tsz],
                                   scalar1=1.0 / D).then_inc(s_st.h, 1)
        s_st.n += 1
        p.vector.wait_ge(s_st.h, s_st.n)
        p.vector.tensor_tensor(out=a_v, in0=mean, in1=mean, op=ALU.mult).then_inc(s_st.h, 1)
        s_st.n += 1
        p.vector.wait_ge(s_st.h, s_st.n)
        p.vector.tensor_tensor(out=ex2, in0=ex2, in1=a_v, op=ALU.subtract).then_inc(s_st.h, 1)
        s_st.n += 1
        p.scalar.wait_ge(s_st.h, s_st.n)
        p.scalar.activation(out=a_v, in_=ex2, func=AF.Ln,
                            bias=pool.eps1[0:1, 0:1]).then_inc(s_st.h, 1)
        s_st.n += 1
        p.scalar.wait_ge(s_st.h, s_st.n)
        p.scalar.activation(out=a_v, in_=a_v, func=AF.Exp, scale=-0.5).then_inc(s_st.h, 1)
        s_st.n += 1
        p.vector.wait_ge(s_st.h, s_st.n)
        p.vector.tensor_tensor(out=b_v, in0=mean, in1=a_v, op=ALU.mult).then_inc(s_st.h, 1)
        s_st.n += 1
        p.vector.wait_ge(s_st.h, s_st.n)
        p.vector.tensor_scalar_mul(out=b_v, in0=b_v, scalar1=-1.0).then_inc(s_st.h, 1)
        s_st.n += 1
        p.tensor.wait_ge(s_st.h, s_st.n)
        p.tensor.matmul(pool.ps_c[0][:, 0:tsz], pool.ones_row[:, :], a_v, start=True, stop=True)
        p.tensor.matmul(pool.ps_c[1][:, 0:tsz], pool.ones_row[:, :], b_v, start=True, stop=True).then_inc(s_bc.h, 1)
        s_bc.n += 1
        p.vector.wait_ge(s_bc.h, s_bc.n)
        p.vector.tensor_copy(out=pool.ab[:, 0:tsz], in_=pool.ps_c[0][:, 0:tsz])
        p.vector.tensor_copy(out=pool.ab[:, 512:512 + tsz], in_=pool.ps_c[1][:, 0:tsz]).then_inc(s_ab.h, 1)
        s_ab.n += 1
        p.vector.wait_ge(s_ab.h, s_ab.n)
        for dc in range(DC):
            slot = dc % 2
            xsl = x_sb[:, dc * xT + xt0 + t0: dc * xT + xt0 + t0 + tsz]
            osl = xn_sb[:, dc * xnT + xnt0 + t0: dc * xnT + xnt0 + t0 + tsz]
            p.vector.tensor_tensor(out=pool.sq[:, slot * 512: slot * 512 + tsz],
                                   in0=xsl, in1=pool.ab[:, 0:tsz], op=ALU.mult)
            p.vector.tensor_tensor(out=osl, in0=pool.sq[:, slot * 512: slot * 512 + tsz],
                                   in1=pool.ab[:, 512:512 + tsz], op=ALU.add).then_inc(s_ap.h, 1)
            s_ap.n += 1
    return s_ap


def emit_ffn0_tb(c: Ctx, pool: Pool, name, x_sb, xT, xt0, xn_sb, xnT, xnt0, nT,
                 bufs, wdr, s_xn, half=0.5):
    assert nT <= 512
    p = c.p
    NCH = 64
    HCH = 16
    w0b, w1b, h_sb = bufs["w0b"], bufs["w1b"], bufs["h_sb"]
    bias0c, bias1r = bufs["bias0c"], bufs["bias1r"]

    s_w0 = c.sem(f"{name}_w0")
    s_w1 = c.sem(f"{name}_w1")
    s_hmm = c.sem(f"{name}_hmm")
    s_gel = c.sem(f"{name}_gel")
    s_q = c.sem(f"{name}_q")
    s_emm = c.sem(f"{name}_emm")
    s_ev = c.sem(f"{name}_ev")

    xn_gate = s_xn.n
    p.tensor.wait_ge(s_xn.h, xn_gate)

    for ncb in range(NCH):
        buf = ncb % 2
        if ncb >= 2:
            c.p.sync.wait_ge(s_hmm.h, s_hmm.base + ncb - 1)
        c.dma("sync", w0b[buf][:, :], wdr["w0h"][ncb], sem=s_w0)
        pb = ncb % 2
        p.tensor.wait_ge(s_w0.h, s_w0.base + (ncb + 1) * 16)
        if ncb >= 2:
            p.tensor.wait_ge(s_gel.h, s_gel.base + ncb - 1)
        for dc in range(DC):
            i = p.tensor.matmul(pool.ps_a[pb][:, 0:nT],
                                w0b[buf][:, dc * 128:(dc + 1) * 128],
                                xn_sb[:, dc * xnT + xnt0: dc * xnT + xnt0 + nT],
                                start=(dc == 0), stop=(dc == DC - 1))
        i.then_inc(s_hmm.h, 1)
        s_hmm.n += 1
        p.scalar.wait_ge(s_hmm.h, s_hmm.n)
        p.scalar.activation(out=h_sb[:, ncb * nT: ncb * nT + nT],
                            in_=pool.ps_a[pb][:, 0:nT], func=AF.Gelu,
                            bias=bias0c[:, ncb:ncb + 1], scale=1.0).then_inc(s_gel.h, 1)
        s_gel.n += 1

    p.tensor.wait_ge(s_gel.h, s_gel.n)
    for k in range(4):
        c.dma("sync", w1b[k % 4][:, :], wdr["w1q"][k], sem=s_w1)
    for hcb in range(HCH):
        pb = hcb % 2
        if hcb >= 2:
            p.tensor.wait_ge(s_ev.h, s_ev.base + hcb - 1)
        for qi in range(4):
            k = 4 * hcb + qi
            buf = k % 4
            p.tensor.wait_ge(s_w1.h, s_w1.base + (k + 1) * 16)
            for j in range(16):
                ncb = qi * 16 + j
                i = p.tensor.matmul(pool.ps_b[pb][:, 0:nT],
                                    w1b[buf][:, j * 128:(j + 1) * 128],
                                    h_sb[:, ncb * nT: ncb * nT + nT],
                                    start=(ncb == 0), stop=False)
            i.then_inc(s_q.h, 1)
            s_q.n += 1
            if k + 4 < 4 * HCH:
                c.p.sync.wait_ge(s_q.h, s_q.base + k + 1)
                c.dma("sync", w1b[buf][:, :], wdr["w1q"][k + 4], sem=s_w1)
        i = p.tensor.matmul(pool.ps_b[pb][:, 0:nT],
                            bias1r[0:1, hcb * 128:(hcb + 1) * 128],
                            pool.ones_row_t[0:1, 0:nT],
                            start=False, stop=True)
        i.then_inc(s_emm.h, 1)
        s_emm.n += 1
        p.vector.wait_ge(s_emm.h, s_emm.n)
        p.vector.scalar_tensor_tensor(
            out=x_sb[:, hcb * xT + xt0: hcb * xT + xt0 + nT],
            in0=pool.ps_b[pb][:, 0:nT], scalar=half,
            in1=x_sb[:, hcb * xT + xt0: hcb * xT + xt0 + nT],
            op0=ALU.mult, op1=ALU.add).then_inc(s_ev.h, 1)
        s_ev.n += 1
    return s_ev


def _blocks(T, TB):
    out = []
    t0 = 0
    while t0 < T:
        out.append((t0, min(TB, T - t0)))
        t0 += min(TB, T - t0)
    return out


def emit_ffn0(c: Ctx, pool: Pool, name, x_sb, T, wdr, bufs, s_x, TB=512, half=0.5):
    sevs = []
    for (i, (t0, tb)) in enumerate(_blocks(T, TB)):
        s_ap = emit_ln(c, pool, f"{name}_ln{i}", x_sb, T, t0, bufs["xn_sb"], TB, 0, tb, s_x=s_x)
        s_ev = emit_ffn0_tb(c, pool, f"{name}_tb{i}", x_sb, T, t0,
                            bufs["xn_sb"], TB, 0, tb, bufs, wdr, s_ap, half=half)
        sevs.append(s_ev)
        s_x = s_ev
    return sevs



def emit_ffn0_dram(c: Ctx, pool: Pool, name, x_src, y_dst, T, bufs, wdr,
                   s_pre=None, half=0.5, TB=512):
    p = c.p
    s_ld = c.sem(f"{name}_ld")
    s_st = c.sem(f"{name}_st")
    if s_pre is not None:
        c.wait("sync", s_pre)
    nst_prev = 0
    for (i, (t0, tsz)) in enumerate(_blocks(T, TB)):
        if i > 0:
            c.wait("sync", s_st)
        for dc in range(DC):
            c.dma("sync", bufs["x_tb"][:, dc * TB: dc * TB + tsz],
                  x_src(dc, t0, tsz), sem=s_ld)
        for e in ("scalar", "tensor", "vector"):
            c.wait(e, s_ld)
        s_ap = emit_ln(c, pool, f"{name}_ln{i}", bufs["x_tb"], TB, 0,
                       bufs["xn_sb"], TB, 0, tsz)
        s_ev = emit_ffn0_tb(c, pool, f"{name}_tb{i}", bufs["x_tb"], TB, 0,
                            bufs["xn_sb"], TB, 0, tsz, bufs, wdr, s_ap, half=half)
        c.wait("sync", s_ev)
        for dc in range(DC):
            c.dma("sync", y_dst(dc, t0, tsz),
                  bufs["x_tb"][:, dc * TB: dc * TB + tsz], sem=s_st)
    return s_st


def emit_proj(c: Ctx, pool: Pool, name, whd, MC, KC, src_sb, srcT, src_t0, nT,
              evac_fn, s_src, wpb, psum_pair=None, wgate=None, extra_mm=None,
              evac_eng="vector"):
    p = c.p
    s_w = c.sem("PJw")
    s_mm = c.sem("PJmm")
    s_ev = c.sem("PJev" + evac_eng[0])
    ps = psum_pair or pool.ps_a
    src_gate = s_src.n if s_src is not None else None
    TCH = chunks512(0, nT)
    ntch = len(TCH)
    g = 0
    for mc in range(MC):
        buf = mc % 2
        if mc >= 2:
            c.p.sync.wait_ge(s_mm.h, s_mm.base + (mc - 1) * ntch)
        elif wgate is not None:
            c.p.sync.wait_ge(wgate[0].h, wgate[1])
        c.dma("sync", wpb[buf][:, 0:KC * 128], whd[mc], sem=s_w)
        for (t0, tsz) in TCH:
            pb = g % 2
            p.tensor.wait_ge(s_w.h, s_w.base + (mc + 1) * 16)
            if g == 0 and src_gate is not None:
                p.tensor.wait_ge(s_src.h, src_gate)
            if g >= 2:
                p.tensor.wait_ge(s_ev.h, s_ev.base + g - 1)
            for kc in range(KC):
                i = p.tensor.matmul(ps[pb][:, 0:tsz],
                                    wpb[buf][:, kc * 128:(kc + 1) * 128],
                                    src_sb[:, kc * srcT + src_t0 + t0: kc * srcT + src_t0 + t0 + tsz],
                                    start=(kc == 0),
                                    stop=(kc == KC - 1 and extra_mm is None))
            if extra_mm is not None:
                i = extra_mm(mc, t0, tsz, ps[pb])
            i.then_inc(s_mm.h, 1)
            s_mm.n += 1
            evac_fn(mc, t0, tsz, ps[pb], s_mm, s_ev)
            g += 1
    return s_mm, s_ev


ATT0_SCALE = float(HD) ** -0.5


def emit_ident(c: Ctx, name):
    p = c.p
    ident = c.sbuf(f"{name}_ident", [128, 128], F32)
    s_msk = c.sem(f"{name}_identc")
    p.gpsimd.memset(ident[:], 0.0)
    p.gpsimd.affine_select(out=ident[:], in_=ident[:], compare_op=ALU.not_equal,
                           fill=1.0, base=0, pattern=[[-1, 128]], channel_multiplier=1).then_inc(s_msk.h, 1)
    s_msk.n += 1
    return dict(ident=ident, s_msk=s_msk)


def make_att0_masks_host(gq0, NQT):
    i = np.arange(128)[:, None]
    j = np.arange(128)[None, :]
    mown = np.where(j <= i, 0.0, NEG).astype(np.float32)
    mprev = np.zeros((128, NQT * 128), np.float32)
    mpers = np.zeros((128, NQT * 16), np.float32)
    for qt in range(NQT):
        gq = gq0 + qt
        if gq == 0:
            mprev[:, qt * 128:(qt + 1) * 128] = NEG
            mpers[:, qt * 16:(qt + 1) * 16] = NEG
        elif gq == 1:
            mprev[:, qt * 128:(qt + 1) * 128] = np.where((j > i) | (j < 16), 0.0, NEG)
            mpers[:, qt * 16:(qt + 1) * 16] = NEG
        else:
            mprev[:, qt * 128:(qt + 1) * 128] = np.where(j > i, 0.0, NEG)
    return mown, mprev, mpers


def emit_att0(c: Ctx, pool: Pool, name, x_src, y_src, y_dst, TLOCAL, wdr, bufs):
    p = c.p
    EXTW = 128 + TLOCAL + 128
    NQT = TLOCAL // 128
    u1, qy, k_sb, lat = bufs["u1"], bufs["qy_sb"], bufs["k_sb"], bufs["lat_sb"]
    masks = bufs["masks"]

    s_ld = c.sem(f"{name}_ld")

    s_ap = None
    for (i, (t0, tsz)) in enumerate(chunks512(0, EXTW)):
        for dc in range(DC):
            c.dma("sync", bufs["x_tb"][:, dc * 512: dc * 512 + tsz],
                  x_src(dc, t0, tsz), sem=s_ld)
        for e in ("scalar", "tensor", "vector"):
            c.wait(e, s_ld)
        s_ap = emit_ln(c, pool, f"{name}_ln{i}", bufs["x_tb"], 512, 0,
                       u1, EXTW, t0, tsz)
        c.wait("sync", s_ap)
    s_lnap = s_ap

    def mk_evac_bf16(dst, dstT, scale=None):
        def evac(mc, t0, tsz, ps_ap, s_mm, s_ev):
            p.scalar.wait_ge(s_mm.h, s_mm.n)
            if scale is None:
                i = p.scalar.copy(out=dst[:, mc * dstT + t0: mc * dstT + t0 + tsz],
                                  in_=ps_ap[:, 0:tsz])
            else:
                i = p.scalar.mul(out=dst[:, mc * dstT + t0: mc * dstT + t0 + tsz],
                                 in_=ps_ap[:, 0:tsz], mul=scale)
            i.then_inc(s_ev.h, 1)
            s_ev.n += 1
        return evac

    mm_lat, ev_lat = emit_proj(c, pool, f"{name}_lat", wdr["wkvh"], 4, DC,
                               u1, EXTW, 0, EXTW, mk_evac_bf16(lat, EXTW),
                               s_ap, bufs["wpb"], evac_eng="scalar")
    mm_q, ev_q = emit_proj(c, pool, f"{name}_q", wdr["wqh"], 16, DC,
                           u1, EXTW, 128, TLOCAL, mk_evac_bf16(qy, TLOCAL, ATT0_SCALE),
                           s_ap, bufs["wpb"], wgate=(mm_lat, mm_lat.n),
                           evac_eng="scalar")
    mm_k, ev_k = emit_proj(c, pool, f"{name}_k", wdr["wkh"], 16, 4,
                           lat, EXTW, 0, EXTW, mk_evac_bf16(k_sb, EXTW),
                           ev_lat, bufs["wpb"], wgate=(mm_q, mm_q.n),
                           evac_eng="scalar")

    s_vstg = c.sem(f"{name}_vstg")
    s_trn = c.sem(f"{name}_trn")
    s_vtm = c.sem(f"{name}_vtm")
    NTILE = EXTW // 128
    vstate = {"vg": 0, "trn_after": []}

    def evac_v(mc, t0, tsz, ps_ap, s_mm, s_ev):
        vg = vstate["vg"]
        stg = bufs["vstg"][vg % 2]
        p.scalar.wait_ge(s_mm.h, s_mm.n)
        if vg >= 2:
            p.scalar.wait_ge(s_trn.h, vstate["trn_after"][vg - 2])
        p.scalar.copy(out=stg[:, 0:tsz], in_=ps_ap[:, 0:tsz]).then_inc(s_vstg.h, 1)
        s_vstg.n += 1
        p.tensor.wait_ge(s_vstg.h, s_vstg.n)
        nsub = (tsz + 127) // 128
        for j in range(nsub):
            w = min(128, tsz - j * 128)
            tau = s_trn.n - s_trn.base
            bank = pool.ps_b[tau % 2]
            if tau >= 2:
                p.tensor.wait_ge(s_vtm.h, s_vtm.base + tau - 1)
            i = p.tensor.transpose(bank[0:w, 0:128],
                                   stg[:, j * 128: j * 128 + w], masks["ident"][:, :])
            i.then_inc(s_trn.h, 1)
            s_trn.n += 1
            p.vector.wait_ge(s_trn.h, s_trn.n)
            jt = (t0 + j * 128) // 128
            p.vector.tensor_copy(
                out=u1[0:w, mc * EXTW + jt * 128: mc * EXTW + jt * 128 + 128],
                in_=bank[0:w, 0:128]).then_inc(s_vtm.h, 1)
            s_vtm.n += 1
        p.vector.engine_nop().then_inc(s_ev.h, 1)
        s_ev.n += 1
        vstate["trn_after"].append(s_trn.n)
        vstate["vg"] = vg + 1

    mm_v, ev_v = emit_proj(c, pool, f"{name}_v", wdr["wvh"], 16, 4,
                           lat, EXTW, 0, EXTW, evac_v,
                           ev_lat, bufs["wpb"], wgate=(mm_k, mm_k.n))

    small = bufs["small"]
    bcol = bufs["bcol"]
    s_bnd = c.sem(f"{name}_bnd")
    s_bmm = c.sem(f"{name}_bmm")
    s_ch = c.sem(f"{name}_ch")

    def vedge():
        s_ch.n += 1
        return (s_ch.h, s_ch.n)

    def emit_norm_max(src, ncols, out_idx, gate):
        p.vector.memset(small[0:1, out_idx:out_idx + 1], 0.0)
        for (t0, tsz) in chunks512(0, ncols):
            p.scalar.wait_ge(gate.h, gate.n)
            if s_bmm.n > 0:
                p.scalar.wait_ge(s_bmm.h, s_bmm.n)
            p.scalar.activation(out=pool.sq[:, 0:tsz], in_=src[:, t0:t0 + tsz],
                                func=AF.Square).then_inc(s_bnd.h, 1)
            s_bnd.n += 1
            p.tensor.wait_ge(s_bnd.h, s_bnd.n)
            p.tensor.matmul(pool.ps_s2[:, 0:tsz], pool.ones128[:, :],
                            pool.sq[:, 0:tsz], start=True, stop=True).then_inc(s_bmm.h, 1)
            s_bmm.n += 1
            p.vector.wait_ge(s_bmm.h, s_bmm.n)
            h1, v1 = s_ch.h, None
            p.vector.reduce_max(out=small[0:1, 3:4], in_=pool.ps_s2[0:1, 0:tsz],
                                axis=mybir.AxisListType.X).then_inc(*vedge())
            p.vector.wait_ge(s_ch.h, s_ch.n)
            p.vector.tensor_tensor(out=small[0:1, out_idx:out_idx + 1],
                                   in0=small[0:1, out_idx:out_idx + 1],
                                   in1=small[0:1, 3:4], op=ALU.max).then_inc(s_bnd.h, 1)
            s_bnd.n += 1

    emit_norm_max(qy, 16 * TLOCAL, 0, ev_q)
    emit_norm_max(k_sb, 16 * EXTW, 1, ev_k)
    p.vector.tensor_tensor(out=small[0:1, 2:3], in0=small[0:1, 0:1],
                           in1=small[0:1, 1:2], op=ALU.mult).then_inc(s_bnd.h, 1)
    s_bnd.n += 1
    p.scalar.wait_ge(s_bnd.h, s_bnd.n)
    p.scalar.activation(out=small[0:1, 2:3], in_=small[0:1, 2:3],
                        func=AF.Sqrt).then_inc(*vedge())
    p.scalar.wait_ge(s_ch.h, s_ch.n)
    p.scalar.mul(out=small[0:1, 2:3], in_=small[0:1, 2:3], mul=-1.0).then_inc(s_bnd.h, 1)
    s_bnd.n += 1
    p.tensor.wait_ge(s_bnd.h, s_bnd.n)
    p.tensor.matmul(pool.ps_s2[:, 0:1], pool.ones_row[:, :], small[0:1, 2:3],
                    start=True, stop=True).then_inc(s_bmm.h, 1)
    s_bmm.n += 1
    p.vector.wait_ge(s_bmm.h, s_bmm.n)
    p.vector.tensor_copy(out=bcol[:, 0:1], in_=pool.ps_s2[:, 0:1]).then_inc(s_bnd.h, 1)
    s_bnd.n += 1

    if "dbg_preattn" in bufs:
        bufs["dbg_preattn"](c, s_bnd)

    p_sb, pn_sb, pt_sb = bufs["p_sb"], bufs["pn_sb"], bufs["pt_sb"]
    sm_sb = bufs["sm_sb"]
    den = bufs["den"]
    rcol = bufs["rcol"]
    s_smm = c.sem(f"{name}_smm")
    s_msk2 = c.sem(f"{name}_msk2")
    s_exp = c.sem(f"{name}_exp")
    s_pn = c.sem(f"{name}_pn")
    s_rc = c.sem(f"{name}_rc")
    s_ptr = c.sem(f"{name}_ptr")
    s_ptv = c.sem(f"{name}_ptv")
    s_pv = c.sem(f"{name}_pv")
    s_oev = c.sem(f"{name}_oev")

    for e in ("tensor", "vector", "scalar"):
        c.wait(e, s_bnd)
        c.wait(e, s_vtm)
    u = 0
    for h in range(HEADS):
        for qt in range(NQT):
            pb = u % 2
            qtile = qy[:, h * TLOCAL + qt * 128: h * TLOCAL + (qt + 1) * 128]
            prevoff = (128 + (qt - 1) * 128) if qt > 0 else 0
            blocks = [(0, 128, prevoff, bufs["mprev_sb"][:, qt * 128:(qt + 1) * 128], qt),
                      (128, 128, 128 + qt * 128, bufs["mown_sb"][:, 0:128], qt + 1),
                      (256, 16, 128 + TLOCAL, bufs["mpers_sb"][:, qt * 16:(qt + 1) * 16], NTILE - 1)]
            swidth = 272
            if u >= 2:
                p.tensor.wait_ge(s_exp.h, s_exp.base + u - 1)
            for (so, sw, ko, msk, jt) in blocks:
                i = p.tensor.matmul(pool.ps_a[pb][:, so:so + sw], qtile,
                                    k_sb[:, h * EXTW + ko: h * EXTW + ko + sw],
                                    start=True, stop=True)
            i.then_inc(s_smm.h, 1)
            s_smm.n += 1
            p.vector.wait_ge(s_smm.h, s_smm.n)
            if u >= 1:
                p.vector.wait_ge(s_exp.h, s_exp.n)
            for (so, sw, ko, msk, jt) in blocks:
                p.vector.tensor_tensor(out=sm_sb[:, so:so + sw],
                                       in0=pool.ps_a[pb][:, so:so + sw],
                                       in1=msk if msk.shape[1] == sw else msk[:, 0:sw],
                                       op=ALU.add).then_inc(s_msk2.h, 1)
                s_msk2.n += 1
            p.scalar.wait_ge(s_msk2.h, s_msk2.n)
            if u >= 1:
                p.scalar.wait_ge(s_pn.h, s_pn.n)
            p.scalar.activation(out=p_sb[:, 0:swidth], in_=sm_sb[:, 0:swidth],
                                func=AF.Exp, bias=bcol[:, 0:1],
                                accum_out=den[:, 0:1]).then_inc(s_exp.h, 1)
            s_exp.n += 1
            p.vector.wait_ge(s_exp.h, s_exp.n)
            if u >= 1:
                p.vector.wait_ge(s_ptr.h, s_ptr.n)
            p.vector.reciprocal(out=rcol[:, 0:1], in_=den[:, 0:1]).then_inc(s_rc.h, 1)
            s_rc.n += 1
            p.vector.wait_ge(s_rc.h, s_rc.n)
            p.vector.tensor_scalar(out=pn_sb[:, 0:swidth], in0=p_sb[:, 0:swidth],
                                   scalar1=rcol[:, 0:1], scalar2=None,
                                   op0=ALU.mult).then_inc(s_pn.h, 1)
            s_pn.n += 1
            p.tensor.wait_ge(s_pn.h, s_pn.n)
            for (bi, (so, sw, ko, msk, jt)) in enumerate(blocks):
                tau = s_ptr.n - s_ptr.base
                bank = (pool.ps_b[1], pool.ps_c[0])[tau % 2]
                if tau >= 2:
                    p.tensor.wait_ge(s_ptv.h, s_ptv.base + tau - 1)
                i = p.tensor.transpose(bank[0:sw, 0:128],
                                       pn_sb[:, so:so + sw], masks["ident"][:, :])
                i.then_inc(s_ptr.h, 1)
                s_ptr.n += 1
                p.vector.wait_ge(s_ptr.h, s_ptr.n)
                if bi == 0 and u >= 1:
                    p.vector.wait_ge(s_pv.h, s_pv.n)
                p.vector.tensor_copy(out=pt_sb[0:sw, bi * 128: bi * 128 + 128],
                                     in_=bank[0:sw, 0:128]).then_inc(s_ptv.h, 1)
                s_ptv.n += 1
            p.tensor.wait_ge(s_ptv.h, s_ptv.n)
            if u >= 1:
                p.tensor.wait_ge(s_oev.h, s_oev.base + u)
            for (bi, (so, sw, ko, msk, jt)) in enumerate(blocks):
                i = p.tensor.matmul(pool.ps_b[0][:, 0:128],
                                    u1[0:sw, h * EXTW + jt * 128: h * EXTW + jt * 128 + 128],
                                    pt_sb[0:sw, bi * 128: bi * 128 + 128],
                                    start=(bi == 0), stop=(bi == len(blocks) - 1))
            i.then_inc(s_pv.h, 1)
            s_pv.n += 1
            p.scalar.wait_ge(s_pv.h, s_pv.n)
            p.scalar.copy(out=qy[:, h * TLOCAL + qt * 128: h * TLOCAL + (qt + 1) * 128],
                          in_=pool.ps_b[0][:, 0:128]).then_inc(s_oev.h, 1)
            s_oev.n += 1
            u += 1

    s_xld = c.sem(f"{name}_xld")
    s_st = c.sem(f"{name}_st")

    def evac_wo(mc, t0, tsz, ps_ap, s_mm, s_ev):
        p.vector.wait_ge(s_mm.h, s_mm.n)
        p.vector.wait_ge(s_xld.h, s_xld.n)
        p.vector.scalar_tensor_tensor(
            out=bufs["x_tb"][:, mc * 512: mc * 512 + tsz],
            in0=ps_ap[:, 0:tsz], scalar=1.0,
            in1=bufs["x_tb"][:, mc * 512: mc * 512 + tsz],
            op0=ALU.mult, op1=ALU.add).then_inc(s_ev.h, 1)
        s_ev.n += 1

    wo_gate = (mm_v, mm_v.n)
    for (ti, (t0, tsz)) in enumerate(chunks512(0, TLOCAL)):
        if ti > 0:
            c.wait("gpsimd", s_st)
        else:
            c.wait("gpsimd", s_lnap)
        for dc in range(DC):
            c.dma("gpsimd", bufs["x_tb"][:, dc * 512: dc * 512 + tsz],
                  y_src(dc, t0, tsz), sem=s_xld)
        mm_wo, ev_wo = emit_proj(c, pool, f"{name}_wo{ti}", wdr["woh"], 16, DC,
                                 qy, TLOCAL, t0, tsz, evac_wo,
                                 s_oev, bufs["wpb"],
                                 wgate=wo_gate, psum_pair=pool.ps_c)
        wo_gate = (mm_wo, mm_wo.n)
        c.wait("gpsimd", ev_wo)
        for dc in range(DC):
            c.dma("gpsimd", y_dst(dc, t0, tsz),
                  bufs["x_tb"][:, dc * 512: dc * 512 + tsz], sem=s_st)
    return s_st



def build_ffn0_launch(TLOC=1024, TB=512, half=0.5):
    nc = bass.Bass("TRN2", target_bir_lowering=False, debug=False)
    x_in = nc.dram_tensor("x", [D, TLOC], F32, kind="ExternalInput").ap()
    w0h_d = nc.dram_tensor("w0h", [64, 128, DC * 128], BF16, kind="ExternalInput").ap()
    w1h_d = nc.dram_tensor("w1q", [64, 128, 16 * 128], BF16, kind="ExternalInput").ap()
    b0_d = nc.dram_tensor("b0c", [128, 64], F32, kind="ExternalInput").ap()
    b1_d = nc.dram_tensor("b1r", [1, D], F32, kind="ExternalInput").ap()
    y_out = nc.dram_tensor("y", [D, TLOC], F32, kind="ExternalOutput").ap()

    with ExitStack() as stack:
        c = Ctx(nc, stack)
        p = c.p
        pool = Pool(c)
        x_sb = c.sbuf("x_sb", [128, DC * TLOC], F32)
        xn_sb = c.sbuf("xn_sb", [128, DC * TB], BF16)
        w0b = [c.sbuf(f"w0b{i}", [128, DC * 128], BF16) for i in range(2)]
        w1b = [c.sbuf(f"w1b{i}", [128, 16 * 128], BF16) for i in range(4)]
        h_sb = c.sbuf("h_sb", [128, 64 * TB], BF16)
        bias0c = c.sbuf("bias0c", [128, 64], F32)
        bias1r = c.sbuf("bias1r", [1, D], F32)

        s_ld = c.sem("ld")
        s_out = c.sem("out")

        pool.emit_consts()
        c.dma("sync", x_sb[:], x_in.rearrange("(dc p) t -> p dc t", p=128), sem=s_ld)
        c.dma("sync", bias0c[:], b0_d[:, :], sem=s_ld)
        c.dma("sync", bias1r[:], b1_d[:, :], sem=s_ld)
        for e in ("scalar", "vector", "tensor"):
            c.wait(e, s_ld)

        wdr = dict(w0h=[w0h_d[i] for i in range(64)],
                   w1q=[w1h_d[i] for i in range(64)])
        bufs = dict(w0b=w0b, w1b=w1b, h_sb=h_sb, bias0c=bias0c, bias1r=bias1r,
                    xn_sb=xn_sb)

        sevs = emit_ffn0(c, pool, "f", x_sb, TLOC, wdr, bufs, s_ld, TB=TB, half=half)
        for s in sevs:
            c.wait("sync", s)
        c.dma("sync", y_out.rearrange("(dc p) t -> p dc t", p=128), x_sb[:], sem=s_out)
        c.wait("sync", s_out)

        with nc.Block() as block:
            c.p.replay(block)
    return nc


def prep_ffn0_weights(W0, b0, W1, b1):
    w0h = host_tile_lhsT(np.asarray(W0, dtype=np.float32).T)
    w1h = host_tile_lhsT(np.asarray(W1, dtype=np.float32).T)
    w1q = np.ascontiguousarray(
        w1h.reshape(16, 128, 4, 16 * 128).transpose(0, 2, 1, 3).reshape(64, 128, 16 * 128))
    b0c = f32c(np.asarray(b0).reshape(64, 128).T)
    b1r = f32c(np.asarray(b1).reshape(1, D))
    return w0h, w1q, b0c, b1r


def emit_rwkv(c: Ctx, pool: Pool, name, x_src, y_src, y_dst, TLOCAL, wdr, bufs):
    p = c.p
    EXTW = 264 + TLOCAL
    SCANW = 256 + TLOCAL
    ln_sb = bufs["ln_sb"]
    s_ld = c.sem(f"{name}_ld")
    s_ch = c.sem(f"{name}_ch")

    s_ap = None
    for (i, (t0, tsz)) in enumerate(chunks512(0, EXTW)):
        for dc in range(DC):
            srcs = x_src(dc, t0, tsz)
            if not isinstance(srcs, list):
                srcs = [(srcs, 0, tsz)]
            for (ap, ro, w) in srcs:
                c.dma("sync", bufs["x_tb"][:, dc * 512 + ro: dc * 512 + ro + w],
                      ap, sem=s_ld)
        for e in ("scalar", "tensor", "vector"):
            c.wait(e, s_ld)
        s_ap = emit_ln(c, pool, f"{name}_ln{i}", bufs["x_tb"], 512, 0,
                       ln_sb, EXTW, t0, tsz)
        c.wait("sync", s_ap)
    s_lnap = s_ap

    s_w2 = c.sem(f"{name}_w2")
    p.scalar.wait_ge(s_ld.h, s_ld.n)
    p.scalar.activation(out=bufs["e1_sb"][:], in_=bufs["td_sb"][:],
                        func=AF.Exp).then_inc(s_ch.h, 1)
    s_ch.n += 1
    p.scalar.wait_ge(s_ch.h, s_ch.n)
    p.scalar.activation(out=bufs["w_sb"][:], in_=bufs["e1_sb"][:],
                        func=AF.Exp, scale=-1.0).then_inc(s_w2.h, 1)
    s_w2.n += 1
    c.wait("vector", s_w2)

    s_wld = c.sem(f"{name}_wld")
    s_mm = c.sem(f"{name}_mm")
    s_ev = c.sem(f"{name}_ev")
    s_dve = c.sem(f"{name}_dve")
    s_xld = c.sem(f"{name}_xld")
    s_st = c.sem(f"{name}_st")
    ek, hmk, v_sb, a_sb = bufs["ek"], bufs["hmk"], bufs["v"], bufs["a"]
    num, den = bufs["num"], bufs["den"]
    rec, wkv, outb, r_sb = bufs["rec"], bufs["wkv"], bufs["outb"], bufs["r_sb"]
    hm_sb = bufs["hm_sb"]
    SCH = chunks512(0, SCANW)
    LCH = chunks512(0, TLOCAL)
    g = 0
    for dc in range(DC):
        if dc >= 1:
            c.p.sync.wait_ge(s_mm.h, s_mm.base + dc * 8)
        else:
            c.p.sync.wait_ge(s_lnap.h, s_lnap.n)
        c.dma("sync", bufs["wpb3"][0][:, :], wdr["wkh"][dc], sem=s_wld)
        c.dma("sync", bufs["wpb3"][1][:, :], wdr["wvh"][dc], sem=s_wld)
        c.dma("sync", bufs["wpb3"][2][:, :], wdr["wrh"][dc], sem=s_wld)
        specs = [(0, SCH, 7, ek, "exp"), (1, SCH, 7, v_sb, "copy"),
                 (2, LCH, 264, r_sb, "copy")]
        if dc == 0:
            p.tensor.wait_ge(s_lnap.h, s_lnap.n)
        for (wi, CH, coff, dst, mode) in specs:
            p.tensor.wait_ge(s_wld.h, s_wld.base + (dc * 3 + wi + 1) * 16)
            for (t0, tsz) in CH:
                pb = g % 2
                if g >= 2:
                    p.tensor.wait_ge(s_ev.h, s_ev.base + g - 1)
                for kc in range(DC):
                    i = p.tensor.matmul(pool.ps_a[pb][:, 0:tsz],
                                        bufs["wpb3"][wi][:, kc * 128:(kc + 1) * 128],
                                        ln_sb[:, kc * EXTW + coff + t0: kc * EXTW + coff + t0 + tsz],
                                        start=(kc == 0), stop=(kc == DC - 1))
                i.then_inc(s_mm.h, 1)
                s_mm.n += 1
                p.scalar.wait_ge(s_mm.h, s_mm.n)
                if dc >= 1:
                    p.scalar.wait_ge(s_dve.h, s_dve.base + 2 * dc - 1)
                if mode == "exp":
                    i = p.scalar.activation(out=dst[:, t0:t0 + tsz],
                                            in_=pool.ps_a[pb][:, 0:tsz], func=AF.Exp)
                else:
                    i = p.scalar.copy(out=dst[:, t0:t0 + tsz], in_=pool.ps_a[pb][:, 0:tsz])
                i.then_inc(s_ev.h, 1)
                s_ev.n += 1
                g += 1
        p.vector.wait_ge(s_ev.h, s_ev.n)
        if dc >= 1:
            pass
        w_bcast = bass.AP(bufs["w_sb"], dc, [[16, 128], [0, SCANW]])
        p.vector.tensor_tensor(out=hmk[:, 0:SCANW], in0=ek[:, 0:SCANW],
                               in1=hm_sb[:, 0:SCANW], op=ALU.mult)
        p.vector.tensor_tensor(out=a_sb[:, 0:SCANW], in0=hmk[:, 0:SCANW],
                               in1=v_sb[:, 0:SCANW], op=ALU.mult)
        p.vector.tensor_tensor_scan(out=num[:, 0:SCANW], data0=w_bcast,
                                    data1=a_sb[:, 0:SCANW], initial=0.0,
                                    op0=ALU.mult, op1=ALU.add)
        p.vector.tensor_tensor_scan(out=den[:, 0:SCANW], data0=w_bcast,
                                    data1=hmk[:, 0:SCANW], initial=0.0,
                                    op0=ALU.mult, op1=ALU.add)
        p.vector.tensor_scalar_add(out=den[:, 0:SCANW], in0=den[:, 0:SCANW],
                                   scalar1=1e-8).then_inc(s_ch.h, 1)
        s_ch.n += 1
        p.vector.wait_ge(s_ch.h, s_ch.n)
        p.vector.reciprocal(out=rec[:, 0:TLOCAL], in_=den[:, 256:SCANW]).then_inc(s_ch.h, 1)
        s_ch.n += 1
        p.vector.wait_ge(s_ch.h, s_ch.n)
        p.vector.tensor_tensor(out=wkv[:, 0:TLOCAL], in0=num[:, 256:SCANW],
                               in1=rec[:, 0:TLOCAL], op=ALU.mult)
        p.vector.tensor_tensor(out=outb[:, 0:TLOCAL], in0=r_sb[:, 0:TLOCAL],
                               in1=wkv[:, 0:TLOCAL], op=ALU.mult).then_inc(s_dve.h, 1)
        s_dve.n += 1
        if dc == 0:
            c.wait("gpsimd", s_lnap)
        for (t0, tsz) in LCH:
            c.dma("gpsimd", bufs["x_tb"][:, t0:t0 + tsz], y_src(dc, t0, tsz), sem=s_xld)
        p.vector.wait_ge(s_xld.h, s_xld.n)
        p.vector.tensor_tensor(out=outb[:, 0:TLOCAL], in0=outb[:, 0:TLOCAL],
                               in1=bufs["x_tb"][:, 0:TLOCAL], op=ALU.add).then_inc(s_dve.h, 1)
        s_dve.n += 1
        c.wait("gpsimd", s_dve)
        for (t0, tsz) in LCH:
            c.dma("gpsimd", y_dst(dc, t0, tsz), outb[:, t0:t0 + tsz], sem=s_st)
        c.p.vector.wait_ge(s_st.h, s_st.n)
    return s_st


L1_SCALE = float(HD) ** -0.5


def emit_l1_colscale(c: Ctx, pool: Pool, name, bufs, gate):
    p = c.p
    ws, wr = bufs["wstd_row"], bufs["wrec_row"]
    s_cs = c.sem(f"{name}_cs")
    s_ch = c.sem(f"{name}_ch")
    tmp = bufs["l1tmp"]
    colscale = bufs["colscale"]
    swi = bufs["swi"]
    swf = bufs["swf"]
    sg = bufs["sg"]
    s_in = bufs["s_wsin"]

    def edge(i):
        i.then_inc(s_ch.h, 1)
        s_ch.n += 1

    c.wait("vector", s_in)
    c.wait("scalar", s_in)
    edge(p.vector.tensor_scalar_max(out=tmp[0:1, 0:16], in0=ws[0:1, 0:16], scalar1=1e-8))
    edge(p.vector.tensor_scalar_max(out=tmp[0:1, 16:32], in0=wr[0:1, 0:16], scalar1=1e-8))
    p.scalar.wait_ge(s_ch.h, s_ch.n)
    edge(p.scalar.activation(out=tmp[0:1, 0:16], in_=tmp[0:1, 0:16], func=AF.Sqrt))
    p.scalar.wait_ge(s_ch.h, s_ch.n)
    edge(p.scalar.activation(out=tmp[0:1, 16:32], in_=tmp[0:1, 16:32], func=AF.Sqrt))
    edge(p.vector.tensor_scalar(out=tmp[0:1, 32:48], in0=wr[0:1, 0:16],
                                scalar1=0.1, scalar2=None, op0=ALU.is_gt))
    p.tensor.wait_ge(s_ch.h, s_ch.n)
    p.tensor.wait_ge(gate[0].h, gate[1])
    p.tensor.matmul(pool.ps_c[0][:, 0:16], pool.ones_row[:, :], tmp[0:1, 0:16],
                    start=True, stop=True)
    p.tensor.matmul(pool.ps_c[0][:, 16:32], pool.ones_row[:, :], tmp[0:1, 16:32],
                    start=True, stop=True)
    i = p.tensor.matmul(pool.ps_c[0][:, 32:48], pool.ones_row[:, :], tmp[0:1, 32:48],
                        start=True, stop=True)
    i.then_inc(s_cs.h, 1)
    s_cs.n += 1
    p.vector.wait_ge(s_cs.h, s_cs.n)
    edge(p.vector.tensor_copy(out=swf[:, 0:16], in_=pool.ps_c[0][:, 0:16]))
    edge(p.vector.tensor_copy(out=sg[:, 0:16], in_=pool.ps_c[0][:, 16:32]))
    edge(p.vector.tensor_copy(out=colscale[:, 32:48], in_=pool.ps_c[0][:, 32:48]))
    c.wait("gpsimd", s_ch)
    p.gpsimd.affine_select(out=swf[:, 0:16], in_=swf[:, 0:16], compare_op=ALU.is_ge,
                           fill=0.0, base=119, pattern=[[0, 16]], channel_multiplier=-1)
    p.gpsimd.affine_select(out=sg[:, 0:16], in_=sg[:, 0:16], compare_op=ALU.is_ge,
                           fill=0.0, base=-120, pattern=[[0, 16]], channel_multiplier=1)
    i = p.gpsimd.affine_select(out=colscale[:, 32:48], in_=colscale[:, 32:48],
                               compare_op=ALU.is_ge, fill=0.0, base=-120,
                               pattern=[[0, 16]], channel_multiplier=1)
    i.then_inc(s_cs.h, 1)
    s_cs.n += 1
    p.vector.wait_ge(s_cs.h, s_cs.n)
    p.vector.tensor_copy(out=swi[:, 0:16], in_=colscale[:, 32:48])
    edge(p.vector.tensor_tensor(out=colscale[:, 0:16], in0=swf[:, 0:16],
                                in1=sg[:, 0:16], op=ALU.add))
    p.vector.wait_ge(s_ch.h, s_ch.n)
    edge(p.vector.tensor_copy(out=colscale[:, 16:32], in_=colscale[:, 0:16]))
    p.vector.wait_ge(s_ch.h, s_ch.n)
    i = p.vector.memset(colscale[:, 32:48], 1.0)
    i.then_inc(s_cs.h, 1)
    s_cs.n += 1
    return s_cs


def emit_l1ffn_pub(c: Ctx, pool: Pool, name, x_src, TLOCAL, wdr, bufs, outs):
    p = c.p
    NT = TLOCAL // 128
    XTB = bufs.get("xtb_cols", 512)
    s_ld = c.sem(f"{name}_ld")

    s_ap = None
    t0 = 0
    i = 0
    while t0 < TLOCAL:
        tsz = min(XTB, TLOCAL - t0)
        for dc in range(DC):
            c.dma("sync", bufs["x_tb"][:, dc * XTB: dc * XTB + tsz],
                  x_src(dc, t0, tsz), sem=s_ld)
        for e in ("scalar", "tensor", "vector"):
            c.wait(e, s_ld)
        s_ap = emit_ln(c, pool, f"{name}_ln{i}", bufs["x_tb"], XTB, 0,
                       bufs["xn_sb"], TLOCAL, t0, tsz)
        c.wait("sync", s_ap)
        t0 += tsz
        i += 1

    s_cs = emit_l1_colscale(c, pool, name + "cs", bufs, (s_ap, s_ap.n))

    fused = bufs["fused_sb"]
    c.wait("vector", s_cs)

    def evac_fused(mc, t0, tsz, ps_ap, s_mm, s_ev):
        p.vector.wait_ge(s_mm.h, s_mm.n)
        p.vector.tensor_scalar(out=fused[:, mc * TLOCAL + t0: mc * TLOCAL + t0 + tsz],
                               in0=ps_ap[:, 0:tsz], scalar1=bufs["colscale"][:, mc:mc + 1],
                               scalar2=None, op0=ALU.mult).then_inc(s_ev.h, 1)
        s_ev.n += 1

    mm_f, ev_f = emit_proj(c, pool, f"{name}_fp", wdr["waph"], 48, DC,
                           bufs["xn_sb"], TLOCAL, 0, TLOCAL, evac_fused,
                           s_ap, bufs["wpb"])

    s_sel = c.sem(f"{name}_sel")
    s_pq = c.sem(f"{name}_pq")
    s_nrm = c.sem(f"{name}_nrm")
    s_nmm = c.sem(f"{name}_nmm")
    s_ch2 = c.sem(f"{name}_ch2")
    stage = bufs["stage"]
    nsq = bufs["nsq"]
    small = bufs["l1tmp"]
    c.wait("vector", ev_f)
    p.vector.memset(small[0:1, 48:49], 0.0)
    p.vector.memset(small[0:1, 49:50], 0.0)
    pi = 0
    for (sect, base_self, base_oth, out_dram, nidx) in (
            ("q", 0, 16, outs["qf"], 48), ("k", 16, 0, outs["kf"], 49)):
        for h in range(HEADS):
            st_buf = stage[pi % 2]
            if pi >= 2:
                c.wait("vector", s_pq, s_pq.base + 16 * (pi - 1))
            mycol = (base_self + h) * TLOCAL
            othcol = (base_oth + h) * TLOCAL
            swm = bcast_col(bufs["swi"][:, h:h + 1], TLOCAL)
            p.vector.select(out=st_buf[:, 0:TLOCAL], mask=swm,
                            on_true=fused[:, othcol:othcol + TLOCAL],
                            on_false=fused[:, mycol:mycol + TLOCAL]).then_inc(s_sel.h, 1)
            s_sel.n += 1
            for (t0, tsz) in chunks512(0, TLOCAL):
                p.scalar.wait_ge(s_sel.h, s_sel.n)
                if s_nmm.n > 0:
                    p.scalar.wait_ge(s_nmm.h, s_nmm.n)
                p.scalar.activation(out=nsq[:, 0:tsz], in_=st_buf[:, t0:t0 + tsz],
                                    func=AF.Square).then_inc(s_nrm.h, 1)
                s_nrm.n += 1
                p.tensor.wait_ge(s_nrm.h, s_nrm.n)
                p.tensor.matmul(pool.ps_s2[:, 0:tsz], pool.ones128[:, :],
                                nsq[:, 0:tsz], start=True, stop=True).then_inc(s_nmm.h, 1)
                s_nmm.n += 1
                p.vector.wait_ge(s_nmm.h, s_nmm.n)
                p.vector.reduce_max(out=small[0:1, 50:51], in_=pool.ps_s2[0:1, 0:tsz],
                                    axis=mybir.AxisListType.X).then_inc(s_ch2.h, 1)
                s_ch2.n += 1
                p.vector.wait_ge(s_ch2.h, s_ch2.n)
                p.vector.tensor_tensor(out=small[0:1, nidx:nidx + 1],
                                       in0=small[0:1, nidx:nidx + 1],
                                       in1=small[0:1, 50:51], op=ALU.max).then_inc(s_ch2.h, 1)
                s_ch2.n += 1
            c.wait("sync", s_sel)
            c.wait("sync", s_ch2)
            c.dma("sync", out_dram[:, h * TLOCAL:(h + 1) * TLOCAL],
                  st_buf[:, 0:TLOCAL], sem=s_pq)
            pi += 1
    c.wait("sync", s_ch2)
    c.dma("sync", outs["nrm"][0:1, 0:2], small[0:1, 48:50], sem=s_pq)

    s_v32 = c.sem(f"{name}_v32")
    s_tr = c.sem(f"{name}_tr")
    s_tv = c.sem(f"{name}_tv")
    s_pv = c.sem(f"{name}_pv")
    vt32 = bufs["vt32"]
    c.wait("vector", s_pq)
    for jt in range(NT):
        st_buf = stage[jt % 2]
        if jt >= 2:
            c.wait("vector", s_pv, s_pv.base + 16 * (jt - 1))
        for h in range(HEADS):
            tau = s_tr.n - s_tr.base
            p.scalar.wait_ge(ev_f.h, ev_f.n)
            if tau >= 2:
                p.scalar.wait_ge(s_tr.h, s_tr.base + tau - 1)
            p.scalar.copy(out=vt32[tau % 2][:, :],
                          in_=fused[:, (32 + h) * TLOCAL + jt * 128: (32 + h) * TLOCAL + jt * 128 + 128]).then_inc(s_v32.h, 1)
            s_v32.n += 1
            bank = pool.ps_b[tau % 2]
            p.tensor.wait_ge(s_v32.h, s_v32.n)
            if tau >= 2:
                p.tensor.wait_ge(s_tv.h, s_tv.base + tau - 1)
            i = p.tensor.transpose(bank[:, 0:128], vt32[tau % 2][:, :],
                                   bufs["masks"]["ident"][:, :])
            i.then_inc(s_tr.h, 1)
            s_tr.n += 1
            p.vector.wait_ge(s_tr.h, s_tr.n)
            p.vector.tensor_copy(out=st_buf[:, h * 128:(h + 1) * 128],
                                 in_=bank[:, 0:128]).then_inc(s_tv.h, 1)
            s_tv.n += 1
        c.wait("sync", s_tv)
        c.dma("sync", outs["vtm"][:, jt * 2048:(jt + 1) * 2048],
              st_buf[:, 0:2048], sem=s_pv)
    return s_pv, s_pq


def emit_l1ffn_diagmasks(c: Ctx, name, dm=None):
    p = c.p
    if dm is None:
        dm = c.sbuf(f"{name}_dm", [128, 4 * 512], BF16)
    s_m = c.sem(f"{name}_dmc")
    p.gpsimd.memset(dm[:], 0.0)
    for pa in range(4):
        i = p.gpsimd.affine_select(
            out=dm[:, pa * 512:(pa + 1) * 512], in_=dm[:, pa * 512:(pa + 1) * 512],
            compare_op=ALU.is_ge, fill=NEG, base=-128 * pa,
            pattern=[[1, 512]], channel_multiplier=-1)
    i.then_inc(s_m.h, 1)
    s_m.n += 1
    return dm, s_m


def emit_l1ffn_flash(c: Ctx, pool: Pool, name, TLOCAL, wdr, bufs, qs, nbands):
    p = c.p
    s_ch = c.sem(f"{name}_ch")

    def vedge(i):
        i.then_inc(s_ch.h, 1)
        s_ch.n += 1
        return s_ch.n

    qf = bufs["qf_sb"]

    def dacc_sl(idx, dacc=bufs["dacc"]):
        b = 32 * (idx // 6)
        sl = idx % 6
        return dacc[b:b + 1, sl * 512:(sl + 1) * 512]
    bufs["dacc_sl"] = dacc_sl
    kband, vband = bufs["kband"], bufs["vband"]
    pt_sb = bufs["pt_sb"]
    sm_sb = bufs["sm_sb"]
    oacc, dacc = bufs["oacc"], bufs["dacc"]
    dmasks, s_dm = bufs["dmasks"]
    bmat = bufs["bmat"]
    s_kv = c.sem(f"{name}_kv")
    s_smm = c.sem(f"{name}_smm")
    s_msk = c.sem(f"{name}_msk")
    s_exp = c.sem(f"{name}_exp")
    s_ptv = c.sem(f"{name}_ptv")
    s_oev = c.sem(f"{name}_oev")
    s_dev = c.sem(f"{name}_dev")

    if not bufs.get("bias_built", False):
        bufs["bias_built"] = True
        nrm = bufs["nrm_sb"]
        s_b = c.sem(f"{name}_b")
        c.wait("vector", bufs["s_nrm_in"])
        c.wait("scalar", bufs["s_nrm_in"])
        kstrided = bass.AP(nrm.tensor if isinstance(nrm, bass.AP) else nrm, 1,
                           [[20, 1], [2, 8]])
        vedge(p.vector.reduce_max(out=nrm[0:1, 18:19], in_=kstrided,
                                  axis=mybir.AxisListType.X))
        p.vector.wait_ge(s_ch.h, s_ch.n)
        vedge(p.vector.tensor_tensor(out=nrm[0:1, 10:11], in0=nrm[0:1, 18:19],
                                     in1=nrm[0:1, 16:17], op=ALU.mult))
        p.scalar.wait_ge(s_ch.h, s_ch.n)
        vedge(p.scalar.activation(out=nrm[0:1, 10:11], in_=nrm[0:1, 10:11], func=AF.Sqrt))
        p.scalar.wait_ge(s_ch.h, s_ch.n)
        vedge(p.scalar.mul(out=nrm[0:1, 10:11], in_=nrm[0:1, 10:11], mul=-L1_SCALE))
        p.tensor.wait_ge(s_ch.h, s_ch.n)
        i = p.tensor.matmul(pool.ps_s2[:, 0:1], pool.ones_row[:, :], nrm[0:1, 10:11],
                            start=True, stop=True)
        i.then_inc(s_b.h, 1)
        s_b.n += 1
        p.vector.wait_ge(s_b.h, s_b.n)
        c.wait("vector", bufs["s_tm_in"])
        vedge(p.vector.tensor_copy(out=bufs["bcol"][:, 0:1], in_=pool.ps_s2[:, 0:1]))
        p.vector.wait_ge(s_ch.h, s_ch.n)
        vedge(p.vector.tensor_scalar(out=bmat[:, 0:128], in0=bufs["tmask_sb"][:, 0:128],
                                     scalar1=bufs["bcol"][:, 0:1], scalar2=None,
                                     op0=ALU.add))
        bufs["bias_done"] = s_ch.n
    sbd = bufs["bias_done"]

    for e in ("tensor", "vector", "scalar"):
        c.wait(e, bufs["s_qf"])
        p.eng(e).wait_ge(s_ch.h, sbd)
        c.wait(e, s_dm)
    p.vector.memset(oacc[:, :], 0.0)
    i = p.vector.memset(dacc[:, :], 0.0)
    vedge(i)
    s_zero = s_ch.n
    for e in ("tensor", "scalar"):
        p.eng(e).wait_ge(s_ch.h, s_zero)

    seq = [("g", b // 2, b % 2) for b in range(2 * nbands)] + [("l", 0, qs)]
    nsb = len(seq)
    u = 0
    gi = 0
    bank_release = {}
    ptv_after_sb = []
    ptv_after_gi = []
    oev_after_gi = {}

    def sub_k(base_ap, half):
        return bass.AP(base_ap.tensor, base_ap.offset + half * 512,
                       [list(base_ap.ap[0]), [1024, 16], [1, 512]])

    for sb, (kind, blk, half) in enumerate(seq):
        buf = sb % 2
        if sb >= 2:
            c.p.sync.wait_ge(s_ptv.h, ptv_after_sb[sb - 2])
        if kind == "g":
            c.dma("sync", kband[buf][:, :], sub_k(wdr["kg"][blk], half), sem=s_kv)
            c.dma("sync", vband[buf][:, :],
                  bass.AP(wdr["vg"][blk].tensor, wdr["vg"][blk].offset + half * 8192,
                          [list(wdr["vg"][blk].ap[0]), [1, 8192]]), sem=s_kv)
        else:
            c.dma("sync", kband[buf][:, :], sub_k(wdr["kfl"], half), sem=s_kv)
            c.dma("sync", vband[buf][:, :],
                  bass.AP(wdr["vtl"].tensor, wdr["vtl"].offset + half * 8192,
                          [list(wdr["vtl"].ap[0]), [1, 8192]]), sem=s_kv)
        kv_gate = s_kv.n
        local = (kind == "l")
        for h in range(HEADS):
            obank = pool.ps_b[gi % 2]
            dbank = (pool.ps_s1, pool.ps_s2)[gi % 2]
            ptg = gi % 2
            if gi >= 2:
                p.tensor.wait_ge(s_oev.h, oev_after_gi[gi - 2])
            for jt in range(4):
                g = sb * 4 + jt if not local else 0
                pb = u % 2
                ktile = kband[buf][:, h * 512 + jt * 128: h * 512 + jt * 128 + 128]
                vtile = vband[buf][:, jt * 2048 + h * 128: jt * 2048 + h * 128 + 128]
                p.tensor.wait_ge(s_kv.h, kv_gate)
                if u >= 2:
                    rs, rv = bank_release[u - 2]
                    p.tensor.wait_ge(rs, rv)
                i = p.tensor.matmul(pool.ps_a[pb][:, 0:512], ktile,
                                    qf[:, h * 512: (h + 1) * 512],
                                    start=True, stop=True)
                i.then_inc(s_smm.h, 1)
                s_smm.n += 1
                ptslot = pt_sb[:, (ptg * 4 + jt) * 512: (ptg * 4 + jt) * 512 + 512]
                if local:
                    p.vector.wait_ge(s_smm.h, s_smm.n)
                    p.vector.wait_ge(s_exp.h, s_exp.n)
                    p.vector.tensor_tensor(out=sm_sb[:, 0:512],
                                           in0=pool.ps_a[pb][:, 0:512],
                                           in1=dmasks[:, jt * 512:(jt + 1) * 512],
                                           op=ALU.add).then_inc(s_msk.h, 1)
                    s_msk.n += 1
                    bank_release[u] = (s_msk.h, s_msk.n)
                    exp_in = sm_sb[:, 0:512]
                    bias_ap = bufs["bcol"][:, 0:1]
                    p.scalar.wait_ge(s_msk.h, s_msk.n)
                else:
                    exp_in = pool.ps_a[pb][:, 0:512]
                    bias_ap = bmat[:, qs * 64 + g: qs * 64 + g + 1]
                    p.scalar.wait_ge(s_smm.h, s_smm.n)
                if jt == 0 and gi >= 2:
                    p.scalar.wait_ge(s_ptv.h, ptv_after_gi[gi - 2])
                i = p.scalar.activation(out=ptslot, in_=exp_in,
                                        func=AF.Exp, bias=bias_ap, scale=L1_SCALE)
                i.then_inc(s_exp.h, 1)
                s_exp.n += 1
                if not local:
                    bank_release[u] = (s_exp.h, s_exp.n)
                p.tensor.wait_ge(s_exp.h, s_exp.n)
                p.tensor.matmul(dbank[:, 0:512], pool.ones128_bf[:, :],
                                ptslot, start=(jt == 0), stop=(jt == 3))
                i = p.tensor.matmul(obank[:, 0:512], vtile,
                                    ptslot, start=(jt == 0), stop=(jt == 3))
                i.then_inc(s_ptv.h, 1)
                s_ptv.n += 1
                u += 1
            ptv_after_gi.append(s_ptv.n)
            p.vector.wait_ge(s_ptv.h, s_ptv.n)
            p.vector.scalar_tensor_tensor(
                out=oacc[:, h * 512: (h + 1) * 512],
                in0=obank[:, 0:512], scalar=1.0,
                in1=oacc[:, h * 512: (h + 1) * 512],
                op0=ALU.mult, op1=ALU.add).then_inc(s_oev.h, 1)
            s_oev.n += 1
            p.vector.scalar_tensor_tensor(
                out=dacc_sl(h), in0=dbank[0:1, 0:512], scalar=1.0,
                in1=dacc_sl(h), op0=ALU.mult, op1=ALU.add).then_inc(s_oev.h, 1)
            s_oev.n += 1
            oev_after_gi[gi] = s_oev.n
            gi += 1
        ptv_after_sb.append(s_ptv.n)
    st = dict(s_oev=s_oev, s_dev=s_oev, s_ptv=s_ptv, s_ch=s_ch, vedge=vedge)
    return st


def emit_l1ffn_norm_wp(c: Ctx, pool: Pool, name, y_src, y_dst, TLOCAL, wdr,
                       bufs, st, qs, wo_gate=None, half=0.5):
    p = c.p
    s_oev, s_dev, s_ptv = st["s_oev"], st["s_dev"], st["s_ptv"]
    s_ch, vedge = st["s_ch"], st["vedge"]
    oacc = bufs["oacc"]
    dacc_sl = bufs["dacc_sl"]
    s_nb = c.sem(f"{name}_nb")
    s_ymul = c.sem(f"{name}_ymul")
    s_xld = c.sem(f"{name}_xld")
    s_st = c.sem(f"{name}_st")
    y_sb = bufs["y_sb"]
    rec_row = bufs["rec_row"]
    q0 = qs * 512
    for h in range(HEADS):
        p.vector.wait_ge(s_oev.h, s_oev.n)
        p.vector.wait_ge(s_dev.h, s_dev.n)
        if h > 0:
            p.vector.wait_ge(s_ymul.h, s_ymul.n)
        i = p.vector.reciprocal(out=rec_row[0:1, 0:512], in_=dacc_sl(h))
        vedge(i)
        p.tensor.wait_ge(s_ch.h, s_ch.n)
        if h > 0:
            p.tensor.wait_ge(s_ymul.h, s_ymul.n)
        i = p.tensor.matmul(pool.ps_c[1][:, 0:512], pool.ones_row[:, :],
                            rec_row[0:1, 0:512], start=True, stop=True)
        i.then_inc(s_nb.h, 1)
        s_nb.n += 1
        p.vector.wait_ge(s_nb.h, s_nb.n)
        p.vector.tensor_tensor(out=y_sb[:, h * 512: (h + 1) * 512],
                               in0=oacc[:, h * 512: (h + 1) * 512],
                               in1=pool.ps_c[1][:, 0:512],
                               op=ALU.mult).then_inc(s_ymul.h, 1)
        s_ymul.n += 1
    c.wait("gpsimd", s_ptv)
    for dc in range(DC):
        c.dma("gpsimd", bufs["x_tb"][:, dc * 512: dc * 512 + 512],
              y_src(dc, q0, 512), sem=s_xld)

    def evac_wp(mc, t0, tsz, ps_ap, s_mm, s_ev):
        p.vector.wait_ge(s_mm.h, s_mm.n)
        p.vector.wait_ge(s_xld.h, s_xld.n)
        p.vector.scalar_tensor_tensor(
            out=bufs["x_tb"][:, mc * 512: mc * 512 + tsz],
            in0=ps_ap[:, 0:tsz], scalar=half,
            in1=bufs["x_tb"][:, mc * 512: mc * 512 + tsz],
            op0=ALU.mult, op1=ALU.add).then_inc(s_ev.h, 1)
        s_ev.n += 1

    mm_wp, ev_wp = emit_proj(c, pool, f"{name}_wpj", wdr["wph"], 16, 16,
                             y_sb, 512, 0, 512, evac_wp,
                             s_ymul, bufs["wpb"], wgate=wo_gate,
                             psum_pair=pool.ps_b)
    c.wait("gpsimd", ev_wp)
    for dc in range(DC):
        c.dma("gpsimd", y_dst(dc, q0, 512),
              bufs["x_tb"][:, dc * 512: dc * 512 + 512], sem=s_st)
    c.p.vector.wait_ge(mm_wp.h, mm_wp.n)
    return s_st, (mm_wp, mm_wp.n)


def build_att0_launch(TLOCAL=1024):
    EXTW = 128 + TLOCAL + 128
    NQT = TLOCAL // 128
    nc = bass.Bass("TRN2", target_bir_lowering=False, debug=False)
    xe_in = nc.dram_tensor("xe", [D, EXTW], F32, kind="ExternalInput").ap()
    mown_in = nc.dram_tensor("mown", [128, 128], F32, kind="ExternalInput").ap()
    mprev_in = nc.dram_tensor("mprev", [128, NQT * 128], F32, kind="ExternalInput").ap()
    mpers_in = nc.dram_tensor("mpers", [128, NQT * 16], F32, kind="ExternalInput").ap()
    wq_d = nc.dram_tensor("wqh", [16, 128, DC * 128], BF16, kind="ExternalInput").ap()
    wkv_d = nc.dram_tensor("wkvh", [4, 128, DC * 128], BF16, kind="ExternalInput").ap()
    wk_d = nc.dram_tensor("wkh", [16, 128, 4 * 128], BF16, kind="ExternalInput").ap()
    wv_d = nc.dram_tensor("wvh", [16, 128, 4 * 128], BF16, kind="ExternalInput").ap()
    wo_d = nc.dram_tensor("woh", [16, 128, DC * 128], BF16, kind="ExternalInput").ap()
    y_out = nc.dram_tensor("y", [D, TLOCAL], F32, kind="ExternalOutput").ap()

    with ExitStack() as stack:
        c = Ctx(nc, stack)
        pool = Pool(c)
        u1 = c.sbuf("u1", [128, 16 * EXTW], BF16)
        qy = c.sbuf("qy", [128, 16 * TLOCAL], BF16)
        k_sb = c.sbuf("k_sb", [128, 16 * EXTW], BF16)
        lat = c.sbuf("lat", [128, 4 * EXTW], BF16)
        x_tb = c.sbuf("x_tb", [128, 16 * 512], F32)
        vstg = [c.sbuf(f"vstg{i}", [128, 512], F32) for i in range(2)]
        p_sb = c.sbuf("p_sb", [128, 272], F32)
        sm_sb = c.sbuf("sm_sb", [128, 272], F32)
        pn_sb = c.sbuf("pn_sb", [128, 272], F32)
        pt_sb = c.sbuf("pt_sb", [128, 3 * 128], BF16)
        wpb = [c.sbuf(f"wpb{i}", [128, 16 * 128], BF16) for i in range(2)]
        bcol = c.sbuf("bcol", [128, 1], F32)
        den = c.sbuf("den", [128, 1], F32)
        rcol = c.sbuf("rcol", [128, 1], F32)
        small = c.sbuf("small", [1, 64], F32)
        mown_sb = c.sbuf("mown_sb", [128, 128], F32)
        mprev_sb = c.sbuf("mprev_sb", [128, NQT * 128], F32)
        mpers_sb = c.sbuf("mpers_sb", [128, NQT * 16], F32)

        pool.emit_consts()
        masks = emit_ident(c, "msk")
        s_mld = c.sem("mld")
        c.dma("sync", mown_sb[:], mown_in[:, :], sem=s_mld)
        c.dma("sync", mprev_sb[:], mprev_in[:, :], sem=s_mld)
        c.dma("sync", mpers_sb[:], mpers_in[:, :], sem=s_mld)
        for e in ("tensor", "vector", "scalar"):
            c.wait(e, masks["s_msk"])
            c.wait(e, s_mld)

        bufs = dict(u1=u1, qy_sb=qy, k_sb=k_sb, lat_sb=lat, x_tb=x_tb,
                    vstg=vstg, p_sb=p_sb, pn_sb=pn_sb, pt_sb=pt_sb, wpb=wpb,
                    masks=masks, bcol=bcol, den=den, rcol=rcol, small=small,
                    sm_sb=sm_sb,
                    mown_sb=mown_sb, mprev_sb=mprev_sb, mpers_sb=mpers_sb)
        wdr = dict(wqh=[wq_d[i] for i in range(16)],
                   wkvh=[wkv_d[i] for i in range(4)],
                   wkh=[wk_d[i] for i in range(16)],
                   wvh=[wv_d[i] for i in range(16)],
                   woh=[wo_d[i] for i in range(16)])

        def x_src(dc, t0, tsz):
            return xe_in[dc * 128:(dc + 1) * 128, t0:t0 + tsz]

        def y_src(dc, t0, tsz):
            return xe_in[dc * 128:(dc + 1) * 128, 128 + t0:128 + t0 + tsz]

        def y_dst(dc, t0, tsz):
            return y_out[dc * 128:(dc + 1) * 128, t0:t0 + tsz]

        s_st = emit_att0(c, pool, "a0", x_src, y_src, y_dst, TLOCAL, wdr, bufs)
        c.wait("gpsimd", s_st)
        with nc.Block() as block:
            c.p.replay(block)
    return nc


def prep_att0_weights(Wq, Wkv, Wk, Wv, Wo):
    return (host_tile_lhsT(np.asarray(Wq, np.float32)),
            host_tile_lhsT(np.asarray(Wkv, np.float32)),
            host_tile_lhsT(np.asarray(Wk, np.float32)),
            host_tile_lhsT(np.asarray(Wv, np.float32)),
            host_tile_lhsT(np.asarray(Wo, np.float32)))


def build_rwkv_launch(TLOCAL=1024):
    EXTW = 264 + TLOCAL
    SCANW = 256 + TLOCAL
    nc = bass.Bass("TRN2", target_bir_lowering=False, debug=False)
    xe_in = nc.dram_tensor("xe", [D, EXTW], F32, kind="ExternalInput").ap()
    wk_d = nc.dram_tensor("wkh", [16, 128, DC * 128], BF16, kind="ExternalInput").ap()
    wv_d = nc.dram_tensor("wvh", [16, 128, DC * 128], BF16, kind="ExternalInput").ap()
    wr_d = nc.dram_tensor("wrh", [16, 128, DC * 128], BF16, kind="ExternalInput").ap()
    td_in = nc.dram_tensor("td", [128, 16], F32, kind="ExternalInput").ap()
    hm_in = nc.dram_tensor("hm", [128, SCANW], F32, kind="ExternalInput").ap()
    y_out = nc.dram_tensor("y", [D, TLOCAL], F32, kind="ExternalOutput").ap()

    with ExitStack() as stack:
        c = Ctx(nc, stack)
        pool = Pool(c)
        ln_sb = c.sbuf("ln_sb", [128, DC * EXTW], BF16)
        x_tb = c.sbuf("x_tb", [128, DC * 512], F32)
        wpb3 = [c.sbuf(f"wpb3_{i}", [128, DC * 128], BF16) for i in range(3)]
        ek = c.sbuf("ek", [128, SCANW], F32)
        hmk = c.sbuf("hmk", [128, SCANW], F32)
        v_b = c.sbuf("v_b", [128, SCANW], F32)
        a_b = c.sbuf("a_b", [128, SCANW], F32)
        num = c.sbuf("num", [128, SCANW], F32)
        den = c.sbuf("den", [128, SCANW], F32)
        rec = c.sbuf("rec", [128, TLOCAL], F32)
        wkv = c.sbuf("wkv", [128, TLOCAL], F32)
        outb = c.sbuf("outb", [128, TLOCAL], F32)
        r_sb = c.sbuf("r_sb", [128, TLOCAL], F32)
        hm_sb = c.sbuf("hm_sb", [128, SCANW], F32)
        td_sb = c.sbuf("td_sb", [128, 16], F32)
        e1_sb = c.sbuf("e1_sb", [128, 16], F32)
        w_sb = c.sbuf("w_sb", [128, 16], F32)

        pool.emit_consts()
        s_ld0 = c.sem("ld0")
        c.dma("sync", td_sb[:], td_in[:, :], sem=s_ld0)
        c.dma("sync", hm_sb[:], hm_in[:, :], sem=s_ld0)
        for e in ("scalar", "vector"):
            c.wait(e, s_ld0)

        bufs = dict(ln_sb=ln_sb, x_tb=x_tb, wpb3=wpb3, ek=ek, hmk=hmk, v=v_b,
                    a=a_b, num=num, den=den, rec=rec, wkv=wkv, outb=outb,
                    r_sb=r_sb, hm_sb=hm_sb, td_sb=td_sb, e1_sb=e1_sb, w_sb=w_sb)
        wdr = dict(wkh=[wk_d[i] for i in range(16)],
                   wvh=[wv_d[i] for i in range(16)],
                   wrh=[wr_d[i] for i in range(16)])

        def x_src(dc, t0, tsz):
            return xe_in[dc * 128:(dc + 1) * 128, t0:t0 + tsz]

        def y_src(dc, t0, tsz):
            return xe_in[dc * 128:(dc + 1) * 128, 264 + t0:264 + t0 + tsz]

        def y_dst(dc, t0, tsz):
            return y_out[dc * 128:(dc + 1) * 128, t0:t0 + tsz]

        s_st = emit_rwkv(c, pool, "rw", x_src, y_src, y_dst, TLOCAL, wdr, bufs)
        c.wait("gpsimd", s_st)
        with nc.Block() as block:
            c.p.replay(block)
    return nc


def prep_rwkv_weights(Wk, Wv, Wr, td):
    return (host_tile_lhsT(np.asarray(Wk, np.float32).T),
            host_tile_lhsT(np.asarray(Wv, np.float32).T),
            host_tile_lhsT(np.asarray(Wr, np.float32).T),
            f32c(np.asarray(td).reshape(16, 128).T))


def build_l1pub_launch(TLOCAL=1024, dbg=False):
    NT = TLOCAL // 128
    nc = bass.Bass("TRN2", target_bir_lowering=False, debug=False)
    xe_in = nc.dram_tensor("xe", [D, TLOCAL], F32, kind="ExternalInput").ap()
    wa_d = nc.dram_tensor("waph", [48, 128, DC * 128], BF16, kind="ExternalInput").ap()
    ws_in = nc.dram_tensor("wstd", [1, 16], F32, kind="ExternalInput").ap()
    wr_in = nc.dram_tensor("wrec", [1, 16], F32, kind="ExternalInput").ap()
    qf_o = nc.dram_tensor("qf", [128, 16 * TLOCAL], BF16, kind="ExternalOutput").ap()
    kf_o = nc.dram_tensor("kf", [128, 16 * TLOCAL], BF16, kind="ExternalOutput").ap()
    vtm_o = nc.dram_tensor("vtm", [128, NT * 2048], BF16, kind="ExternalOutput").ap()
    nrm_o = nc.dram_tensor("nrm", [1, 2], F32, kind="ExternalOutput").ap()

    with ExitStack() as stack:
        c = Ctx(nc, stack)
        pool = Pool(c)
        fused = c.sbuf("fused", [128, 48 * TLOCAL], BF16)
        xn_sb = c.sbuf("xn_sb", [128, DC * TLOCAL], BF16)
        x_tb = c.sbuf("x_tb", [128, DC * 256], F32)
        stage = [c.sbuf(f"stage{i}", [128, 2048], BF16) for i in range(2)]
        wpb = [c.sbuf(f"wpb{i}", [128, DC * 128], BF16) for i in range(2)]
        vt32 = [c.sbuf(f"vt32_{i}", [128, 128], F32) for i in range(2)]
        colscale = c.sbuf("colscale", [128, 48], F32)
        swi = c.sbuf("swi", [128, 16], I32)
        swf = c.sbuf("swf", [128, 16], F32)
        sg = c.sbuf("sg", [128, 16], F32)
        l1tmp = c.sbuf("l1tmp", [1, 64], F32)
        wstd_row = c.sbuf("wstd_row", [1, 16], F32)
        wrec_row = c.sbuf("wrec_row", [1, 16], F32)

        pool.emit_consts()
        masks = emit_ident(c, "id")
        for e in ("tensor", "vector", "scalar"):
            c.wait(e, masks["s_msk"])
        s_ws = c.sem("wsld")
        c.dma("sync", wstd_row[:], ws_in[:, :], sem=s_ws)
        c.dma("sync", wrec_row[:], wr_in[:, :], sem=s_ws)

        bufs = dict(fused_sb=fused, xn_sb=xn_sb, x_tb=x_tb, stage=stage,
                    wpb=wpb, vt32=vt32, colscale=colscale, swi=swi, swf=swf,
                    sg=sg, l1tmp=l1tmp, wstd_row=wstd_row, wrec_row=wrec_row,
                    s_wsin=s_ws, masks=masks, nsq=pool.sq, xtb_cols=256)
        wdr = dict(waph=[wa_d[i] for i in range(48)])
        outs = dict(qf=qf_o, kf=kf_o, vtm=vtm_o, nrm=nrm_o)

        def x_src(dc, t0, tsz):
            return xe_in[dc * 128:(dc + 1) * 128, t0:t0 + tsz]

        s_pv, s_pq = emit_l1ffn_pub(c, pool, "pub", x_src, TLOCAL, wdr, bufs, outs)
        c.wait("sync", s_pv)
        c.wait("sync", s_pq)
        if dbg:
            d_cs = nc.dram_tensor("d_cs", [128, 48], F32, kind="ExternalOutput").ap()
            d_swi = nc.dram_tensor("d_swi", [128, 16], I32, kind="ExternalOutput").ap()
            s_dbg = c.sem("dbg")
            c.dma("sync", d_cs[:, :], colscale[:], sem=s_dbg)
            c.dma("sync", d_swi[:, :], swi[:], sem=s_dbg)
            c.wait("sync", s_dbg)
        with nc.Block() as block:
            c.p.replay(block)
    return nc


def build_l1flash_launch(TLOCAL=1024, slot_bands=(8, 8)):
    nc = bass.Bass("TRN2", target_bir_lowering=False, debug=False)
    xe_in = nc.dram_tensor("xe", [D, TLOCAL], F32, kind="ExternalInput").ap()
    qf_in = nc.dram_tensor("qf", [128, 16 * TLOCAL], BF16, kind="ExternalInput").ap()
    kg_in = nc.dram_tensor("kg", [8, 128, 16 * 1024], BF16, kind="ExternalInput").ap()
    vg_in = nc.dram_tensor("vg", [8, 128, 8 * 2048], BF16, kind="ExternalInput").ap()
    kfl_in = nc.dram_tensor("kfl", [128, 16 * 1024], BF16, kind="ExternalInput").ap()
    vtl_in = nc.dram_tensor("vtl", [128, 8 * 2048], BF16, kind="ExternalInput").ap()
    nrm_in = nc.dram_tensor("nrmv", [1, 20], F32, kind="ExternalInput").ap()
    tm_in = nc.dram_tensor("tmask", [128, 128], F32, kind="ExternalInput").ap()
    wp_d = nc.dram_tensor("wph", [16, 128, 16 * 128], BF16, kind="ExternalInput").ap()
    y_out = nc.dram_tensor("y", [D, TLOCAL], F32, kind="ExternalOutput").ap()

    with ExitStack() as stack:
        c = Ctx(nc, stack)
        pool = Pool(c, ln=False)
        qf_sb = c.sbuf("qf_sb", [128, 16 * 512], BF16)
        oacc = c.sbuf("oacc", [128, 16 * 512], F32)
        dacc = c.sbuf("dacc", [128, 6 * 512], F32)
        kband = [c.sbuf(f"kband{i}", [128, 16 * 512], BF16) for i in range(2)]
        vband = [c.sbuf(f"vband{i}", [128, 4 * 2048], BF16) for i in range(2)]
        pt_sb = c.sbuf("pt_sb", [128, 8 * 512], BF16)
        sm_sb = c.sbuf("sm_sb", [128, 512], F32)
        bmat = c.sbuf("bmat", [128, 130], F32)
        tmask_sb = c.sbuf("tmask_sb", [128, 128], F32)
        bcol = c.sbuf("bcolf", [128, 1], F32)
        nrm_sb = c.sbuf("nrm_sb", [1, 20], F32)
        rec_row = c.sbuf("rec_row", [1, 512], F32)
        x_tb = c.sbuf("x_tbf", [128, DC * 512], F32)
        wpb = [c.sbuf(f"wpbf{i}", [128, DC * 128], BF16) for i in range(2)]

        pool.emit_consts()
        dmasks, s_dm = emit_l1ffn_diagmasks(c, "dm")
        s_in = c.sem("flin")
        c.dma("sync", nrm_sb[:], nrm_in[:, :], sem=s_in)
        c.dma("sync", tmask_sb[:], tm_in[:, :], sem=s_in)

        bufs = dict(qf_sb=qf_sb, kband=kband, vband=vband, pt_sb=pt_sb,
                    bmat=bmat, tmask_sb=tmask_sb, bcol=bcol, nrm_sb=nrm_sb,
                    sm_sb=sm_sb, oacc=oacc, dacc=dacc, dmasks=(dmasks, s_dm),
                    y_sb=qf_sb, x_tb=x_tb, wpb=wpb, rec_row=rec_row,
                    s_nrm_in=s_in, s_tm_in=s_in)
        wdr = dict(kg=[kg_in[i][:, :] for i in range(8)],
                   vg=[vg_in[i][:, :] for i in range(8)],
                   kfl=kfl_in[:, :], vtl=vtl_in[:, :],
                   wph=[wp_d[i] for i in range(16)])

        def y_src(dc, t0, tsz):
            return xe_in[dc * 128:(dc + 1) * 128, t0:t0 + tsz]

        def y_dst(dc, t0, tsz):
            return y_out[dc * 128:(dc + 1) * 128, t0:t0 + tsz]

        wo_gate = None
        s_st = None
        for qs in range(TLOCAL // 512):
            s_qf = c.sem("qfld")
            if wo_gate is not None:
                c.p.sync.wait_ge(wo_gate[0].h, wo_gate[1])
            for h in range(HEADS):
                c.dma("sync", qf_sb[:, h * 512:(h + 1) * 512],
                      qf_in[:, h * TLOCAL + qs * 512: h * TLOCAL + qs * 512 + 512],
                      sem=s_qf)
            bufs["s_qf"] = s_qf
            st = emit_l1ffn_flash(c, pool, "fl", TLOCAL, wdr, bufs, qs,
                                  slot_bands[qs])
            s_st, wo_gate = emit_l1ffn_norm_wp(c, pool, "wp", y_src, y_dst,
                                               TLOCAL, wdr, bufs, st, qs, wo_gate)
        c.wait("gpsimd", s_st)
        with nc.Block() as block:
            c.p.replay(block)
    return nc


def prep_l1ffn_weights(Wa, Wp):
    Ds, R = 120, 8
    q_std_end = HEADS * Ds
    k_std_end = 2 * q_std_end
    v_end = k_std_end + D
    q_low_end = v_end + HEADS * R
    perm = []
    for h in range(HEADS):
        perm += list(range(h * Ds, (h + 1) * Ds))
        perm += list(range(v_end + h * R, v_end + (h + 1) * R))
    for h in range(HEADS):
        perm += list(range(q_std_end + h * Ds, q_std_end + (h + 1) * Ds))
        perm += list(range(q_low_end + h * R, q_low_end + (h + 1) * R))
    perm += list(range(k_std_end, v_end))
    WaP = np.ascontiguousarray(np.asarray(Wa, np.float32)[:, perm])
    return host_tile_lhsT(WaP), host_tile_lhsT(np.asarray(Wp, np.float32))


def make_l1_tmask_host(my_chunks):
    m = np.zeros((128, 128), np.float32)
    for (qs, qc) in enumerate(my_chunks):
        for g in range(64):
            if not (g < 4 * qc):
                m[:, qs * 64 + g] = NEG
    return m


def build_fused_launch(TLOC=1024):
    EXTA = 128 + TLOC + 128
    RH = 264
    EXTR = RH + TLOC
    SCANW = 256 + TLOC
    NT = TLOC // 128
    nc = bass.Bass("TRN2", target_bir_lowering=False, debug=False)

    xe0 = nc.dram_tensor("xe0", [D, EXTA], F32, kind="ExternalInput").ap()
    w0h_d = nc.dram_tensor("w0h", [64, 128, DC * 128], BF16, kind="ExternalInput").ap()
    w1q_d = nc.dram_tensor("w1q", [64, 128, 16 * 128], BF16, kind="ExternalInput").ap()
    b0_d = nc.dram_tensor("b0c", [128, 64], F32, kind="ExternalInput").ap()
    b1_d = nc.dram_tensor("b1r", [1, D], F32, kind="ExternalInput").ap()
    wq_d = nc.dram_tensor("wqh", [16, 128, DC * 128], BF16, kind="ExternalInput").ap()
    wkv_d = nc.dram_tensor("wkvh", [4, 128, DC * 128], BF16, kind="ExternalInput").ap()
    wk_d = nc.dram_tensor("wkh", [16, 128, 4 * 128], BF16, kind="ExternalInput").ap()
    wv_d = nc.dram_tensor("wvh", [16, 128, 4 * 128], BF16, kind="ExternalInput").ap()
    wo_d = nc.dram_tensor("woh", [16, 128, DC * 128], BF16, kind="ExternalInput").ap()
    mown_in = nc.dram_tensor("mown", [128, 128], F32, kind="ExternalInput").ap()
    mprev_in = nc.dram_tensor("mprev", [128, NT * 128], F32, kind="ExternalInput").ap()
    mpers_in = nc.dram_tensor("mpers", [128, NT * 16], F32, kind="ExternalInput").ap()
    rk_d = nc.dram_tensor("rkh", [16, 128, DC * 128], BF16, kind="ExternalInput").ap()
    rv_d = nc.dram_tensor("rvh", [16, 128, DC * 128], BF16, kind="ExternalInput").ap()
    rr_d = nc.dram_tensor("rrh", [16, 128, DC * 128], BF16, kind="ExternalInput").ap()
    td_in = nc.dram_tensor("td", [128, 16], F32, kind="ExternalInput").ap()
    hm_in = nc.dram_tensor("hm", [128, SCANW], F32, kind="ExternalInput").ap()
    wa_d = nc.dram_tensor("waph", [48, 128, DC * 128], BF16, kind="ExternalInput").ap()
    wp_d = nc.dram_tensor("wph", [16, 128, 16 * 128], BF16, kind="ExternalInput").ap()
    ws_in = nc.dram_tensor("wstd", [1, 16], F32, kind="ExternalInput").ap()
    wr_in = nc.dram_tensor("wrec", [1, 16], F32, kind="ExternalInput").ap()
    tm_in = nc.dram_tensor("tmask", [128, 128], F32, kind="ExternalInput").ap()
    hsel_in = nc.dram_tensor("hsel", [128, 8], F32, kind="ExternalInput").ap()
    y_out = nc.dram_tensor("y", [D, TLOC], F32, kind="ExternalOutput").ap()

    xA = nc.dram_tensor("xA", [D, EXTA], F32, kind="Internal").ap()
    xB = nc.dram_tensor("xB", [D, TLOC], F32, kind="Internal").ap()
    xC = nc.dram_tensor("xC", [D, TLOC], F32, kind="Internal").ap()
    xD = nc.dram_tensor("xD", [D, TLOC], F32, kind="Internal").ap()
    xE = nc.dram_tensor("xE", [D, TLOC], F32, kind="Internal").ap()
    xh = nc.dram_tensor("xh", [D, RH], F32, kind="Internal").ap()
    pb_in_t = nc.dram_tensor("pb_in", [128, 32768], BF16, kind="Internal")
    pb_out_t = nc.dram_tensor("pb_out", [1024, 32768], BF16, kind="Internal")
    nr_in_t = nc.dram_tensor("nr_in", [1, 2], F32, kind="Internal")
    nr_out_t = nc.dram_tensor("nr_out", [8, 2], F32, kind="Internal")
    rh_in_t = nc.dram_tensor("rh_in", [128, DC * RH], F32, kind="Internal")
    rh_out_t = nc.dram_tensor("rh_out", [1024, DC * RH], F32, kind="Internal")
    qfA = nc.dram_tensor("qfA", [128, 16 * TLOC], BF16, kind="Internal").ap()
    pb_in, pb_out = pb_in_t.ap(), pb_out_t.ap()
    nr_in, nr_out = nr_in_t.ap(), nr_out_t.ap()
    rh_in, rh_out = rh_in_t.ap(), rh_out_t.ap()

    def dsl(ap, dc, t0, tsz):
        return ap[dc * 128:(dc + 1) * 128, t0:t0 + tsz]

    with ExitStack() as stack:
        c = Ctx(nc, stack)
        p = c.p
        pool = Pool(c)
        pool.emit_consts()
        idents = emit_ident(c, "id")

        mown_sb = c.sbuf("mown_sb", [128, 128], F32)
        mprev_sb = c.sbuf("mprev_sb", [128, NT * 128], F32)
        mpers_sb = c.sbuf("mpers_sb", [128, NT * 16], F32)
        td_sb = c.sbuf("td_sb", [128, 16], F32)
        tmask_sb = c.sbuf("tmask_sb", [128, 128], F32)
        hsel_sb = c.sbuf("hsel_sb", [128, 8], F32)
        wstd_row = c.sbuf("wstd_row", [1, 16], F32)
        wrec_row = c.sbuf("wrec_row", [1, 16], F32)
        colscale = c.sbuf("colscale", [128, 48], F32)
        swi = c.sbuf("swi", [128, 16], I32)
        swf = c.sbuf("swf", [128, 16], F32)
        sgb = c.sbuf("sgb", [128, 16], F32)
        l1tmp = c.sbuf("l1tmp", [1, 64], F32)

        s_cld = c.sem("cld")
        for (dst, srcap) in ((mown_sb, mown_in),
                             (mprev_sb, mprev_in), (mpers_sb, mpers_in),
                             (td_sb, td_in), (tmask_sb, tm_in),
                             (hsel_sb, hsel_in), (wstd_row, ws_in),
                             (wrec_row, wr_in)):
            c.dma("sync", dst[:], srcap[:, :], sem=s_cld)

        def barrier(*sems):
            for e in ENGINES:
                for s in sems:
                    c.wait(e, s)

        barrier(pool.s_cst, idents["s_msk"], s_cld)

        wdr_ffn = dict(w0h=[w0h_d[i] for i in range(64)],
                       w1q=[w1q_d[i] for i in range(64)])
        with ExitStack() as st1:
            e1 = st1.enter_context
            bufs1 = dict(
                x_tb=e1(nc.sbuf_tensor("p1_x", [128, DC * 512], F32)),
                xn_sb=e1(nc.sbuf_tensor("p1_xn", [128, DC * 512], BF16)),
                w0b=[e1(nc.sbuf_tensor(f"p1_w0{i}", [128, DC * 128], BF16)) for i in range(2)],
                w1b=[e1(nc.sbuf_tensor(f"p1_w1{i}", [128, 16 * 128], BF16)) for i in range(4)],
                h_sb=e1(nc.sbuf_tensor("p1_h", [128, 64 * 512], BF16)),
                bias0c=e1(nc.sbuf_tensor("p1_b0", [128, 64], F32)),
                bias1r=e1(nc.sbuf_tensor("p1_b1", [1, D], F32)))
            s_bld = c.sem("fbld")
            c.dma("sync", bufs1["bias0c"][:], b0_d[:, :], sem=s_bld)
            c.dma("sync", bufs1["bias1r"][:], b1_d[:, :], sem=s_bld)
            for e in ("scalar", "tensor"):
                c.wait(e, s_bld)
            s_st1 = emit_ffn0_dram(c, pool, "f",
                                   lambda dc, t0, tsz: dsl(xe0, dc, t0, tsz),
                                   lambda dc, t0, tsz: dsl(xA, dc, t0, tsz),
                                   EXTA, bufs1, wdr_ffn)

        wdr_att = dict(wqh=[wq_d[i] for i in range(16)],
                       wkvh=[wkv_d[i] for i in range(4)],
                       wkh=[wk_d[i] for i in range(16)],
                       wvh=[wv_d[i] for i in range(16)],
                       woh=[wo_d[i] for i in range(16)])
        with ExitStack() as st2:
            e2 = st2.enter_context
            bufs2 = dict(
                u1=e2(nc.sbuf_tensor("p2_u1", [128, 16 * EXTA], BF16)),
                qy_sb=e2(nc.sbuf_tensor("p2_qy", [128, 16 * TLOC], BF16)),
                k_sb=e2(nc.sbuf_tensor("p2_k", [128, 16 * EXTA], BF16)),
                lat_sb=e2(nc.sbuf_tensor("p2_lat", [128, 4 * EXTA], BF16)),
                x_tb=e2(nc.sbuf_tensor("p2_x", [128, DC * 512], F32)),
                vstg=[e2(nc.sbuf_tensor(f"p2_vs{i}", [128, 512], F32)) for i in range(2)],
                p_sb=e2(nc.sbuf_tensor("p2_p", [128, 272], F32)),
                sm_sb=e2(nc.sbuf_tensor("p2_sm", [128, 272], F32)),
                pn_sb=e2(nc.sbuf_tensor("p2_pn", [128, 272], F32)),
                pt_sb=e2(nc.sbuf_tensor("p2_pt", [128, 3 * 128], BF16)),
                wpb=[e2(nc.sbuf_tensor(f"p2_wp{i}", [128, 16 * 128], BF16)) for i in range(2)],
                bcol=e2(nc.sbuf_tensor("p2_bc", [128, 1], F32)),
                den=e2(nc.sbuf_tensor("p2_dn", [128, 1], F32)),
                rcol=e2(nc.sbuf_tensor("p2_rc", [128, 1], F32)),
                small=e2(nc.sbuf_tensor("p2_sml", [1, 64], F32)),
                masks=idents, mown_sb=mown_sb, mprev_sb=mprev_sb,
                mpers_sb=mpers_sb)
            s_st2 = emit_att0(c, pool, "a",
                              lambda dc, t0, tsz: dsl(xA, dc, t0, tsz),
                              lambda dc, t0, tsz: dsl(xA, dc, 128 + t0, tsz),
                              lambda dc, t0, tsz: dsl(xB, dc, t0, tsz),
                              TLOC, wdr_att, bufs2)
        c.wait("sync", s_st2)

        with ExitStack() as st3:
            e3 = st3.enter_context
            bufs3 = dict(
                x_tb=e3(nc.sbuf_tensor("p3_x", [128, DC * 512], F32)),
                xn_sb=e3(nc.sbuf_tensor("p3_xn", [128, DC * 512], BF16)),
                w0b=[e3(nc.sbuf_tensor(f"p3_w0{i}", [128, DC * 128], BF16)) for i in range(2)],
                w1b=[e3(nc.sbuf_tensor(f"p3_w1{i}", [128, 16 * 128], BF16)) for i in range(4)],
                h_sb=e3(nc.sbuf_tensor("p3_h", [128, 64 * 512], BF16)),
                bias0c=e3(nc.sbuf_tensor("p3_b0", [128, 64], F32)),
                bias1r=e3(nc.sbuf_tensor("p3_b1", [1, D], F32)))
            s_bld = c.sem("fbld")
            c.dma("sync", bufs3["bias0c"][:], b0_d[:, :], sem=s_bld)
            c.dma("sync", bufs3["bias1r"][:], b1_d[:, :], sem=s_bld)
            for e in ("scalar", "tensor"):
                c.wait(e, s_bld)
            s_st3 = emit_ffn0_dram(c, pool, "f",
                                   lambda dc, t0, tsz: dsl(xB, dc, t0, tsz),
                                   lambda dc, t0, tsz: dsl(xC, dc, t0, tsz),
                                   TLOC, bufs3, wdr_ffn)

        wdr_wa = dict(waph=[wa_d[i] for i in range(48)])
        wdr_fl = dict(kg=[pb_out[r * 128:(r + 1) * 128, 0:16384] for r in range(8)],
                      vg=[pb_out[r * 128:(r + 1) * 128, 16384:32768] for r in range(8)],
                      kfl=pb_in[:, 0:16384], vtl=pb_in[:, 16384:32768],
                      wph=[wp_d[i] for i in range(16)])

        def l1_call(x_in, y_dst_ap, tag):
            with ExitStack() as sp:
                ep = sp.enter_context
                bufsp = dict(
                    fused_sb=ep(nc.sbuf_tensor(f"{tag}_fused", [128, 48 * TLOC], BF16)),
                    xn_sb=ep(nc.sbuf_tensor(f"{tag}_xn", [128, DC * TLOC], BF16)),
                    x_tb=ep(nc.sbuf_tensor(f"{tag}_x", [128, DC * 256], F32)),
                    stage=[ep(nc.sbuf_tensor(f"{tag}_st{i}", [128, 2048], BF16)) for i in range(2)],
                    wpb=[ep(nc.sbuf_tensor(f"{tag}_wp{i}", [128, DC * 128], BF16)) for i in range(2)],
                    vt32=[ep(nc.sbuf_tensor(f"{tag}_v32{i}", [128, 128], F32)) for i in range(2)],
                    colscale=colscale, swi=swi, swf=swf, sg=sgb, l1tmp=l1tmp,
                    wstd_row=wstd_row, wrec_row=wrec_row, s_wsin=s_cld,
                    masks=idents, nsq=pool.sq, xtb_cols=256)
                outs = dict(qf=qfA, kf=pb_in[:, 0:16384],
                            vtm=pb_in[:, 16384:32768], nrm=nr_in[:, :])
                s_pv, s_pq = emit_l1ffn_pub(c, pool, "pub",
                                            lambda dc, t0, tsz: dsl(x_in, dc, t0, tsz),
                                            TLOC, wdr_wa, bufsp, outs)
            c.wait("gpsimd", s_pv)
            c.wait("gpsimd", s_pq)
            s_cc = c.sem(f"cc{tag}g")
            s_cn = c.sem(f"cc{tag}n")
            p.gpsimd.collective_compute(
                "AllGather", ALU.bypass, replica_groups=[list(range(8))],
                ins=[pb_in_t.ap().opt()], outs=[pb_out_t.ap().opt()]).then_inc(s_cc.h)
            s_cc.n += 1
            p.gpsimd.collective_compute(
                "AllGather", ALU.bypass, replica_groups=[list(range(8))],
                ins=[nr_in_t.ap().opt()], outs=[nr_out_t.ap().opt()]).then_inc(s_cn.h)
            s_cn.n += 1
            with ExitStack() as sf:
                ef = sf.enter_context
                bufsf = dict(
                    qf_sb=ef(nc.sbuf_tensor(f"{tag}_qf", [128, 16 * 512], BF16)),
                    oacc=ef(nc.sbuf_tensor(f"{tag}_oa", [128, 16 * 512], F32)),
                    dacc=ef(nc.sbuf_tensor(f"{tag}_da", [128, 6 * 512], F32)),
                    kband=[ef(nc.sbuf_tensor(f"{tag}_kb{i}", [128, 16 * 512], BF16)) for i in range(2)],
                    vband=[ef(nc.sbuf_tensor(f"{tag}_vb{i}", [128, 4 * 2048], BF16)) for i in range(2)],
                    pt_sb=ef(nc.sbuf_tensor(f"{tag}_pt", [128, 8 * 512], BF16)),
                    sm_sb=ef(nc.sbuf_tensor(f"{tag}_sm", [128, 512], F32)),
                    bmat=ef(nc.sbuf_tensor(f"{tag}_bm", [128, 130], F32)),
                    bcol=ef(nc.sbuf_tensor(f"{tag}_bc", [128, 1], F32)),
                    nrm_sb=ef(nc.sbuf_tensor(f"{tag}_nr", [1, 20], F32)),
                    rec_row=ef(nc.sbuf_tensor(f"{tag}_rr", [1, 512], F32)),
                    x_tb=ef(nc.sbuf_tensor(f"{tag}_xt", [128, DC * 512], F32)),
                    wpb=[ef(nc.sbuf_tensor(f"{tag}_fw{i}", [128, DC * 128], BF16)) for i in range(2)],
                    tmask_sb=tmask_sb,
                    s_tm_in=s_cld)
                dm_t = ef(nc.sbuf_tensor(f"{tag}_dm", [128, 4 * 512], BF16))
                dmasks, s_dm = emit_l1ffn_diagmasks(c, "dm", dm_t)
                bufsf["dmasks"] = (dmasks, s_dm)
                bufsf["y_sb"] = bufsf["qf_sb"]
                s_nin = c.sem("flnr")
                c.wait("sync", s_cn)
                c.dma("sync", bufsf["nrm_sb"][0:1, 0:16],
                      nr_out.rearrange("a b -> (a b)"), sem=s_nin)
                c.dma("sync", bufsf["nrm_sb"][0:1, 16:18], nr_in[0:1, 0:2], sem=s_nin)
                bufsf["s_nrm_in"] = s_nin
                wo_gate = None
                for qs in range(TLOC // 512):
                    s_qf = c.sem("flqf")
                    if wo_gate is not None:
                        c.p.sync.wait_ge(wo_gate[0].h, wo_gate[1])
                    for h in range(HEADS):
                        c.dma("sync", bufsf["qf_sb"][:, h * 512:(h + 1) * 512],
                              qfA[:, h * TLOC + qs * 512: h * TLOC + qs * 512 + 512],
                              sem=s_qf)
                    bufsf["s_qf"] = s_qf
                    if qs == 0:
                        c.wait("sync", s_cc)
                    stt = emit_l1ffn_flash(c, pool, "fl", TLOC, wdr_fl, bufsf,
                                           qs, 8)
                    s_stf, wo_gate = emit_l1ffn_norm_wp(
                        c, pool, "wp",
                        lambda dc, t0, tsz: dsl(x_in, dc, t0, tsz),
                        lambda dc, t0, tsz: dsl(y_dst_ap, dc, t0, tsz),
                        TLOC, wdr_fl, bufsf, stt, qs, wo_gate)
            return s_stf

        s_stfA = l1_call(xC, xD, "A")

        s_rp = c.sem("rhpub")
        for dc in range(DC):
            c.dma("gpsimd", rh_in[:, dc * RH:(dc + 1) * RH],
                  dsl(xD, dc, TLOC - RH, RH), sem=s_rp)
        c.wait("gpsimd", s_rp)
        s_cr = c.sem("ccr")
        p.gpsimd.collective_compute(
            "AllGather", ALU.bypass, replica_groups=[list(range(8))],
            ins=[rh_in_t.ap().opt()], outs=[rh_out_t.ap().opt()]).then_inc(s_cr.h)
        s_cr.n += 1
        c.wait("sync", s_cr)
        with ExitStack() as sh:
            eh = sh.enter_context
            hacc = eh(nc.sbuf_tensor("hacc", [128, DC * RH], F32))
            hstg = eh(nc.sbuf_tensor("hstg", [128, DC * RH], F32))
            s_hl = c.sem("rhld")
            s_ha = c.sem("rhacc")
            p.vector.memset(hacc[:], 0.0).then_inc(s_ha.h, 1)
            s_ha.n += 1
            for r in range(8):
                c.wait("sync", s_ha)
                c.dma("sync", hstg[:], rh_out[r * 128:(r + 1) * 128, :], sem=s_hl)
                c.wait("vector", s_hl)
                p.vector.scalar_tensor_tensor(
                    out=hacc[:], in0=hstg[:], scalar=hsel_sb[:, r:r + 1],
                    in1=hacc[:], op0=ALU.mult, op1=ALU.add).then_inc(s_ha.h, 1)
                s_ha.n += 1
            c.wait("sync", s_ha)
            s_hw = c.sem("rhwr")
            for dc in range(DC):
                c.dma("sync", xh[dc * 128:(dc + 1) * 128, 0:RH],
                      hacc[:, dc * RH:(dc + 1) * RH], sem=s_hw)
            c.wait("sync", s_stfA)

        wdr_r = dict(wkh=[rk_d[i] for i in range(16)],
                     wvh=[rv_d[i] for i in range(16)],
                     wrh=[rr_d[i] for i in range(16)])

        def x_src_rwkv(dc, t0, tsz):
            out = []
            if t0 < RH:
                w = min(RH - t0, tsz)
                out.append((dsl(xh, dc, t0, w), 0, w))
            if t0 + tsz > RH:
                lo = max(t0, RH)
                out.append((dsl(xD, dc, lo - RH, t0 + tsz - lo), lo - t0,
                            t0 + tsz - lo))
            return out

        with ExitStack() as s5:
            e5 = s5.enter_context
            bufs5 = dict(
                ln_sb=e5(nc.sbuf_tensor("p5_ln", [128, DC * EXTR], BF16)),
                x_tb=e5(nc.sbuf_tensor("p5_x", [128, DC * 512], F32)),
                wpb3=[e5(nc.sbuf_tensor(f"p5_w{i}", [128, DC * 128], BF16)) for i in range(3)],
                ek=e5(nc.sbuf_tensor("p5_ek", [128, SCANW], F32)),
                hmk=e5(nc.sbuf_tensor("p5_hk", [128, SCANW], F32)),
                v=e5(nc.sbuf_tensor("p5_v", [128, SCANW], F32)),
                a=e5(nc.sbuf_tensor("p5_a", [128, SCANW], F32)),
                num=e5(nc.sbuf_tensor("p5_n", [128, SCANW], F32)),
                den=e5(nc.sbuf_tensor("p5_d", [128, SCANW], F32)),
                rec=e5(nc.sbuf_tensor("p5_r", [128, TLOC], F32)),
                wkv=e5(nc.sbuf_tensor("p5_wv", [128, TLOC], F32)),
                outb=e5(nc.sbuf_tensor("p5_o", [128, TLOC], F32)),
                r_sb=e5(nc.sbuf_tensor("p5_rs", [128, TLOC], F32)),
                hm_sb=e5(nc.sbuf_tensor("p5_hm", [128, SCANW], F32)),
                td_sb=td_sb,
                e1_sb=e5(nc.sbuf_tensor("p5_e1", [128, 16], F32)),
                w_sb=e5(nc.sbuf_tensor("p5_wd", [128, 16], F32)))
            s_hm = c.sem("hmld")
            c.dma("sync", bufs5["hm_sb"][:], hm_in[:, :], sem=s_hm)
            c.wait("scalar", s_cld)
            c.wait("vector", s_cld)
            c.wait("vector", s_hm)
            s_st5 = emit_rwkv(c, pool, "rw", x_src_rwkv,
                              lambda dc, t0, tsz: dsl(xD, dc, t0, tsz),
                              lambda dc, t0, tsz: dsl(xE, dc, t0, tsz),
                              TLOC, wdr_r, bufs5)
        c.wait("sync", s_st5)

        s_stfB = l1_call(xE, y_out, "B")
        barrier(s_stfB)

        with nc.Block() as block:
            c.p.replay(block)
    return nc


_NC_CACHE = {}
_USE_FUSED = True


def _get_nc(key, builder, *a, **kw):
    if key not in _NC_CACHE:
        _NC_CACHE[key] = builder(*a, **kw)
    return _NC_CACHE[key]


def _run(nc, in_maps):
    return run_bass_kernel_spmd(nc, in_maps, core_ids=list(range(NCORES)))


def kernel(L0_attn_x, L0_attn_Wq, L0_attn_Wkv, L0_attn_Wk, L0_attn_Wv,
           L0_attn_Wo, L0_ffn_W0, L0_ffn_bias0, L0_ffn_W1, L0_ffn_bias1,
           L1_attn_Wk, L1_attn_Wv, L1_attn_Wr, L1_attn_time_decay,
           L1_ffn_Wa, L1_ffn_Wp, L1_ffn_w_std, L1_ffn_w_rec):
    TLOC = S // NCORES
    x0 = np.asarray(L0_attn_x, dtype=np.float32).reshape(S, D)
    xT = np.ascontiguousarray(x0.T)

    w0h, w1q, b0c, b1r = prep_ffn0_weights(L0_ffn_W0, L0_ffn_bias0,
                                           L0_ffn_W1, L0_ffn_bias1)
    wqh, wkvh, wkh, wvh, woh = prep_att0_weights(L0_attn_Wq, L0_attn_Wkv,
                                                 L0_attn_Wk, L0_attn_Wv, L0_attn_Wo)
    rkh, rvh, rrh, tdc = prep_rwkv_weights(L1_attn_Wk, L1_attn_Wv, L1_attn_Wr,
                                           L1_attn_time_decay)
    waph, wph = prep_l1ffn_weights(L1_ffn_Wa, L1_ffn_Wp)
    wstd = f32c(np.asarray(L1_ffn_w_std).reshape(1, 16))
    wrec = f32c(np.asarray(L1_ffn_w_rec).reshape(1, 16))

    def split(xt):
        return [np.ascontiguousarray(xt[:, c * TLOC:(c + 1) * TLOC])
                for c in range(NCORES)]

    def cat(res, key="y"):
        return np.concatenate([res.results[c][key] for c in range(NCORES)], axis=1)

    def run_ffn0(xt):
        nc = _get_nc("ffn0", build_ffn0_launch, TLOC)
        maps = [{"x": s, "w0h": w0h, "w1q": w1q, "b0c": b0c, "b1r": b1r}
                for s in split(xt)]
        return cat(_run(nc, maps))

    def run_att0(xt):
        nc = _get_nc("att0", build_att0_launch, TLOC)
        maps = []
        for c in range(NCORES):
            t0 = c * TLOC
            xe = np.zeros((D, 128 + TLOC + 128), np.float32)
            if c > 0:
                xe[:, 0:128] = xt[:, t0 - 128:t0]
            xe[:, 128:128 + TLOC] = xt[:, t0:t0 + TLOC]
            xe[:, 128 + TLOC:128 + TLOC + 16] = xt[:, 0:16]
            mown, mprev, mpers = make_att0_masks_host(8 * c, TLOC // 128)
            maps.append({"xe": xe, "wqh": wqh, "wkvh": wkvh, "wkh": wkh,
                         "wvh": wvh, "woh": woh, "mown": mown, "mprev": mprev,
                         "mpers": mpers})
        return cat(_run(nc, maps))

    def run_rwkv(xt):
        nc = _get_nc("rwkv", build_rwkv_launch, TLOC)
        SCANW = 256 + TLOC
        maps = []
        for c in range(NCORES):
            t0 = c * TLOC
            xe = np.zeros((D, 264 + TLOC), np.float32)
            if c > 0:
                xe[:, 0:264] = xt[:, t0 - 264:t0]
            xe[:, 264:] = xt[:, t0:t0 + TLOC]
            hm = np.ones((128, SCANW), np.float32)
            if c == 0:
                hm[:, 0:256] = 0.0
            maps.append({"xe": xe, "wkh": rkh, "wvh": rvh, "wrh": rrh,
                         "td": tdc, "hm": hm})
        return cat(_run(nc, maps))

    def run_l1ffn(xt):
        nc_a = _get_nc("l1pub", build_l1pub_launch, TLOC)
        maps = [{"xe": s, "waph": waph, "wstd": wstd, "wrec": wrec}
                for s in split(xt)]
        ra = _run(nc_a, maps)
        kg = np.ascontiguousarray(np.stack([ra.results[c]["kf"] for c in range(NCORES)]))
        vg = np.ascontiguousarray(np.stack([ra.results[c]["vtm"] for c in range(NCORES)]))
        nc_b = _get_nc("l1flash", build_l1flash_launch, TLOC, (8, 8))
        maps_b = []
        xs = split(xt)
        for c in range(NCORES):
            nrmv = np.zeros((1, 20), np.float32)
            for j in range(NCORES):
                nrmv[0, 2 * j] = ra.results[j]["nrm"][0, 0]
                nrmv[0, 2 * j + 1] = ra.results[j]["nrm"][0, 1]
            nrmv[0, 16] = ra.results[c]["nrm"][0, 0]
            nrmv[0, 17] = ra.results[c]["nrm"][0, 1]
            tm = make_l1_tmask_host((2 * c, 2 * c + 1))
            maps_b.append({"xe": xs[c], "qf": ra.results[c]["qf"], "kg": kg,
                           "vg": vg, "kfl": ra.results[c]["kf"],
                           "vtl": ra.results[c]["vtm"], "nrmv": nrmv,
                           "tmask": tm, "wph": wph})
        return cat(_run(nc_b, maps_b))

    if _USE_FUSED:
        EXTA = 128 + TLOC + 128
        SCANW = 256 + TLOC
        nc = _get_nc("fused", build_fused_launch, TLOC)
        maps = []
        for c in range(NCORES):
            t0 = c * TLOC
            xe0 = np.zeros((D, EXTA), np.float32)
            if c > 0:
                xe0[:, 0:128] = xT[:, t0 - 128:t0]
            xe0[:, 128:128 + TLOC] = xT[:, t0:t0 + TLOC]
            xe0[:, 128 + TLOC:128 + TLOC + 16] = xT[:, 0:16]
            mown, mprev, mpers = make_att0_masks_host(8 * c, TLOC // 128)
            hm = np.ones((128, SCANW), np.float32)
            if c == 0:
                hm[:, 0:256] = 0.0
            hsel = np.zeros((128, 8), np.float32)
            if c > 0:
                hsel[:, c - 1] = 1.0
            tm = make_l1_tmask_host((2 * c, 2 * c + 1))
            maps.append({"xe0": xe0, "w0h": w0h, "w1q": w1q, "b0c": b0c,
                         "b1r": b1r, "wqh": wqh, "wkvh": wkvh, "wkh": wkh,
                         "wvh": wvh, "woh": woh, "mown": mown, "mprev": mprev,
                         "mpers": mpers, "rkh": rkh, "rvh": rvh, "rrh": rrh,
                         "td": tdc, "hm": hm, "waph": waph, "wph": wph,
                         "wstd": wstd, "wrec": wrec, "tmask": tm, "hsel": hsel})
        res = _run(nc, maps)
        x6 = cat(res)
        return np.ascontiguousarray(x6.T).reshape(1, S, D).astype(np.float32)

    x1 = run_ffn0(xT)
    x2 = run_att0(x1)
    x3 = run_ffn0(x2)
    x4 = run_l1ffn(x3)
    x5 = run_rwkv(x4)
    x6 = run_l1ffn(x5)
    return np.ascontiguousarray(x6.T).reshape(1, S, D).astype(np.float32)


# version 08
# speedup vs baseline: 1.1348x; 1.1348x over previous
import numpy as np
import ml_dtypes
from contextlib import ExitStack

import concourse.bass as bass
import concourse.mybir as mybir
from concourse.bass_utils import run_bass_kernel_spmd

F32 = mybir.dt.float32
BF16 = mybir.dt.bfloat16
I32 = mybir.dt.int32
AF = mybir.ActivationFunctionType
ALU = mybir.AluOpType
BF16NP = ml_dtypes.bfloat16

D = 2048
DC = D // 128
S = 8192
NCORES = 8
HEADS = 16
HD = 128
NEG = -1e30

ENGINES = ("sync", "scalar", "vector", "tensor", "gpsimd")


class _Holder:
    __slots__ = ("incs",)

    def __init__(self):
        self.incs = []

    def then_inc(self, sem, amt=None):
        self.incs.append((sem, amt))
        return self


class _EngRec:

    def __init__(self, prog, name):
        self._prog = prog
        self._name = name

    def __getattr__(self, attr):
        prog, name = self._prog, self._name

        def call(*args, **kwargs):
            h = _Holder()

            def fn(e, _attr=attr, _args=args, _kwargs=kwargs, _h=h):
                inst = getattr(e, _attr)(*_args, **_kwargs)
                for (sem, amt) in _h.incs:
                    if amt is None:
                        inst.then_inc(sem)
                    else:
                        inst.then_inc(sem, amt)

            prog.fns[name].append(fn)
            return h

        return call


class Prog:
    def __init__(self):
        self.fns = {e: [] for e in ENGINES}
        self.sync = _EngRec(self, "sync")
        self.scalar = _EngRec(self, "scalar")
        self.vector = _EngRec(self, "vector")
        self.tensor = _EngRec(self, "tensor")
        self.gpsimd = _EngRec(self, "gpsimd")

    def eng(self, name):
        return getattr(self, name)

    def replay(self, block):
        @block.sync
        def _(e):
            for fn in self.fns["sync"]:
                fn(e)

        @block.scalar
        def _(e):
            for fn in self.fns["scalar"]:
                fn(e)

        @block.vector
        def _(e):
            for fn in self.fns["vector"]:
                fn(e)

        @block.tensor
        def _(e):
            for fn in self.fns["tensor"]:
                fn(e)

        @block.gpsimd
        def _(e):
            for fn in self.fns["gpsimd"]:
                fn(e)


class Sem:

    __slots__ = ("h", "name", "_cnt", "base")

    def __init__(self, handle, cnt, name=""):
        self.h = handle
        self._cnt = cnt
        self.base = cnt[0]
        self.name = name

    @property
    def n(self):
        return self._cnt[0]

    @n.setter
    def n(self, v):
        self._cnt[0] = v


import re as _re


class Ctx:

    def __init__(self, nc, stack):
        self.nc = nc
        self.stack = stack
        self.p = Prog()
        self._semcnt = 0
        self._sempool = {}

    def sem(self, name=None):
        name = name or f"s{self._semcnt}"
        self._semcnt += 1
        key = _re.sub(r"\d+", "", name)
        if key not in self._sempool:
            self._sempool[key] = (
                self.stack.enter_context(self.nc.semaphore(key)), [0])
        h, cnt = self._sempool[key]
        return Sem(h, cnt, key)

    def sbuf(self, name, shape, dtype):
        return self.stack.enter_context(self.nc.sbuf_tensor(name, shape, dtype))

    def psum(self, name, shape, dtype=F32):
        return self.stack.enter_context(self.nc.psum_tensor(name, shape, dtype))

    def dma(self, qname, out, in_, sem=None, **kw):
        i = self.p.eng(qname).dma_start(out=out, in_=in_, **kw)
        if sem is not None:
            i.then_inc(sem.h, 16)
            sem.n += 16
        return i

    def wait(self, qname, sem, val=None):
        self.p.eng(qname).wait_ge(sem.h, sem.n if val is None else val)


def bcast_col(ap_col, nfree):
    return bass.AP(ap_col.tensor, ap_col.offset, [list(ap_col.ap[0]), [0, nfree]])


def chunks512(t0, n):
    out = []
    while n > 0:
        sz = min(512, n)
        out.append((t0, sz))
        t0 += sz
        n -= sz
    return out


def host_tile_lhsT(A):
    K, M = A.shape
    assert K % 128 == 0 and M % 128 == 0
    H = np.ascontiguousarray(
        A.reshape(K // 128, 128, M // 128, 128).transpose(2, 1, 0, 3)
        .reshape(M // 128, 128, (K // 128) * 128)
    ).astype(BF16NP)
    return H


def f32c(x):
    return np.ascontiguousarray(x, dtype=np.float32)


class Pool:

    def __init__(self, c: Ctx, ln=True):
        self.c = c
        self.ps_a = [c.psum("ps_a0", [128, 512], F32), c.psum("ps_a1", [128, 512], F32)]
        self.ps_b = [c.psum("ps_b0", [128, 512], F32), c.psum("ps_b1", [128, 512], F32)]
        self.ps_s1 = c.psum("ps_s1", [128, 512], F32)
        self.ps_s2 = c.psum("ps_s2", [128, 512], F32)
        self.ps_c = [c.psum("ps_c0", [128, 512], F32), c.psum("ps_c1", [128, 512], F32)]
        self.ones_col = c.sbuf("ones_col", [128, 1], F32)
        self.ones_col_bf = c.sbuf("ones_col_bf", [128, 1], BF16)
        self.ones128 = c.sbuf("ones128", [128, 128], F32)
        self.ones128_bf = c.sbuf("ones128_bf", [128, 128], BF16)
        self.ones_row = c.sbuf("ones_row", [1, 128], F32)
        self.eps1 = c.sbuf("eps1", [1, 1], F32)
        self.ln = ln
        if ln:
            self.ones_row_t = c.sbuf("ones_row_t", [1, 512], F32)
            self.stat = c.sbuf("stat", [1, 4 * 512], F32)
            self.sq = c.sbuf("sq", [128, 2 * 512], F32)
            self.ab = c.sbuf("ab", [128, 2 * 512], F32)
        self.s_cst = c.sem("cst")

    def emit_consts(self):
        c = self.c
        p = c.p
        p.gpsimd.memset(self.ones_col[:], 1.0)
        p.gpsimd.memset(self.ones_col_bf[:], 1.0)
        p.gpsimd.memset(self.ones128[:], 1.0)
        p.gpsimd.memset(self.ones128_bf[:], 1.0)
        p.gpsimd.memset(self.eps1[:], 1e-5)
        i = p.gpsimd.memset(self.ones_row[:], 1.0)
        if self.ln:
            i = p.gpsimd.memset(self.ones_row_t[:], 1.0)
        i.then_inc(self.s_cst.h, 1)
        self.s_cst.n += 1
        for e in ("scalar", "vector", "tensor"):
            c.wait(e, self.s_cst)


def emit_ln(c: Ctx, pool: Pool, name, x_sb, xT, xt0, xn_sb, xnT, xnt0, nT,
            s_x=None, eps=1e-5, out_dtype_note="bf16"):
    p = c.p
    s_sq = c.sem("LNsq")
    s_mm = c.sem("LNmm")
    s_st = c.sem("LNst")
    s_bc = c.sem("LNbc")
    s_ab = c.sem("LNab")
    s_ap = c.sem("LNap")

    if s_x is not None:
        for e in ("scalar", "tensor", "vector"):
            c.wait(e, s_x)

    for (t0, tsz) in chunks512(0, nT):
        mm_base = s_mm.n
        for dc in range(DC):
            slot = dc % 2
            xsl = x_sb[:, dc * xT + xt0 + t0: dc * xT + xt0 + t0 + tsz]
            if dc >= 2:
                p.scalar.wait_ge(s_mm.h, mm_base + dc - 1)
            p.scalar.activation(out=pool.sq[:, slot * 512: slot * 512 + tsz],
                                in_=xsl, func=AF.Square).then_inc(s_sq.h, 1)
            s_sq.n += 1
            p.tensor.wait_ge(s_sq.h, s_sq.n)
            p.tensor.matmul(pool.ps_s1[:, 0:tsz], pool.ones128[:, :], xsl,
                            start=(dc == 0), stop=(dc == DC - 1))
            p.tensor.matmul(pool.ps_s2[:, 0:tsz], pool.ones128[:, :],
                            pool.sq[:, slot * 512: slot * 512 + tsz],
                            start=(dc == 0), stop=(dc == DC - 1)).then_inc(s_mm.h, 1)
            s_mm.n += 1
        mean = pool.stat[0:1, 0:tsz]
        ex2 = pool.stat[0:1, 512:512 + tsz]
        a_v = pool.stat[0:1, 1024:1024 + tsz]
        b_v = pool.stat[0:1, 1536:1536 + tsz]
        p.vector.wait_ge(s_mm.h, s_mm.n)
        p.vector.tensor_scalar_mul(out=mean, in0=pool.ps_s1[0:1, 0:tsz], scalar1=1.0 / D)
        p.vector.tensor_scalar_mul(out=ex2, in0=pool.ps_s2[0:1, 0:tsz],
                                   scalar1=1.0 / D).then_inc(s_st.h, 1)
        s_st.n += 1
        p.vector.wait_ge(s_st.h, s_st.n)
        p.vector.tensor_tensor(out=a_v, in0=mean, in1=mean, op=ALU.mult).then_inc(s_st.h, 1)
        s_st.n += 1
        p.vector.wait_ge(s_st.h, s_st.n)
        p.vector.tensor_tensor(out=ex2, in0=ex2, in1=a_v, op=ALU.subtract).then_inc(s_st.h, 1)
        s_st.n += 1
        p.scalar.wait_ge(s_st.h, s_st.n)
        p.scalar.activation(out=a_v, in_=ex2, func=AF.Ln,
                            bias=pool.eps1[0:1, 0:1]).then_inc(s_st.h, 1)
        s_st.n += 1
        p.scalar.wait_ge(s_st.h, s_st.n)
        p.scalar.activation(out=a_v, in_=a_v, func=AF.Exp, scale=-0.5).then_inc(s_st.h, 1)
        s_st.n += 1
        p.vector.wait_ge(s_st.h, s_st.n)
        p.vector.tensor_tensor(out=b_v, in0=mean, in1=a_v, op=ALU.mult).then_inc(s_st.h, 1)
        s_st.n += 1
        p.vector.wait_ge(s_st.h, s_st.n)
        p.vector.tensor_scalar_mul(out=b_v, in0=b_v, scalar1=-1.0).then_inc(s_st.h, 1)
        s_st.n += 1
        p.tensor.wait_ge(s_st.h, s_st.n)
        p.tensor.matmul(pool.ps_c[0][:, 0:tsz], pool.ones_row[:, :], a_v, start=True, stop=True)
        p.tensor.matmul(pool.ps_c[1][:, 0:tsz], pool.ones_row[:, :], b_v, start=True, stop=True).then_inc(s_bc.h, 1)
        s_bc.n += 1
        p.vector.wait_ge(s_bc.h, s_bc.n)
        p.vector.tensor_copy(out=pool.ab[:, 0:tsz], in_=pool.ps_c[0][:, 0:tsz])
        p.vector.tensor_copy(out=pool.ab[:, 512:512 + tsz], in_=pool.ps_c[1][:, 0:tsz]).then_inc(s_ab.h, 1)
        s_ab.n += 1
        p.vector.wait_ge(s_ab.h, s_ab.n)
        for dc in range(DC):
            slot = dc % 2
            xsl = x_sb[:, dc * xT + xt0 + t0: dc * xT + xt0 + t0 + tsz]
            osl = xn_sb[:, dc * xnT + xnt0 + t0: dc * xnT + xnt0 + t0 + tsz]
            p.vector.tensor_tensor(out=pool.sq[:, slot * 512: slot * 512 + tsz],
                                   in0=xsl, in1=pool.ab[:, 0:tsz], op=ALU.mult)
            p.vector.tensor_tensor(out=osl, in0=pool.sq[:, slot * 512: slot * 512 + tsz],
                                   in1=pool.ab[:, 512:512 + tsz], op=ALU.add).then_inc(s_ap.h, 1)
            s_ap.n += 1
    return s_ap


def emit_ffn0_tb(c: Ctx, pool: Pool, name, x_sb, xT, xt0, xn_sb, xnT, xnt0, nT,
                 bufs, wdr, s_xn, half=0.5):
    assert nT <= 512
    p = c.p
    NCH = 64
    HCH = 16
    w0b, w1b, h_sb = bufs["w0b"], bufs["w1b"], bufs["h_sb"]
    bias0c, bias1r = bufs["bias0c"], bufs["bias1r"]

    s_w0 = c.sem(f"{name}_w0")
    s_w1 = c.sem(f"{name}_w1")
    s_hmm = c.sem(f"{name}_hmm")
    s_gel = c.sem(f"{name}_gel")
    s_q = c.sem(f"{name}_q")
    s_emm = c.sem(f"{name}_emm")
    s_ev = c.sem(f"{name}_ev")

    xn_gate = s_xn.n
    p.tensor.wait_ge(s_xn.h, xn_gate)

    for ncb in range(NCH):
        buf = ncb % 2
        if ncb >= 2:
            c.p.sync.wait_ge(s_hmm.h, s_hmm.base + ncb - 1)
        c.dma("sync", w0b[buf][:, :], wdr["w0h"][ncb], sem=s_w0)
        pb = ncb % 2
        p.tensor.wait_ge(s_w0.h, s_w0.base + (ncb + 1) * 16)
        if ncb >= 2:
            p.tensor.wait_ge(s_gel.h, s_gel.base + ncb - 1)
        for dc in range(DC):
            i = p.tensor.matmul(pool.ps_a[pb][:, 0:nT],
                                w0b[buf][:, dc * 128:(dc + 1) * 128],
                                xn_sb[:, dc * xnT + xnt0: dc * xnT + xnt0 + nT],
                                start=(dc == 0), stop=(dc == DC - 1))
        i.then_inc(s_hmm.h, 1)
        s_hmm.n += 1
        p.scalar.wait_ge(s_hmm.h, s_hmm.n)
        p.scalar.activation(out=h_sb[:, ncb * nT: ncb * nT + nT],
                            in_=pool.ps_a[pb][:, 0:nT], func=AF.Gelu,
                            bias=bias0c[:, ncb:ncb + 1], scale=1.0).then_inc(s_gel.h, 1)
        s_gel.n += 1

    p.tensor.wait_ge(s_gel.h, s_gel.n)
    for k in range(4):
        c.dma("sync", w1b[k % 4][:, :], wdr["w1q"][k], sem=s_w1)
    for hcb in range(HCH):
        pb = hcb % 2
        if hcb >= 2:
            p.tensor.wait_ge(s_ev.h, s_ev.base + hcb - 1)
        for qi in range(4):
            k = 4 * hcb + qi
            buf = k % 4
            p.tensor.wait_ge(s_w1.h, s_w1.base + (k + 1) * 16)
            for j in range(16):
                ncb = qi * 16 + j
                i = p.tensor.matmul(pool.ps_b[pb][:, 0:nT],
                                    w1b[buf][:, j * 128:(j + 1) * 128],
                                    h_sb[:, ncb * nT: ncb * nT + nT],
                                    start=(ncb == 0), stop=False)
            i.then_inc(s_q.h, 1)
            s_q.n += 1
            if k + 4 < 4 * HCH:
                c.p.sync.wait_ge(s_q.h, s_q.base + k + 1)
                c.dma("sync", w1b[buf][:, :], wdr["w1q"][k + 4], sem=s_w1)
        i = p.tensor.matmul(pool.ps_b[pb][:, 0:nT],
                            bias1r[0:1, hcb * 128:(hcb + 1) * 128],
                            pool.ones_row_t[0:1, 0:nT],
                            start=False, stop=True)
        i.then_inc(s_emm.h, 1)
        s_emm.n += 1
        p.vector.wait_ge(s_emm.h, s_emm.n)
        p.vector.scalar_tensor_tensor(
            out=x_sb[:, hcb * xT + xt0: hcb * xT + xt0 + nT],
            in0=pool.ps_b[pb][:, 0:nT], scalar=half,
            in1=x_sb[:, hcb * xT + xt0: hcb * xT + xt0 + nT],
            op0=ALU.mult, op1=ALU.add).then_inc(s_ev.h, 1)
        s_ev.n += 1
    return s_ev


def _blocks(T, TB):
    out = []
    t0 = 0
    while t0 < T:
        out.append((t0, min(TB, T - t0)))
        t0 += min(TB, T - t0)
    return out


def emit_ffn0(c: Ctx, pool: Pool, name, x_sb, T, wdr, bufs, s_x, TB=512, half=0.5):
    sevs = []
    for (i, (t0, tb)) in enumerate(_blocks(T, TB)):
        s_ap = emit_ln(c, pool, f"{name}_ln{i}", x_sb, T, t0, bufs["xn_sb"], TB, 0, tb, s_x=s_x)
        s_ev = emit_ffn0_tb(c, pool, f"{name}_tb{i}", x_sb, T, t0,
                            bufs["xn_sb"], TB, 0, tb, bufs, wdr, s_ap, half=half)
        sevs.append(s_ev)
        s_x = s_ev
    return sevs



def emit_ffn0_dram(c: Ctx, pool: Pool, name, x_src, y_dst, T, bufs, wdr,
                   s_pre=None, half=0.5, TB=512):
    p = c.p
    s_ld = c.sem(f"{name}_ld")
    s_st = c.sem(f"{name}_st")
    if s_pre is not None:
        c.wait("sync", s_pre)
    nst_prev = 0
    for (i, (t0, tsz)) in enumerate(_blocks(T, TB)):
        if i > 0:
            c.wait("sync", s_st)
        for dc in range(DC):
            c.dma("sync", bufs["x_tb"][:, dc * TB: dc * TB + tsz],
                  x_src(dc, t0, tsz), sem=s_ld)
        for e in ("scalar", "tensor", "vector"):
            c.wait(e, s_ld)
        s_ap = emit_ln(c, pool, f"{name}_ln{i}", bufs["x_tb"], TB, 0,
                       bufs["xn_sb"], TB, 0, tsz)
        s_ev = emit_ffn0_tb(c, pool, f"{name}_tb{i}", bufs["x_tb"], TB, 0,
                            bufs["xn_sb"], TB, 0, tsz, bufs, wdr, s_ap, half=half)
        c.wait("sync", s_ev)
        for dc in range(DC):
            c.dma("sync", y_dst(dc, t0, tsz),
                  bufs["x_tb"][:, dc * TB: dc * TB + tsz], sem=s_st)
    return s_st


def emit_proj(c: Ctx, pool: Pool, name, whd, MC, KC, src_sb, srcT, src_t0, nT,
              evac_fn, s_src, wpb, psum_pair=None, wgate=None, extra_mm=None,
              evac_eng="vector"):
    p = c.p
    s_w = c.sem("PJw")
    s_mm = c.sem("PJmm")
    s_ev = c.sem("PJev" + evac_eng[0])
    ps = psum_pair or pool.ps_a
    src_gate = s_src.n if s_src is not None else None
    TCH = chunks512(0, nT)
    ntch = len(TCH)
    g = 0
    for mc in range(MC):
        buf = mc % 2
        if mc >= 2:
            c.p.sync.wait_ge(s_mm.h, s_mm.base + (mc - 1) * ntch)
        elif wgate is not None:
            c.p.sync.wait_ge(wgate[0].h, wgate[1])
        c.dma("sync", wpb[buf][:, 0:KC * 128], whd[mc], sem=s_w)
        for (t0, tsz) in TCH:
            pb = g % 2
            p.tensor.wait_ge(s_w.h, s_w.base + (mc + 1) * 16)
            if g == 0 and src_gate is not None:
                p.tensor.wait_ge(s_src.h, src_gate)
            if g >= 2:
                p.tensor.wait_ge(s_ev.h, s_ev.base + g - 1)
            for kc in range(KC):
                i = p.tensor.matmul(ps[pb][:, 0:tsz],
                                    wpb[buf][:, kc * 128:(kc + 1) * 128],
                                    src_sb[:, kc * srcT + src_t0 + t0: kc * srcT + src_t0 + t0 + tsz],
                                    start=(kc == 0),
                                    stop=(kc == KC - 1 and extra_mm is None))
            if extra_mm is not None:
                i = extra_mm(mc, t0, tsz, ps[pb])
            i.then_inc(s_mm.h, 1)
            s_mm.n += 1
            evac_fn(mc, t0, tsz, ps[pb], s_mm, s_ev)
            g += 1
    return s_mm, s_ev


ATT0_SCALE = float(HD) ** -0.5


def emit_ident(c: Ctx, name):
    p = c.p
    ident = c.sbuf(f"{name}_ident", [128, 128], F32)
    s_msk = c.sem(f"{name}_identc")
    p.gpsimd.memset(ident[:], 0.0)
    p.gpsimd.affine_select(out=ident[:], in_=ident[:], compare_op=ALU.not_equal,
                           fill=1.0, base=0, pattern=[[-1, 128]], channel_multiplier=1).then_inc(s_msk.h, 1)
    s_msk.n += 1
    return dict(ident=ident, s_msk=s_msk)


def make_att0_masks_host(gq0, NQT):
    i = np.arange(128)[:, None]
    j = np.arange(128)[None, :]
    mown = np.where(j <= i, 0.0, NEG).astype(np.float32)
    mprev = np.zeros((128, NQT * 128), np.float32)
    mpers = np.zeros((128, NQT * 16), np.float32)
    for qt in range(NQT):
        gq = gq0 + qt
        if gq == 0:
            mprev[:, qt * 128:(qt + 1) * 128] = NEG
            mpers[:, qt * 16:(qt + 1) * 16] = NEG
        elif gq == 1:
            mprev[:, qt * 128:(qt + 1) * 128] = np.where((j > i) | (j < 16), 0.0, NEG)
            mpers[:, qt * 16:(qt + 1) * 16] = NEG
        else:
            mprev[:, qt * 128:(qt + 1) * 128] = np.where(j > i, 0.0, NEG)
    return mown, mprev, mpers


def emit_att0(c: Ctx, pool: Pool, name, x_src, y_src, y_dst, TLOCAL, wdr, bufs):
    p = c.p
    EXTW = 128 + TLOCAL + 128
    NQT = TLOCAL // 128
    u1, qy, k_sb, lat = bufs["u1"], bufs["qy_sb"], bufs["k_sb"], bufs["lat_sb"]
    masks = bufs["masks"]

    s_ld = c.sem(f"{name}_ld")

    s_ap = None
    for (i, (t0, tsz)) in enumerate(chunks512(0, EXTW)):
        for dc in range(DC):
            c.dma("sync", bufs["x_tb"][:, dc * 512: dc * 512 + tsz],
                  x_src(dc, t0, tsz), sem=s_ld)
        for e in ("scalar", "tensor", "vector"):
            c.wait(e, s_ld)
        s_ap = emit_ln(c, pool, f"{name}_ln{i}", bufs["x_tb"], 512, 0,
                       u1, EXTW, t0, tsz)
        c.wait("sync", s_ap)
    s_lnap = s_ap

    def mk_evac_bf16(dst, dstT, scale=None):
        def evac(mc, t0, tsz, ps_ap, s_mm, s_ev):
            p.scalar.wait_ge(s_mm.h, s_mm.n)
            if scale is None:
                i = p.scalar.copy(out=dst[:, mc * dstT + t0: mc * dstT + t0 + tsz],
                                  in_=ps_ap[:, 0:tsz])
            else:
                i = p.scalar.mul(out=dst[:, mc * dstT + t0: mc * dstT + t0 + tsz],
                                 in_=ps_ap[:, 0:tsz], mul=scale)
            i.then_inc(s_ev.h, 1)
            s_ev.n += 1
        return evac

    mm_lat, ev_lat = emit_proj(c, pool, f"{name}_lat", wdr["wkvh"], 4, DC,
                               u1, EXTW, 0, EXTW, mk_evac_bf16(lat, EXTW),
                               s_ap, bufs["wpb"], evac_eng="scalar")
    mm_q, ev_q = emit_proj(c, pool, f"{name}_q", wdr["wqh"], 16, DC,
                           u1, EXTW, 128, TLOCAL, mk_evac_bf16(qy, TLOCAL, ATT0_SCALE),
                           s_ap, bufs["wpb"], wgate=(mm_lat, mm_lat.n),
                           evac_eng="scalar")
    mm_k, ev_k = emit_proj(c, pool, f"{name}_k", wdr["wkh"], 16, 4,
                           lat, EXTW, 0, EXTW, mk_evac_bf16(k_sb, EXTW),
                           ev_lat, bufs["wpb"], wgate=(mm_q, mm_q.n),
                           evac_eng="scalar")

    s_vstg = c.sem(f"{name}_vstg")
    s_trn = c.sem(f"{name}_trn")
    s_vtm = c.sem(f"{name}_vtm")
    NTILE = EXTW // 128
    vstate = {"vg": 0, "trn_after": []}

    def evac_v(mc, t0, tsz, ps_ap, s_mm, s_ev):
        vg = vstate["vg"]
        stg = bufs["vstg"][vg % 2]
        p.scalar.wait_ge(s_mm.h, s_mm.n)
        if vg >= 2:
            p.scalar.wait_ge(s_trn.h, vstate["trn_after"][vg - 2])
        p.scalar.copy(out=stg[:, 0:tsz], in_=ps_ap[:, 0:tsz]).then_inc(s_vstg.h, 1)
        s_vstg.n += 1
        p.tensor.wait_ge(s_vstg.h, s_vstg.n)
        nsub = (tsz + 127) // 128
        for j in range(nsub):
            w = min(128, tsz - j * 128)
            tau = s_trn.n - s_trn.base
            bank = pool.ps_b[tau % 2]
            if tau >= 2:
                p.tensor.wait_ge(s_vtm.h, s_vtm.base + tau - 1)
            i = p.tensor.transpose(bank[0:w, 0:128],
                                   stg[:, j * 128: j * 128 + w], masks["ident"][:, :])
            i.then_inc(s_trn.h, 1)
            s_trn.n += 1
            p.vector.wait_ge(s_trn.h, s_trn.n)
            jt = (t0 + j * 128) // 128
            p.vector.tensor_copy(
                out=u1[0:w, mc * EXTW + jt * 128: mc * EXTW + jt * 128 + 128],
                in_=bank[0:w, 0:128]).then_inc(s_vtm.h, 1)
            s_vtm.n += 1
        p.vector.engine_nop().then_inc(s_ev.h, 1)
        s_ev.n += 1
        vstate["trn_after"].append(s_trn.n)
        vstate["vg"] = vg + 1

    mm_v, ev_v = emit_proj(c, pool, f"{name}_v", wdr["wvh"], 16, 4,
                           lat, EXTW, 0, EXTW, evac_v,
                           ev_lat, bufs["wpb"], wgate=(mm_k, mm_k.n))

    small = bufs["small"]
    bcol = bufs["bcol"]
    s_bnd = c.sem(f"{name}_bnd")
    s_bmm = c.sem(f"{name}_bmm")
    s_ch = c.sem(f"{name}_ch")

    def vedge():
        s_ch.n += 1
        return (s_ch.h, s_ch.n)

    def emit_norm_max(src, ncols, out_idx, gate):
        p.vector.memset(small[0:1, out_idx:out_idx + 1], 0.0)
        for (t0, tsz) in chunks512(0, ncols):
            p.scalar.wait_ge(gate.h, gate.n)
            if s_bmm.n > 0:
                p.scalar.wait_ge(s_bmm.h, s_bmm.n)
            p.scalar.activation(out=pool.sq[:, 0:tsz], in_=src[:, t0:t0 + tsz],
                                func=AF.Square).then_inc(s_bnd.h, 1)
            s_bnd.n += 1
            p.tensor.wait_ge(s_bnd.h, s_bnd.n)
            p.tensor.matmul(pool.ps_s2[:, 0:tsz], pool.ones128[:, :],
                            pool.sq[:, 0:tsz], start=True, stop=True).then_inc(s_bmm.h, 1)
            s_bmm.n += 1
            p.vector.wait_ge(s_bmm.h, s_bmm.n)
            h1, v1 = s_ch.h, None
            p.vector.reduce_max(out=small[0:1, 3:4], in_=pool.ps_s2[0:1, 0:tsz],
                                axis=mybir.AxisListType.X).then_inc(*vedge())
            p.vector.wait_ge(s_ch.h, s_ch.n)
            p.vector.tensor_tensor(out=small[0:1, out_idx:out_idx + 1],
                                   in0=small[0:1, out_idx:out_idx + 1],
                                   in1=small[0:1, 3:4], op=ALU.max).then_inc(s_bnd.h, 1)
            s_bnd.n += 1

    emit_norm_max(qy, 16 * TLOCAL, 0, ev_q)
    emit_norm_max(k_sb, 16 * EXTW, 1, ev_k)
    p.vector.tensor_tensor(out=small[0:1, 2:3], in0=small[0:1, 0:1],
                           in1=small[0:1, 1:2], op=ALU.mult).then_inc(s_bnd.h, 1)
    s_bnd.n += 1
    p.scalar.wait_ge(s_bnd.h, s_bnd.n)
    p.scalar.activation(out=small[0:1, 2:3], in_=small[0:1, 2:3],
                        func=AF.Sqrt).then_inc(*vedge())
    p.scalar.wait_ge(s_ch.h, s_ch.n)
    p.scalar.mul(out=small[0:1, 2:3], in_=small[0:1, 2:3], mul=-1.0).then_inc(s_bnd.h, 1)
    s_bnd.n += 1
    p.tensor.wait_ge(s_bnd.h, s_bnd.n)
    p.tensor.matmul(pool.ps_s2[:, 0:1], pool.ones_row[:, :], small[0:1, 2:3],
                    start=True, stop=True).then_inc(s_bmm.h, 1)
    s_bmm.n += 1
    p.vector.wait_ge(s_bmm.h, s_bmm.n)
    p.vector.tensor_copy(out=bcol[:, 0:1], in_=pool.ps_s2[:, 0:1]).then_inc(s_bnd.h, 1)
    s_bnd.n += 1

    if "dbg_preattn" in bufs:
        bufs["dbg_preattn"](c, s_bnd)

    p_sb, pn_sb, pt_sb = bufs["p_sb"], bufs["pn_sb"], bufs["pt_sb"]
    sm_sb = bufs["sm_sb"]
    den = bufs["den"]
    rcol = bufs["rcol"]
    s_smm = c.sem(f"{name}_smm")
    s_msk2 = c.sem(f"{name}_msk2")
    s_exp = c.sem(f"{name}_exp")
    s_pn = c.sem(f"{name}_pn")
    s_rc = c.sem(f"{name}_rc")
    s_ptr = c.sem(f"{name}_ptr")
    s_ptv = c.sem(f"{name}_ptv")
    s_pv = c.sem(f"{name}_pv")
    s_oev = c.sem(f"{name}_oev")

    for e in ("tensor", "vector", "scalar"):
        c.wait(e, s_bnd)
        c.wait(e, s_vtm)
    u = 0
    for h in range(HEADS):
        for qt in range(NQT):
            pb = u % 2
            qtile = qy[:, h * TLOCAL + qt * 128: h * TLOCAL + (qt + 1) * 128]
            prevoff = (128 + (qt - 1) * 128) if qt > 0 else 0
            blocks = [(0, 128, prevoff, bufs["mprev_sb"][:, qt * 128:(qt + 1) * 128], qt),
                      (128, 128, 128 + qt * 128, bufs["mown_sb"][:, 0:128], qt + 1),
                      (256, 16, 128 + TLOCAL, bufs["mpers_sb"][:, qt * 16:(qt + 1) * 16], NTILE - 1)]
            swidth = 272
            if u >= 2:
                p.tensor.wait_ge(s_exp.h, s_exp.base + u - 1)
            for (so, sw, ko, msk, jt) in blocks:
                i = p.tensor.matmul(pool.ps_a[pb][:, so:so + sw], qtile,
                                    k_sb[:, h * EXTW + ko: h * EXTW + ko + sw],
                                    start=True, stop=True)
            i.then_inc(s_smm.h, 1)
            s_smm.n += 1
            p.vector.wait_ge(s_smm.h, s_smm.n)
            if u >= 1:
                p.vector.wait_ge(s_exp.h, s_exp.n)
            for (so, sw, ko, msk, jt) in blocks:
                p.vector.tensor_tensor(out=sm_sb[:, so:so + sw],
                                       in0=pool.ps_a[pb][:, so:so + sw],
                                       in1=msk if msk.shape[1] == sw else msk[:, 0:sw],
                                       op=ALU.add).then_inc(s_msk2.h, 1)
                s_msk2.n += 1
            p.scalar.wait_ge(s_msk2.h, s_msk2.n)
            if u >= 1:
                p.scalar.wait_ge(s_pn.h, s_pn.n)
            p.scalar.activation(out=p_sb[:, 0:swidth], in_=sm_sb[:, 0:swidth],
                                func=AF.Exp, bias=bcol[:, 0:1],
                                accum_out=den[:, 0:1]).then_inc(s_exp.h, 1)
            s_exp.n += 1
            p.vector.wait_ge(s_exp.h, s_exp.n)
            if u >= 1:
                p.vector.wait_ge(s_ptr.h, s_ptr.n)
            p.vector.reciprocal(out=rcol[:, 0:1], in_=den[:, 0:1]).then_inc(s_rc.h, 1)
            s_rc.n += 1
            p.vector.wait_ge(s_rc.h, s_rc.n)
            p.vector.tensor_scalar(out=pn_sb[:, 0:swidth], in0=p_sb[:, 0:swidth],
                                   scalar1=rcol[:, 0:1], scalar2=None,
                                   op0=ALU.mult).then_inc(s_pn.h, 1)
            s_pn.n += 1
            p.tensor.wait_ge(s_pn.h, s_pn.n)
            for (bi, (so, sw, ko, msk, jt)) in enumerate(blocks):
                tau = s_ptr.n - s_ptr.base
                bank = (pool.ps_b[1], pool.ps_c[0])[tau % 2]
                if tau >= 2:
                    p.tensor.wait_ge(s_ptv.h, s_ptv.base + tau - 1)
                i = p.tensor.transpose(bank[0:sw, 0:128],
                                       pn_sb[:, so:so + sw], masks["ident"][:, :])
                i.then_inc(s_ptr.h, 1)
                s_ptr.n += 1
                p.vector.wait_ge(s_ptr.h, s_ptr.n)
                if bi == 0 and u >= 1:
                    p.vector.wait_ge(s_pv.h, s_pv.n)
                p.vector.tensor_copy(out=pt_sb[0:sw, bi * 128: bi * 128 + 128],
                                     in_=bank[0:sw, 0:128]).then_inc(s_ptv.h, 1)
                s_ptv.n += 1
            p.tensor.wait_ge(s_ptv.h, s_ptv.n)
            if u >= 1:
                p.tensor.wait_ge(s_oev.h, s_oev.base + u)
            for (bi, (so, sw, ko, msk, jt)) in enumerate(blocks):
                i = p.tensor.matmul(pool.ps_b[0][:, 0:128],
                                    u1[0:sw, h * EXTW + jt * 128: h * EXTW + jt * 128 + 128],
                                    pt_sb[0:sw, bi * 128: bi * 128 + 128],
                                    start=(bi == 0), stop=(bi == len(blocks) - 1))
            i.then_inc(s_pv.h, 1)
            s_pv.n += 1
            p.scalar.wait_ge(s_pv.h, s_pv.n)
            p.scalar.copy(out=qy[:, h * TLOCAL + qt * 128: h * TLOCAL + (qt + 1) * 128],
                          in_=pool.ps_b[0][:, 0:128]).then_inc(s_oev.h, 1)
            s_oev.n += 1
            u += 1

    s_xld = c.sem(f"{name}_xld")
    s_st = c.sem(f"{name}_st")

    def evac_wo(mc, t0, tsz, ps_ap, s_mm, s_ev):
        p.vector.wait_ge(s_mm.h, s_mm.n)
        p.vector.wait_ge(s_xld.h, s_xld.n)
        p.vector.scalar_tensor_tensor(
            out=bufs["x_tb"][:, mc * 512: mc * 512 + tsz],
            in0=ps_ap[:, 0:tsz], scalar=1.0,
            in1=bufs["x_tb"][:, mc * 512: mc * 512 + tsz],
            op0=ALU.mult, op1=ALU.add).then_inc(s_ev.h, 1)
        s_ev.n += 1

    wo_gate = (mm_v, mm_v.n)
    for (ti, (t0, tsz)) in enumerate(chunks512(0, TLOCAL)):
        if ti > 0:
            c.wait("gpsimd", s_st)
        else:
            c.wait("gpsimd", s_lnap)
        for dc in range(DC):
            c.dma("gpsimd", bufs["x_tb"][:, dc * 512: dc * 512 + tsz],
                  y_src(dc, t0, tsz), sem=s_xld)
        mm_wo, ev_wo = emit_proj(c, pool, f"{name}_wo{ti}", wdr["woh"], 16, DC,
                                 qy, TLOCAL, t0, tsz, evac_wo,
                                 s_oev, bufs["wpb"],
                                 wgate=wo_gate, psum_pair=pool.ps_c)
        wo_gate = (mm_wo, mm_wo.n)
        c.wait("gpsimd", ev_wo)
        for dc in range(DC):
            c.dma("gpsimd", y_dst(dc, t0, tsz),
                  bufs["x_tb"][:, dc * 512: dc * 512 + tsz], sem=s_st)
    return s_st



def build_ffn0_launch(TLOC=1024, TB=512, half=0.5):
    nc = bass.Bass("TRN2", target_bir_lowering=False, debug=False)
    x_in = nc.dram_tensor("x", [D, TLOC], F32, kind="ExternalInput").ap()
    w0h_d = nc.dram_tensor("w0h", [64, 128, DC * 128], BF16, kind="ExternalInput").ap()
    w1h_d = nc.dram_tensor("w1q", [64, 128, 16 * 128], BF16, kind="ExternalInput").ap()
    b0_d = nc.dram_tensor("b0c", [128, 64], F32, kind="ExternalInput").ap()
    b1_d = nc.dram_tensor("b1r", [1, D], F32, kind="ExternalInput").ap()
    y_out = nc.dram_tensor("y", [D, TLOC], F32, kind="ExternalOutput").ap()

    with ExitStack() as stack:
        c = Ctx(nc, stack)
        p = c.p
        pool = Pool(c)
        x_sb = c.sbuf("x_sb", [128, DC * TLOC], F32)
        xn_sb = c.sbuf("xn_sb", [128, DC * TB], BF16)
        w0b = [c.sbuf(f"w0b{i}", [128, DC * 128], BF16) for i in range(2)]
        w1b = [c.sbuf(f"w1b{i}", [128, 16 * 128], BF16) for i in range(4)]
        h_sb = c.sbuf("h_sb", [128, 64 * TB], BF16)
        bias0c = c.sbuf("bias0c", [128, 64], F32)
        bias1r = c.sbuf("bias1r", [1, D], F32)

        s_ld = c.sem("ld")
        s_out = c.sem("out")

        pool.emit_consts()
        c.dma("sync", x_sb[:], x_in.rearrange("(dc p) t -> p dc t", p=128), sem=s_ld)
        c.dma("sync", bias0c[:], b0_d[:, :], sem=s_ld)
        c.dma("sync", bias1r[:], b1_d[:, :], sem=s_ld)
        for e in ("scalar", "vector", "tensor"):
            c.wait(e, s_ld)

        wdr = dict(w0h=[w0h_d[i] for i in range(64)],
                   w1q=[w1h_d[i] for i in range(64)])
        bufs = dict(w0b=w0b, w1b=w1b, h_sb=h_sb, bias0c=bias0c, bias1r=bias1r,
                    xn_sb=xn_sb)

        sevs = emit_ffn0(c, pool, "f", x_sb, TLOC, wdr, bufs, s_ld, TB=TB, half=half)
        for s in sevs:
            c.wait("sync", s)
        c.dma("sync", y_out.rearrange("(dc p) t -> p dc t", p=128), x_sb[:], sem=s_out)
        c.wait("sync", s_out)

        with nc.Block() as block:
            c.p.replay(block)
    return nc


def prep_ffn0_weights(W0, b0, W1, b1):
    w0h = host_tile_lhsT(np.asarray(W0, dtype=np.float32).T)
    w1h = host_tile_lhsT(np.asarray(W1, dtype=np.float32).T)
    w1q = np.ascontiguousarray(
        w1h.reshape(16, 128, 4, 16 * 128).transpose(0, 2, 1, 3).reshape(64, 128, 16 * 128))
    b0c = f32c(np.asarray(b0).reshape(64, 128).T)
    b1r = f32c(np.asarray(b1).reshape(1, D))
    return w0h, w1q, b0c, b1r


def emit_rwkv(c: Ctx, pool: Pool, name, x_src, y_src, y_dst, TLOCAL, wdr, bufs):
    p = c.p
    EXTW = 264 + TLOCAL
    SCANW = 256 + TLOCAL
    ln_sb = bufs["ln_sb"]
    s_ld = c.sem(f"{name}_ld")
    s_ch = c.sem(f"{name}_ch")

    s_ap = None
    for (i, (t0, tsz)) in enumerate(chunks512(0, EXTW)):
        for dc in range(DC):
            srcs = x_src(dc, t0, tsz)
            if not isinstance(srcs, list):
                srcs = [(srcs, 0, tsz)]
            for (ap, ro, w) in srcs:
                c.dma("sync", bufs["x_tb"][:, dc * 512 + ro: dc * 512 + ro + w],
                      ap, sem=s_ld)
        for e in ("scalar", "tensor", "vector"):
            c.wait(e, s_ld)
        s_ap = emit_ln(c, pool, f"{name}_ln{i}", bufs["x_tb"], 512, 0,
                       ln_sb, EXTW, t0, tsz)
        c.wait("sync", s_ap)
    s_lnap = s_ap

    s_w2 = c.sem(f"{name}_w2")
    p.scalar.wait_ge(s_ld.h, s_ld.n)
    p.scalar.activation(out=bufs["e1_sb"][:], in_=bufs["td_sb"][:],
                        func=AF.Exp).then_inc(s_ch.h, 1)
    s_ch.n += 1
    p.scalar.wait_ge(s_ch.h, s_ch.n)
    p.scalar.activation(out=bufs["w_sb"][:], in_=bufs["e1_sb"][:],
                        func=AF.Exp, scale=-1.0).then_inc(s_w2.h, 1)
    s_w2.n += 1
    c.wait("vector", s_w2)

    s_wld = c.sem(f"{name}_wld")
    s_mm = c.sem(f"{name}_mm")
    s_ev = c.sem(f"{name}_ev")
    s_dve = c.sem(f"{name}_dve")
    s_xld = c.sem(f"{name}_xld")
    s_st = c.sem(f"{name}_st")
    ek, hmk, v_sb, a_sb = bufs["ek"], bufs["hmk"], bufs["v"], bufs["a"]
    num, den = bufs["num"], bufs["den"]
    rec, wkv, outb, r_sb = bufs["rec"], bufs["wkv"], bufs["outb"], bufs["r_sb"]
    hm_sb = bufs["hm_sb"]
    SCH = chunks512(0, SCANW)
    LCH = chunks512(0, TLOCAL)
    g = 0
    for dc in range(DC):
        if dc >= 1:
            c.p.sync.wait_ge(s_mm.h, s_mm.base + dc * 8)
        else:
            c.p.sync.wait_ge(s_lnap.h, s_lnap.n)
        c.dma("sync", bufs["wpb3"][0][:, :], wdr["wkh"][dc], sem=s_wld)
        c.dma("sync", bufs["wpb3"][1][:, :], wdr["wvh"][dc], sem=s_wld)
        c.dma("sync", bufs["wpb3"][2][:, :], wdr["wrh"][dc], sem=s_wld)
        specs = [(0, SCH, 7, ek, "exp"), (1, SCH, 7, v_sb, "copy"),
                 (2, LCH, 264, r_sb, "copy")]
        if dc == 0:
            p.tensor.wait_ge(s_lnap.h, s_lnap.n)
        for (wi, CH, coff, dst, mode) in specs:
            p.tensor.wait_ge(s_wld.h, s_wld.base + (dc * 3 + wi + 1) * 16)
            for (t0, tsz) in CH:
                pb = g % 2
                if g >= 2:
                    p.tensor.wait_ge(s_ev.h, s_ev.base + g - 1)
                for kc in range(DC):
                    i = p.tensor.matmul(pool.ps_a[pb][:, 0:tsz],
                                        bufs["wpb3"][wi][:, kc * 128:(kc + 1) * 128],
                                        ln_sb[:, kc * EXTW + coff + t0: kc * EXTW + coff + t0 + tsz],
                                        start=(kc == 0), stop=(kc == DC - 1))
                i.then_inc(s_mm.h, 1)
                s_mm.n += 1
                p.scalar.wait_ge(s_mm.h, s_mm.n)
                if dc >= 1:
                    p.scalar.wait_ge(s_dve.h, s_dve.base + 2 * dc - 1)
                if mode == "exp":
                    i = p.scalar.activation(out=dst[:, t0:t0 + tsz],
                                            in_=pool.ps_a[pb][:, 0:tsz], func=AF.Exp)
                else:
                    i = p.scalar.copy(out=dst[:, t0:t0 + tsz], in_=pool.ps_a[pb][:, 0:tsz])
                i.then_inc(s_ev.h, 1)
                s_ev.n += 1
                g += 1
        p.vector.wait_ge(s_ev.h, s_ev.n)
        if dc >= 1:
            pass
        w_bcast = bass.AP(bufs["w_sb"], dc, [[16, 128], [0, SCANW]])
        p.vector.tensor_tensor(out=hmk[:, 0:SCANW], in0=ek[:, 0:SCANW],
                               in1=hm_sb[:, 0:SCANW], op=ALU.mult)
        p.vector.tensor_tensor(out=a_sb[:, 0:SCANW], in0=hmk[:, 0:SCANW],
                               in1=v_sb[:, 0:SCANW], op=ALU.mult)
        p.vector.tensor_tensor_scan(out=num[:, 0:SCANW], data0=w_bcast,
                                    data1=a_sb[:, 0:SCANW], initial=0.0,
                                    op0=ALU.mult, op1=ALU.add)
        p.vector.tensor_tensor_scan(out=den[:, 0:SCANW], data0=w_bcast,
                                    data1=hmk[:, 0:SCANW], initial=0.0,
                                    op0=ALU.mult, op1=ALU.add)
        p.vector.tensor_scalar_add(out=den[:, 0:SCANW], in0=den[:, 0:SCANW],
                                   scalar1=1e-8).then_inc(s_ch.h, 1)
        s_ch.n += 1
        p.vector.wait_ge(s_ch.h, s_ch.n)
        p.vector.reciprocal(out=rec[:, 0:TLOCAL], in_=den[:, 256:SCANW]).then_inc(s_ch.h, 1)
        s_ch.n += 1
        p.vector.wait_ge(s_ch.h, s_ch.n)
        p.vector.tensor_tensor(out=wkv[:, 0:TLOCAL], in0=num[:, 256:SCANW],
                               in1=rec[:, 0:TLOCAL], op=ALU.mult)
        p.vector.tensor_tensor(out=outb[:, 0:TLOCAL], in0=r_sb[:, 0:TLOCAL],
                               in1=wkv[:, 0:TLOCAL], op=ALU.mult).then_inc(s_dve.h, 1)
        s_dve.n += 1
        if dc == 0:
            c.wait("gpsimd", s_lnap)
        for (t0, tsz) in LCH:
            c.dma("gpsimd", bufs["x_tb"][:, t0:t0 + tsz], y_src(dc, t0, tsz), sem=s_xld)
        p.vector.wait_ge(s_xld.h, s_xld.n)
        p.vector.tensor_tensor(out=outb[:, 0:TLOCAL], in0=outb[:, 0:TLOCAL],
                               in1=bufs["x_tb"][:, 0:TLOCAL], op=ALU.add).then_inc(s_dve.h, 1)
        s_dve.n += 1
        c.wait("gpsimd", s_dve)
        for (t0, tsz) in LCH:
            c.dma("gpsimd", y_dst(dc, t0, tsz), outb[:, t0:t0 + tsz], sem=s_st)
        c.p.vector.wait_ge(s_st.h, s_st.n)
    return s_st


L1_SCALE = float(HD) ** -0.5


def emit_l1_colscale(c: Ctx, pool: Pool, name, bufs, gate):
    p = c.p
    ws, wr = bufs["wstd_row"], bufs["wrec_row"]
    s_cs = c.sem(f"{name}_cs")
    s_ch = c.sem(f"{name}_ch")
    tmp = bufs["l1tmp"]
    colscale = bufs["colscale"]
    swi = bufs["swi"]
    swf = bufs["swf"]
    sg = bufs["sg"]
    s_in = bufs["s_wsin"]

    def edge(i):
        i.then_inc(s_ch.h, 1)
        s_ch.n += 1

    c.wait("vector", s_in)
    c.wait("scalar", s_in)
    edge(p.vector.tensor_scalar_max(out=tmp[0:1, 0:16], in0=ws[0:1, 0:16], scalar1=1e-8))
    edge(p.vector.tensor_scalar_max(out=tmp[0:1, 16:32], in0=wr[0:1, 0:16], scalar1=1e-8))
    p.scalar.wait_ge(s_ch.h, s_ch.n)
    edge(p.scalar.activation(out=tmp[0:1, 0:16], in_=tmp[0:1, 0:16], func=AF.Sqrt))
    p.scalar.wait_ge(s_ch.h, s_ch.n)
    edge(p.scalar.activation(out=tmp[0:1, 16:32], in_=tmp[0:1, 16:32], func=AF.Sqrt))
    edge(p.vector.tensor_scalar(out=tmp[0:1, 32:48], in0=wr[0:1, 0:16],
                                scalar1=0.1, scalar2=None, op0=ALU.is_gt))
    p.tensor.wait_ge(s_ch.h, s_ch.n)
    p.tensor.wait_ge(gate[0].h, gate[1])
    p.tensor.matmul(pool.ps_c[0][:, 0:16], pool.ones_row[:, :], tmp[0:1, 0:16],
                    start=True, stop=True)
    p.tensor.matmul(pool.ps_c[0][:, 16:32], pool.ones_row[:, :], tmp[0:1, 16:32],
                    start=True, stop=True)
    i = p.tensor.matmul(pool.ps_c[0][:, 32:48], pool.ones_row[:, :], tmp[0:1, 32:48],
                        start=True, stop=True)
    i.then_inc(s_cs.h, 1)
    s_cs.n += 1
    p.vector.wait_ge(s_cs.h, s_cs.n)
    edge(p.vector.tensor_copy(out=swf[:, 0:16], in_=pool.ps_c[0][:, 0:16]))
    edge(p.vector.tensor_copy(out=sg[:, 0:16], in_=pool.ps_c[0][:, 16:32]))
    edge(p.vector.tensor_copy(out=colscale[:, 32:48], in_=pool.ps_c[0][:, 32:48]))
    c.wait("gpsimd", s_ch)
    p.gpsimd.affine_select(out=swf[:, 0:16], in_=swf[:, 0:16], compare_op=ALU.is_ge,
                           fill=0.0, base=119, pattern=[[0, 16]], channel_multiplier=-1)
    p.gpsimd.affine_select(out=sg[:, 0:16], in_=sg[:, 0:16], compare_op=ALU.is_ge,
                           fill=0.0, base=-120, pattern=[[0, 16]], channel_multiplier=1)
    i = p.gpsimd.affine_select(out=colscale[:, 32:48], in_=colscale[:, 32:48],
                               compare_op=ALU.is_ge, fill=0.0, base=-120,
                               pattern=[[0, 16]], channel_multiplier=1)
    i.then_inc(s_cs.h, 1)
    s_cs.n += 1
    p.vector.wait_ge(s_cs.h, s_cs.n)
    p.vector.tensor_copy(out=swi[:, 0:16], in_=colscale[:, 32:48])
    edge(p.vector.tensor_tensor(out=colscale[:, 0:16], in0=swf[:, 0:16],
                                in1=sg[:, 0:16], op=ALU.add))
    p.vector.wait_ge(s_ch.h, s_ch.n)
    edge(p.vector.tensor_copy(out=colscale[:, 16:32], in_=colscale[:, 0:16]))
    p.vector.wait_ge(s_ch.h, s_ch.n)
    i = p.vector.memset(colscale[:, 32:48], 1.0)
    i.then_inc(s_cs.h, 1)
    s_cs.n += 1
    return s_cs


def emit_l1ffn_pub(c: Ctx, pool: Pool, name, x_src, TLOCAL, wdr, bufs, outs):
    p = c.p
    NT = TLOCAL // 128
    XTB = bufs.get("xtb_cols", 512)
    s_ld = c.sem(f"{name}_ld")

    s_ap = None
    t0 = 0
    i = 0
    while t0 < TLOCAL:
        tsz = min(XTB, TLOCAL - t0)
        for dc in range(DC):
            c.dma("sync", bufs["x_tb"][:, dc * XTB: dc * XTB + tsz],
                  x_src(dc, t0, tsz), sem=s_ld)
        for e in ("scalar", "tensor", "vector"):
            c.wait(e, s_ld)
        s_ap = emit_ln(c, pool, f"{name}_ln{i}", bufs["x_tb"], XTB, 0,
                       bufs["xn_sb"], TLOCAL, t0, tsz)
        c.wait("sync", s_ap)
        t0 += tsz
        i += 1

    s_cs = emit_l1_colscale(c, pool, name + "cs", bufs, (s_ap, s_ap.n))

    fused = bufs["fused_sb"]
    c.wait("vector", s_cs)

    def evac_fused(mc, t0, tsz, ps_ap, s_mm, s_ev):
        p.vector.wait_ge(s_mm.h, s_mm.n)
        p.vector.tensor_scalar(out=fused[:, mc * TLOCAL + t0: mc * TLOCAL + t0 + tsz],
                               in0=ps_ap[:, 0:tsz], scalar1=bufs["colscale"][:, mc:mc + 1],
                               scalar2=None, op0=ALU.mult).then_inc(s_ev.h, 1)
        s_ev.n += 1

    mm_f, ev_f = emit_proj(c, pool, f"{name}_fp", wdr["waph"], 48, DC,
                           bufs["xn_sb"], TLOCAL, 0, TLOCAL, evac_fused,
                           s_ap, bufs["wpb"])

    s_sel = c.sem(f"{name}_sel")
    s_pq = c.sem(f"{name}_pq")
    s_nrm = c.sem(f"{name}_nrm")
    s_nmm = c.sem(f"{name}_nmm")
    s_ch2 = c.sem(f"{name}_ch2")
    stage = bufs["stage"]
    nsq = bufs["nsq"]
    small = bufs["l1tmp"]
    c.wait("vector", ev_f)
    p.vector.memset(small[0:1, 48:49], 0.0)
    p.vector.memset(small[0:1, 49:50], 0.0)
    pi = 0
    for (sect, base_self, base_oth, out_dram, nidx) in (
            ("q", 0, 16, outs["qf"], 48), ("k", 16, 0, outs["kf"], 49)):
        for h in range(HEADS):
            st_buf = stage[pi % 2]
            if pi >= 2:
                c.wait("vector", s_pq, s_pq.base + 16 * (pi - 1))
            mycol = (base_self + h) * TLOCAL
            othcol = (base_oth + h) * TLOCAL
            swm = bcast_col(bufs["swi"][:, h:h + 1], TLOCAL)
            p.vector.select(out=st_buf[:, 0:TLOCAL], mask=swm,
                            on_true=fused[:, othcol:othcol + TLOCAL],
                            on_false=fused[:, mycol:mycol + TLOCAL]).then_inc(s_sel.h, 1)
            s_sel.n += 1
            for (t0, tsz) in chunks512(0, TLOCAL):
                p.scalar.wait_ge(s_sel.h, s_sel.n)
                if s_nmm.n > 0:
                    p.scalar.wait_ge(s_nmm.h, s_nmm.n)
                p.scalar.activation(out=nsq[:, 0:tsz], in_=st_buf[:, t0:t0 + tsz],
                                    func=AF.Square).then_inc(s_nrm.h, 1)
                s_nrm.n += 1
                p.tensor.wait_ge(s_nrm.h, s_nrm.n)
                p.tensor.matmul(pool.ps_s2[:, 0:tsz], pool.ones128[:, :],
                                nsq[:, 0:tsz], start=True, stop=True).then_inc(s_nmm.h, 1)
                s_nmm.n += 1
                p.vector.wait_ge(s_nmm.h, s_nmm.n)
                p.vector.reduce_max(out=small[0:1, 50:51], in_=pool.ps_s2[0:1, 0:tsz],
                                    axis=mybir.AxisListType.X).then_inc(s_ch2.h, 1)
                s_ch2.n += 1
                p.vector.wait_ge(s_ch2.h, s_ch2.n)
                p.vector.tensor_tensor(out=small[0:1, nidx:nidx + 1],
                                       in0=small[0:1, nidx:nidx + 1],
                                       in1=small[0:1, 50:51], op=ALU.max).then_inc(s_ch2.h, 1)
                s_ch2.n += 1
            c.wait("sync", s_sel)
            c.wait("sync", s_ch2)
            c.dma("sync", out_dram[:, h * TLOCAL:(h + 1) * TLOCAL],
                  st_buf[:, 0:TLOCAL], sem=s_pq)
            pi += 1
    c.wait("sync", s_ch2)
    c.dma("sync", outs["nrm"][0:1, 0:2], small[0:1, 48:50], sem=s_pq)

    s_v32 = c.sem(f"{name}_v32")
    s_tr = c.sem(f"{name}_tr")
    s_tv = c.sem(f"{name}_tv")
    s_pv = c.sem(f"{name}_pv")
    vt32 = bufs["vt32"]
    c.wait("vector", s_pq)
    for jt in range(NT):
        st_buf = stage[jt % 2]
        if jt >= 2:
            c.wait("vector", s_pv, s_pv.base + 16 * (jt - 1))
        for h in range(HEADS):
            tau = s_tr.n - s_tr.base
            p.scalar.wait_ge(ev_f.h, ev_f.n)
            if tau >= 2:
                p.scalar.wait_ge(s_tr.h, s_tr.base + tau - 1)
            p.scalar.copy(out=vt32[tau % 2][:, :],
                          in_=fused[:, (32 + h) * TLOCAL + jt * 128: (32 + h) * TLOCAL + jt * 128 + 128]).then_inc(s_v32.h, 1)
            s_v32.n += 1
            bank = pool.ps_b[tau % 2]
            p.tensor.wait_ge(s_v32.h, s_v32.n)
            if tau >= 2:
                p.tensor.wait_ge(s_tv.h, s_tv.base + tau - 1)
            i = p.tensor.transpose(bank[:, 0:128], vt32[tau % 2][:, :],
                                   bufs["masks"]["ident"][:, :])
            i.then_inc(s_tr.h, 1)
            s_tr.n += 1
            p.vector.wait_ge(s_tr.h, s_tr.n)
            p.vector.tensor_copy(out=st_buf[:, h * 128:(h + 1) * 128],
                                 in_=bank[:, 0:128]).then_inc(s_tv.h, 1)
            s_tv.n += 1
        c.wait("sync", s_tv)
        c.dma("sync", outs["vtm"][:, jt * 2048:(jt + 1) * 2048],
              st_buf[:, 0:2048], sem=s_pv)
    return s_pv, s_pq


def emit_l1ffn_diagmasks(c: Ctx, name, dm=None):
    p = c.p
    if dm is None:
        dm = c.sbuf(f"{name}_dm", [128, 4 * 512], BF16)
    s_m = c.sem(f"{name}_dmc")
    p.gpsimd.memset(dm[:], 0.0)
    for pa in range(4):
        i = p.gpsimd.affine_select(
            out=dm[:, pa * 512:(pa + 1) * 512], in_=dm[:, pa * 512:(pa + 1) * 512],
            compare_op=ALU.is_ge, fill=NEG, base=-128 * pa,
            pattern=[[1, 512]], channel_multiplier=-1)
    i.then_inc(s_m.h, 1)
    s_m.n += 1
    return dm, s_m


def emit_l1ffn_flash(c: Ctx, pool: Pool, name, TLOCAL, wdr, bufs, qs, nbands):
    p = c.p
    s_ch = c.sem(f"{name}_ch")

    def vedge(i):
        i.then_inc(s_ch.h, 1)
        s_ch.n += 1
        return s_ch.n

    qf = bufs["qf_sb"]

    def dacc_sl(idx, dacc=bufs["dacc"]):
        b = 32 * (idx // 6)
        sl = idx % 6
        return dacc[b:b + 1, sl * 512:(sl + 1) * 512]
    bufs["dacc_sl"] = dacc_sl
    kband, vband = bufs["kband"], bufs["vband"]
    pt_sb = bufs["pt_sb"]
    sm_sb = bufs["sm_sb"]
    oacc, dacc = bufs["oacc"], bufs["dacc"]
    dmasks, s_dm = bufs["dmasks"]
    bmat = bufs["bmat"]
    s_kv = c.sem(f"{name}_kv")
    s_smm = c.sem(f"{name}_smm")
    s_msk = c.sem(f"{name}_msk")
    s_exp = c.sem(f"{name}_exp")
    s_ptv = c.sem(f"{name}_ptv")
    s_oev = c.sem(f"{name}_oev")
    s_dev = c.sem(f"{name}_dev")

    if not bufs.get("bias_built", False):
        bufs["bias_built"] = True
        nrm = bufs["nrm_sb"]
        s_b = c.sem(f"{name}_b")
        c.wait("vector", bufs["s_nrm_in"])
        c.wait("scalar", bufs["s_nrm_in"])
        kstrided = bass.AP(nrm.tensor if isinstance(nrm, bass.AP) else nrm, 1,
                           [[20, 1], [2, 8]])
        vedge(p.vector.reduce_max(out=nrm[0:1, 18:19], in_=kstrided,
                                  axis=mybir.AxisListType.X))
        p.vector.wait_ge(s_ch.h, s_ch.n)
        vedge(p.vector.tensor_tensor(out=nrm[0:1, 10:11], in0=nrm[0:1, 18:19],
                                     in1=nrm[0:1, 16:17], op=ALU.mult))
        p.scalar.wait_ge(s_ch.h, s_ch.n)
        vedge(p.scalar.activation(out=nrm[0:1, 10:11], in_=nrm[0:1, 10:11], func=AF.Sqrt))
        p.scalar.wait_ge(s_ch.h, s_ch.n)
        vedge(p.scalar.mul(out=nrm[0:1, 10:11], in_=nrm[0:1, 10:11], mul=-L1_SCALE))
        p.tensor.wait_ge(s_ch.h, s_ch.n)
        i = p.tensor.matmul(pool.ps_s2[:, 0:1], pool.ones_row[:, :], nrm[0:1, 10:11],
                            start=True, stop=True)
        i.then_inc(s_b.h, 1)
        s_b.n += 1
        p.vector.wait_ge(s_b.h, s_b.n)
        c.wait("vector", bufs["s_tm_in"])
        vedge(p.vector.tensor_copy(out=bufs["bcol"][:, 0:1], in_=pool.ps_s2[:, 0:1]))
        p.vector.wait_ge(s_ch.h, s_ch.n)
        vedge(p.vector.tensor_scalar(out=bmat[:, 0:128], in0=bufs["tmask_sb"][:, 0:128],
                                     scalar1=bufs["bcol"][:, 0:1], scalar2=None,
                                     op0=ALU.add))
        bufs["bias_done"] = s_ch.n
    sbd = bufs["bias_done"]

    for e in ("tensor", "vector", "scalar"):
        c.wait(e, bufs["s_qf"])
        p.eng(e).wait_ge(s_ch.h, sbd)
        c.wait(e, s_dm)
    p.vector.memset(oacc[:, :], 0.0)
    i = p.vector.memset(dacc[:, :], 0.0)
    vedge(i)
    s_zero = s_ch.n
    for e in ("tensor", "scalar"):
        p.eng(e).wait_ge(s_ch.h, s_zero)

    seq = [("l", 0, qs)] + [("g", b // 2, b % 2) for b in range(2 * nbands)]
    nsb = len(seq)
    u = 0
    gi = 0
    bank_release = {}
    ptv_after_sb = []
    ptv_after_gi = []
    oev_after_gi = {}

    def sub_k(base_ap, half):
        return bass.AP(base_ap.tensor, base_ap.offset + half * 512,
                       [list(base_ap.ap[0]), [1024, 16], [1, 512]])

    for sb, (kind, blk, half) in enumerate(seq):
        buf = sb % 2
        if sb >= 2:
            c.p.sync.wait_ge(s_ptv.h, ptv_after_sb[sb - 2])
        if kind == "g":
            c.dma("sync", kband[buf][:, :], sub_k(wdr["kg"][blk], half), sem=s_kv)
            c.dma("sync", vband[buf][:, :],
                  bass.AP(wdr["vg"][blk].tensor, wdr["vg"][blk].offset + half * 8192,
                          [list(wdr["vg"][blk].ap[0]), [1, 8192]]), sem=s_kv)
        else:
            c.dma("sync", kband[buf][:, :], sub_k(wdr["kfl"], half), sem=s_kv)
            c.dma("sync", vband[buf][:, :],
                  bass.AP(wdr["vtl"].tensor, wdr["vtl"].offset + half * 8192,
                          [list(wdr["vtl"].ap[0]), [1, 8192]]), sem=s_kv)
        kv_gate = s_kv.n
        local = (kind == "l")
        for h in range(HEADS):
            obank = pool.ps_b[gi % 2]
            dbank = (pool.ps_s1, pool.ps_s2)[gi % 2]
            ptg = gi % 2
            if gi >= 2:
                p.tensor.wait_ge(s_oev.h, oev_after_gi[gi - 2])
            elif "psb_gate" in bufs:
                p.tensor.wait_ge(*bufs["psb_gate"])
            for jt in range(4):
                g = (sb - 1) * 4 + jt if not local else 0
                sbank = (pool.ps_a[0], pool.ps_a[1], pool.ps_c[0])[u % 3]
                ktile = kband[buf][:, h * 512 + jt * 128: h * 512 + jt * 128 + 128]
                vtile = vband[buf][:, jt * 2048 + h * 128: jt * 2048 + h * 128 + 128]
                p.tensor.wait_ge(s_kv.h, kv_gate)
                if u >= 3:
                    rs, rv = bank_release[u - 3]
                    p.tensor.wait_ge(rs, rv)
                i = p.tensor.matmul(sbank[:, 0:512], ktile,
                                    qf[:, h * 512: (h + 1) * 512],
                                    start=True, stop=True)
                i.then_inc(s_smm.h, 1)
                s_smm.n += 1
                ptslot = pt_sb[:, (ptg * 4 + jt) * 512: (ptg * 4 + jt) * 512 + 512]
                if local:
                    p.vector.wait_ge(s_smm.h, s_smm.n)
                    p.vector.wait_ge(s_exp.h, s_exp.n)
                    p.vector.tensor_tensor(out=sm_sb[:, 0:512],
                                           in0=sbank[:, 0:512],
                                           in1=dmasks[:, jt * 512:(jt + 1) * 512],
                                           op=ALU.add).then_inc(s_msk.h, 1)
                    s_msk.n += 1
                    bank_release[u] = (s_msk.h, s_msk.n)
                    exp_in = sm_sb[:, 0:512]
                    bias_ap = bufs["bcol"][:, 0:1]
                    p.scalar.wait_ge(s_msk.h, s_msk.n)
                else:
                    exp_in = sbank[:, 0:512]
                    bias_ap = bmat[:, qs * 64 + g: qs * 64 + g + 1]
                    p.scalar.wait_ge(s_smm.h, s_smm.n)
                if jt == 0 and gi >= 2:
                    p.scalar.wait_ge(s_ptv.h, ptv_after_gi[gi - 2])
                i = p.scalar.activation(out=ptslot, in_=exp_in,
                                        func=AF.Exp, bias=bias_ap, scale=L1_SCALE)
                i.then_inc(s_exp.h, 1)
                s_exp.n += 1
                if not local:
                    bank_release[u] = (s_exp.h, s_exp.n)
                p.tensor.wait_ge(s_exp.h, s_exp.n)
                p.tensor.matmul(dbank[:, 0:512], pool.ones128_bf[:, :],
                                ptslot, start=(jt == 0), stop=(jt == 3))
                i = p.tensor.matmul(obank[:, 0:512], vtile,
                                    ptslot, start=(jt == 0), stop=(jt == 3))
                i.then_inc(s_ptv.h, 1)
                s_ptv.n += 1
                u += 1
            ptv_after_gi.append(s_ptv.n)
            p.vector.wait_ge(s_ptv.h, s_ptv.n)
            p.vector.scalar_tensor_tensor(
                out=oacc[:, h * 512: (h + 1) * 512],
                in0=obank[:, 0:512], scalar=1.0,
                in1=oacc[:, h * 512: (h + 1) * 512],
                op0=ALU.mult, op1=ALU.add).then_inc(s_oev.h, 1)
            s_oev.n += 1
            p.vector.scalar_tensor_tensor(
                out=dacc_sl(h), in0=dbank[0:1, 0:512], scalar=1.0,
                in1=dacc_sl(h), op0=ALU.mult, op1=ALU.add).then_inc(s_oev.h, 1)
            s_oev.n += 1
            oev_after_gi[gi] = s_oev.n
            gi += 1
        ptv_after_sb.append(s_ptv.n)
    st = dict(s_oev=s_oev, s_dev=s_oev, s_ptv=s_ptv, s_ch=s_ch, vedge=vedge)
    return st


def emit_l1ffn_norm_wp(c: Ctx, pool: Pool, name, y_src, y_dst, TLOCAL, wdr,
                       bufs, st, qs, wo_gate=None, half=0.5):
    p = c.p
    s_oev, s_dev, s_ptv = st["s_oev"], st["s_dev"], st["s_ptv"]
    s_ch, vedge = st["s_ch"], st["vedge"]
    oacc = bufs["oacc"]
    dacc_sl = bufs["dacc_sl"]
    s_nb = c.sem(f"{name}_nb")
    s_ymul = c.sem(f"{name}_ymul")
    s_xld = c.sem(f"{name}_xld")
    s_st = c.sem(f"{name}_st")
    y_sb = bufs["y_sb"]
    rec_row = bufs["rec_row"]
    q0 = qs * 512
    for h in range(HEADS):
        p.vector.wait_ge(s_oev.h, s_oev.n)
        p.vector.wait_ge(s_dev.h, s_dev.n)
        if h > 0:
            p.vector.wait_ge(s_ymul.h, s_ymul.n)
        i = p.vector.reciprocal(out=rec_row[0:1, 0:512], in_=dacc_sl(h))
        vedge(i)
        p.tensor.wait_ge(s_ch.h, s_ch.n)
        if h > 0:
            p.tensor.wait_ge(s_ymul.h, s_ymul.n)
        i = p.tensor.matmul(pool.ps_c[1][:, 0:512], pool.ones_row[:, :],
                            rec_row[0:1, 0:512], start=True, stop=True)
        i.then_inc(s_nb.h, 1)
        s_nb.n += 1
        p.vector.wait_ge(s_nb.h, s_nb.n)
        p.vector.tensor_tensor(out=y_sb[:, h * 512: (h + 1) * 512],
                               in0=oacc[:, h * 512: (h + 1) * 512],
                               in1=pool.ps_c[1][:, 0:512],
                               op=ALU.mult).then_inc(s_ymul.h, 1)
        s_ymul.n += 1
    c.wait("gpsimd", s_ptv)
    for dc in range(DC):
        c.dma("gpsimd", bufs["x_tb"][:, dc * 512: dc * 512 + 512],
              y_src(dc, q0, 512), sem=s_xld)

    def evac_wp(mc, t0, tsz, ps_ap, s_mm, s_ev):
        p.vector.wait_ge(s_mm.h, s_mm.n)
        p.vector.wait_ge(s_xld.h, s_xld.n)
        p.vector.scalar_tensor_tensor(
            out=bufs["x_tb"][:, mc * 512: mc * 512 + tsz],
            in0=ps_ap[:, 0:tsz], scalar=half,
            in1=bufs["x_tb"][:, mc * 512: mc * 512 + tsz],
            op0=ALU.mult, op1=ALU.add).then_inc(s_ev.h, 1)
        s_ev.n += 1

    mm_wp, ev_wp = emit_proj(c, pool, f"{name}_wpj", wdr["wph"], 16, 16,
                             y_sb, 512, 0, 512, evac_wp,
                             s_ymul, bufs["wpb"], wgate=wo_gate,
                             psum_pair=pool.ps_b)
    bufs["psb_gate"] = (ev_wp.h, ev_wp.n)
    c.wait("gpsimd", ev_wp)
    for dc in range(DC):
        c.dma("gpsimd", y_dst(dc, q0, 512),
              bufs["x_tb"][:, dc * 512: dc * 512 + 512], sem=s_st)
    c.p.vector.wait_ge(mm_wp.h, mm_wp.n)
    return s_st, (mm_wp, mm_wp.n)


def build_att0_launch(TLOCAL=1024):
    EXTW = 128 + TLOCAL + 128
    NQT = TLOCAL // 128
    nc = bass.Bass("TRN2", target_bir_lowering=False, debug=False)
    xe_in = nc.dram_tensor("xe", [D, EXTW], F32, kind="ExternalInput").ap()
    mown_in = nc.dram_tensor("mown", [128, 128], F32, kind="ExternalInput").ap()
    mprev_in = nc.dram_tensor("mprev", [128, NQT * 128], F32, kind="ExternalInput").ap()
    mpers_in = nc.dram_tensor("mpers", [128, NQT * 16], F32, kind="ExternalInput").ap()
    wq_d = nc.dram_tensor("wqh", [16, 128, DC * 128], BF16, kind="ExternalInput").ap()
    wkv_d = nc.dram_tensor("wkvh", [4, 128, DC * 128], BF16, kind="ExternalInput").ap()
    wk_d = nc.dram_tensor("wkh", [16, 128, 4 * 128], BF16, kind="ExternalInput").ap()
    wv_d = nc.dram_tensor("wvh", [16, 128, 4 * 128], BF16, kind="ExternalInput").ap()
    wo_d = nc.dram_tensor("woh", [16, 128, DC * 128], BF16, kind="ExternalInput").ap()
    y_out = nc.dram_tensor("y", [D, TLOCAL], F32, kind="ExternalOutput").ap()

    with ExitStack() as stack:
        c = Ctx(nc, stack)
        pool = Pool(c)
        u1 = c.sbuf("u1", [128, 16 * EXTW], BF16)
        qy = c.sbuf("qy", [128, 16 * TLOCAL], BF16)
        k_sb = c.sbuf("k_sb", [128, 16 * EXTW], BF16)
        lat = c.sbuf("lat", [128, 4 * EXTW], BF16)
        x_tb = c.sbuf("x_tb", [128, 16 * 512], F32)
        vstg = [c.sbuf(f"vstg{i}", [128, 512], F32) for i in range(2)]
        p_sb = c.sbuf("p_sb", [128, 272], F32)
        sm_sb = c.sbuf("sm_sb", [128, 272], F32)
        pn_sb = c.sbuf("pn_sb", [128, 272], F32)
        pt_sb = c.sbuf("pt_sb", [128, 3 * 128], BF16)
        wpb = [c.sbuf(f"wpb{i}", [128, 16 * 128], BF16) for i in range(2)]
        bcol = c.sbuf("bcol", [128, 1], F32)
        den = c.sbuf("den", [128, 1], F32)
        rcol = c.sbuf("rcol", [128, 1], F32)
        small = c.sbuf("small", [1, 64], F32)
        mown_sb = c.sbuf("mown_sb", [128, 128], F32)
        mprev_sb = c.sbuf("mprev_sb", [128, NQT * 128], F32)
        mpers_sb = c.sbuf("mpers_sb", [128, NQT * 16], F32)

        pool.emit_consts()
        masks = emit_ident(c, "msk")
        s_mld = c.sem("mld")
        c.dma("sync", mown_sb[:], mown_in[:, :], sem=s_mld)
        c.dma("sync", mprev_sb[:], mprev_in[:, :], sem=s_mld)
        c.dma("sync", mpers_sb[:], mpers_in[:, :], sem=s_mld)
        for e in ("tensor", "vector", "scalar"):
            c.wait(e, masks["s_msk"])
            c.wait(e, s_mld)

        bufs = dict(u1=u1, qy_sb=qy, k_sb=k_sb, lat_sb=lat, x_tb=x_tb,
                    vstg=vstg, p_sb=p_sb, pn_sb=pn_sb, pt_sb=pt_sb, wpb=wpb,
                    masks=masks, bcol=bcol, den=den, rcol=rcol, small=small,
                    sm_sb=sm_sb,
                    mown_sb=mown_sb, mprev_sb=mprev_sb, mpers_sb=mpers_sb)
        wdr = dict(wqh=[wq_d[i] for i in range(16)],
                   wkvh=[wkv_d[i] for i in range(4)],
                   wkh=[wk_d[i] for i in range(16)],
                   wvh=[wv_d[i] for i in range(16)],
                   woh=[wo_d[i] for i in range(16)])

        def x_src(dc, t0, tsz):
            return xe_in[dc * 128:(dc + 1) * 128, t0:t0 + tsz]

        def y_src(dc, t0, tsz):
            return xe_in[dc * 128:(dc + 1) * 128, 128 + t0:128 + t0 + tsz]

        def y_dst(dc, t0, tsz):
            return y_out[dc * 128:(dc + 1) * 128, t0:t0 + tsz]

        s_st = emit_att0(c, pool, "a0", x_src, y_src, y_dst, TLOCAL, wdr, bufs)
        c.wait("gpsimd", s_st)
        with nc.Block() as block:
            c.p.replay(block)
    return nc


def prep_att0_weights(Wq, Wkv, Wk, Wv, Wo):
    return (host_tile_lhsT(np.asarray(Wq, np.float32)),
            host_tile_lhsT(np.asarray(Wkv, np.float32)),
            host_tile_lhsT(np.asarray(Wk, np.float32)),
            host_tile_lhsT(np.asarray(Wv, np.float32)),
            host_tile_lhsT(np.asarray(Wo, np.float32)))


def build_rwkv_launch(TLOCAL=1024):
    EXTW = 264 + TLOCAL
    SCANW = 256 + TLOCAL
    nc = bass.Bass("TRN2", target_bir_lowering=False, debug=False)
    xe_in = nc.dram_tensor("xe", [D, EXTW], F32, kind="ExternalInput").ap()
    wk_d = nc.dram_tensor("wkh", [16, 128, DC * 128], BF16, kind="ExternalInput").ap()
    wv_d = nc.dram_tensor("wvh", [16, 128, DC * 128], BF16, kind="ExternalInput").ap()
    wr_d = nc.dram_tensor("wrh", [16, 128, DC * 128], BF16, kind="ExternalInput").ap()
    td_in = nc.dram_tensor("td", [128, 16], F32, kind="ExternalInput").ap()
    hm_in = nc.dram_tensor("hm", [128, SCANW], F32, kind="ExternalInput").ap()
    y_out = nc.dram_tensor("y", [D, TLOCAL], F32, kind="ExternalOutput").ap()

    with ExitStack() as stack:
        c = Ctx(nc, stack)
        pool = Pool(c)
        ln_sb = c.sbuf("ln_sb", [128, DC * EXTW], BF16)
        x_tb = c.sbuf("x_tb", [128, DC * 512], F32)
        wpb3 = [c.sbuf(f"wpb3_{i}", [128, DC * 128], BF16) for i in range(3)]
        ek = c.sbuf("ek", [128, SCANW], F32)
        hmk = c.sbuf("hmk", [128, SCANW], F32)
        v_b = c.sbuf("v_b", [128, SCANW], F32)
        a_b = c.sbuf("a_b", [128, SCANW], F32)
        num = c.sbuf("num", [128, SCANW], F32)
        den = c.sbuf("den", [128, SCANW], F32)
        rec = c.sbuf("rec", [128, TLOCAL], F32)
        wkv = c.sbuf("wkv", [128, TLOCAL], F32)
        outb = c.sbuf("outb", [128, TLOCAL], F32)
        r_sb = c.sbuf("r_sb", [128, TLOCAL], F32)
        hm_sb = c.sbuf("hm_sb", [128, SCANW], F32)
        td_sb = c.sbuf("td_sb", [128, 16], F32)
        e1_sb = c.sbuf("e1_sb", [128, 16], F32)
        w_sb = c.sbuf("w_sb", [128, 16], F32)

        pool.emit_consts()
        s_ld0 = c.sem("ld0")
        c.dma("sync", td_sb[:], td_in[:, :], sem=s_ld0)
        c.dma("sync", hm_sb[:], hm_in[:, :], sem=s_ld0)
        for e in ("scalar", "vector"):
            c.wait(e, s_ld0)

        bufs = dict(ln_sb=ln_sb, x_tb=x_tb, wpb3=wpb3, ek=ek, hmk=hmk, v=v_b,
                    a=a_b, num=num, den=den, rec=rec, wkv=wkv, outb=outb,
                    r_sb=r_sb, hm_sb=hm_sb, td_sb=td_sb, e1_sb=e1_sb, w_sb=w_sb)
        wdr = dict(wkh=[wk_d[i] for i in range(16)],
                   wvh=[wv_d[i] for i in range(16)],
                   wrh=[wr_d[i] for i in range(16)])

        def x_src(dc, t0, tsz):
            return xe_in[dc * 128:(dc + 1) * 128, t0:t0 + tsz]

        def y_src(dc, t0, tsz):
            return xe_in[dc * 128:(dc + 1) * 128, 264 + t0:264 + t0 + tsz]

        def y_dst(dc, t0, tsz):
            return y_out[dc * 128:(dc + 1) * 128, t0:t0 + tsz]

        s_st = emit_rwkv(c, pool, "rw", x_src, y_src, y_dst, TLOCAL, wdr, bufs)
        c.wait("gpsimd", s_st)
        with nc.Block() as block:
            c.p.replay(block)
    return nc


def prep_rwkv_weights(Wk, Wv, Wr, td):
    return (host_tile_lhsT(np.asarray(Wk, np.float32).T),
            host_tile_lhsT(np.asarray(Wv, np.float32).T),
            host_tile_lhsT(np.asarray(Wr, np.float32).T),
            f32c(np.asarray(td).reshape(16, 128).T))


def build_l1pub_launch(TLOCAL=1024, dbg=False):
    NT = TLOCAL // 128
    nc = bass.Bass("TRN2", target_bir_lowering=False, debug=False)
    xe_in = nc.dram_tensor("xe", [D, TLOCAL], F32, kind="ExternalInput").ap()
    wa_d = nc.dram_tensor("waph", [48, 128, DC * 128], BF16, kind="ExternalInput").ap()
    ws_in = nc.dram_tensor("wstd", [1, 16], F32, kind="ExternalInput").ap()
    wr_in = nc.dram_tensor("wrec", [1, 16], F32, kind="ExternalInput").ap()
    qf_o = nc.dram_tensor("qf", [128, 16 * TLOCAL], BF16, kind="ExternalOutput").ap()
    kf_o = nc.dram_tensor("kf", [128, 16 * TLOCAL], BF16, kind="ExternalOutput").ap()
    vtm_o = nc.dram_tensor("vtm", [128, NT * 2048], BF16, kind="ExternalOutput").ap()
    nrm_o = nc.dram_tensor("nrm", [1, 2], F32, kind="ExternalOutput").ap()

    with ExitStack() as stack:
        c = Ctx(nc, stack)
        pool = Pool(c)
        fused = c.sbuf("fused", [128, 48 * TLOCAL], BF16)
        xn_sb = c.sbuf("xn_sb", [128, DC * TLOCAL], BF16)
        x_tb = c.sbuf("x_tb", [128, DC * 256], F32)
        stage = [c.sbuf(f"stage{i}", [128, 2048], BF16) for i in range(2)]
        wpb = [c.sbuf(f"wpb{i}", [128, DC * 128], BF16) for i in range(2)]
        vt32 = [c.sbuf(f"vt32_{i}", [128, 128], F32) for i in range(2)]
        colscale = c.sbuf("colscale", [128, 48], F32)
        swi = c.sbuf("swi", [128, 16], I32)
        swf = c.sbuf("swf", [128, 16], F32)
        sg = c.sbuf("sg", [128, 16], F32)
        l1tmp = c.sbuf("l1tmp", [1, 64], F32)
        wstd_row = c.sbuf("wstd_row", [1, 16], F32)
        wrec_row = c.sbuf("wrec_row", [1, 16], F32)

        pool.emit_consts()
        masks = emit_ident(c, "id")
        for e in ("tensor", "vector", "scalar"):
            c.wait(e, masks["s_msk"])
        s_ws = c.sem("wsld")
        c.dma("sync", wstd_row[:], ws_in[:, :], sem=s_ws)
        c.dma("sync", wrec_row[:], wr_in[:, :], sem=s_ws)

        bufs = dict(fused_sb=fused, xn_sb=xn_sb, x_tb=x_tb, stage=stage,
                    wpb=wpb, vt32=vt32, colscale=colscale, swi=swi, swf=swf,
                    sg=sg, l1tmp=l1tmp, wstd_row=wstd_row, wrec_row=wrec_row,
                    s_wsin=s_ws, masks=masks, nsq=pool.sq, xtb_cols=256)
        wdr = dict(waph=[wa_d[i] for i in range(48)])
        outs = dict(qf=qf_o, kf=kf_o, vtm=vtm_o, nrm=nrm_o)

        def x_src(dc, t0, tsz):
            return xe_in[dc * 128:(dc + 1) * 128, t0:t0 + tsz]

        s_pv, s_pq = emit_l1ffn_pub(c, pool, "pub", x_src, TLOCAL, wdr, bufs, outs)
        c.wait("sync", s_pv)
        c.wait("sync", s_pq)
        if dbg:
            d_cs = nc.dram_tensor("d_cs", [128, 48], F32, kind="ExternalOutput").ap()
            d_swi = nc.dram_tensor("d_swi", [128, 16], I32, kind="ExternalOutput").ap()
            s_dbg = c.sem("dbg")
            c.dma("sync", d_cs[:, :], colscale[:], sem=s_dbg)
            c.dma("sync", d_swi[:, :], swi[:], sem=s_dbg)
            c.wait("sync", s_dbg)
        with nc.Block() as block:
            c.p.replay(block)
    return nc


def build_l1flash_launch(TLOCAL=1024, slot_bands=(8, 8)):
    nc = bass.Bass("TRN2", target_bir_lowering=False, debug=False)
    xe_in = nc.dram_tensor("xe", [D, TLOCAL], F32, kind="ExternalInput").ap()
    qf_in = nc.dram_tensor("qf", [128, 16 * TLOCAL], BF16, kind="ExternalInput").ap()
    kg_in = nc.dram_tensor("kg", [8, 128, 16 * 1024], BF16, kind="ExternalInput").ap()
    vg_in = nc.dram_tensor("vg", [8, 128, 8 * 2048], BF16, kind="ExternalInput").ap()
    kfl_in = nc.dram_tensor("kfl", [128, 16 * 1024], BF16, kind="ExternalInput").ap()
    vtl_in = nc.dram_tensor("vtl", [128, 8 * 2048], BF16, kind="ExternalInput").ap()
    nrm_in = nc.dram_tensor("nrmv", [1, 20], F32, kind="ExternalInput").ap()
    tm_in = nc.dram_tensor("tmask", [128, 128], F32, kind="ExternalInput").ap()
    wp_d = nc.dram_tensor("wph", [16, 128, 16 * 128], BF16, kind="ExternalInput").ap()
    y_out = nc.dram_tensor("y", [D, TLOCAL], F32, kind="ExternalOutput").ap()

    with ExitStack() as stack:
        c = Ctx(nc, stack)
        pool = Pool(c, ln=False)
        qf_sb = c.sbuf("qf_sb", [128, 16 * 512], BF16)
        oacc = c.sbuf("oacc", [128, 16 * 512], F32)
        dacc = c.sbuf("dacc", [128, 6 * 512], F32)
        kband = [c.sbuf(f"kband{i}", [128, 16 * 512], BF16) for i in range(2)]
        vband = [c.sbuf(f"vband{i}", [128, 4 * 2048], BF16) for i in range(2)]
        pt_sb = c.sbuf("pt_sb", [128, 8 * 512], BF16)
        sm_sb = c.sbuf("sm_sb", [128, 512], F32)
        bmat = c.sbuf("bmat", [128, 130], F32)
        tmask_sb = c.sbuf("tmask_sb", [128, 128], F32)
        bcol = c.sbuf("bcolf", [128, 1], F32)
        nrm_sb = c.sbuf("nrm_sb", [1, 20], F32)
        rec_row = c.sbuf("rec_row", [1, 512], F32)
        x_tb = c.sbuf("x_tbf", [128, DC * 512], F32)
        wpb = [c.sbuf(f"wpbf{i}", [128, DC * 128], BF16) for i in range(2)]

        pool.emit_consts()
        dmasks, s_dm = emit_l1ffn_diagmasks(c, "dm")
        s_in = c.sem("flin")
        c.dma("sync", nrm_sb[:], nrm_in[:, :], sem=s_in)
        c.dma("sync", tmask_sb[:], tm_in[:, :], sem=s_in)

        bufs = dict(qf_sb=qf_sb, kband=kband, vband=vband, pt_sb=pt_sb,
                    bmat=bmat, tmask_sb=tmask_sb, bcol=bcol, nrm_sb=nrm_sb,
                    sm_sb=sm_sb, oacc=oacc, dacc=dacc, dmasks=(dmasks, s_dm),
                    y_sb=qf_sb, x_tb=x_tb, wpb=wpb, rec_row=rec_row,
                    s_nrm_in=s_in, s_tm_in=s_in)
        wdr = dict(kg=[kg_in[i][:, :] for i in range(8)],
                   vg=[vg_in[i][:, :] for i in range(8)],
                   kfl=kfl_in[:, :], vtl=vtl_in[:, :],
                   wph=[wp_d[i] for i in range(16)])

        def y_src(dc, t0, tsz):
            return xe_in[dc * 128:(dc + 1) * 128, t0:t0 + tsz]

        def y_dst(dc, t0, tsz):
            return y_out[dc * 128:(dc + 1) * 128, t0:t0 + tsz]

        wo_gate = None
        s_st = None
        for qs in range(TLOCAL // 512):
            s_qf = c.sem("qfld")
            if wo_gate is not None:
                c.p.sync.wait_ge(wo_gate[0].h, wo_gate[1])
            for h in range(HEADS):
                c.dma("sync", qf_sb[:, h * 512:(h + 1) * 512],
                      qf_in[:, h * TLOCAL + qs * 512: h * TLOCAL + qs * 512 + 512],
                      sem=s_qf)
            bufs["s_qf"] = s_qf
            st = emit_l1ffn_flash(c, pool, "fl", TLOCAL, wdr, bufs, qs,
                                  slot_bands[qs])
            s_st, wo_gate = emit_l1ffn_norm_wp(c, pool, "wp", y_src, y_dst,
                                               TLOCAL, wdr, bufs, st, qs, wo_gate)
        c.wait("gpsimd", s_st)
        with nc.Block() as block:
            c.p.replay(block)
    return nc


def prep_l1ffn_weights(Wa, Wp):
    Ds, R = 120, 8
    q_std_end = HEADS * Ds
    k_std_end = 2 * q_std_end
    v_end = k_std_end + D
    q_low_end = v_end + HEADS * R
    perm = []
    for h in range(HEADS):
        perm += list(range(h * Ds, (h + 1) * Ds))
        perm += list(range(v_end + h * R, v_end + (h + 1) * R))
    for h in range(HEADS):
        perm += list(range(q_std_end + h * Ds, q_std_end + (h + 1) * Ds))
        perm += list(range(q_low_end + h * R, q_low_end + (h + 1) * R))
    perm += list(range(k_std_end, v_end))
    WaP = np.ascontiguousarray(np.asarray(Wa, np.float32)[:, perm])
    return host_tile_lhsT(WaP), host_tile_lhsT(np.asarray(Wp, np.float32))


def make_l1_tmask_host(my_chunks):
    m = np.zeros((128, 128), np.float32)
    for (qs, qc) in enumerate(my_chunks):
        for g in range(64):
            if not (g < 4 * qc):
                m[:, qs * 64 + g] = NEG
    return m


def build_fused_launch(TLOC=1024):
    EXTA = 128 + TLOC + 128
    RH = 264
    EXTR = RH + TLOC
    SCANW = 256 + TLOC
    NT = TLOC // 128
    nc = bass.Bass("TRN2", target_bir_lowering=False, debug=False)

    xe0 = nc.dram_tensor("xe0", [D, EXTA], F32, kind="ExternalInput").ap()
    w0h_d = nc.dram_tensor("w0h", [64, 128, DC * 128], BF16, kind="ExternalInput").ap()
    w1q_d = nc.dram_tensor("w1q", [64, 128, 16 * 128], BF16, kind="ExternalInput").ap()
    b0_d = nc.dram_tensor("b0c", [128, 64], F32, kind="ExternalInput").ap()
    b1_d = nc.dram_tensor("b1r", [1, D], F32, kind="ExternalInput").ap()
    wq_d = nc.dram_tensor("wqh", [16, 128, DC * 128], BF16, kind="ExternalInput").ap()
    wkv_d = nc.dram_tensor("wkvh", [4, 128, DC * 128], BF16, kind="ExternalInput").ap()
    wk_d = nc.dram_tensor("wkh", [16, 128, 4 * 128], BF16, kind="ExternalInput").ap()
    wv_d = nc.dram_tensor("wvh", [16, 128, 4 * 128], BF16, kind="ExternalInput").ap()
    wo_d = nc.dram_tensor("woh", [16, 128, DC * 128], BF16, kind="ExternalInput").ap()
    mown_in = nc.dram_tensor("mown", [128, 128], F32, kind="ExternalInput").ap()
    mprev_in = nc.dram_tensor("mprev", [128, NT * 128], F32, kind="ExternalInput").ap()
    mpers_in = nc.dram_tensor("mpers", [128, NT * 16], F32, kind="ExternalInput").ap()
    rk_d = nc.dram_tensor("rkh", [16, 128, DC * 128], BF16, kind="ExternalInput").ap()
    rv_d = nc.dram_tensor("rvh", [16, 128, DC * 128], BF16, kind="ExternalInput").ap()
    rr_d = nc.dram_tensor("rrh", [16, 128, DC * 128], BF16, kind="ExternalInput").ap()
    td_in = nc.dram_tensor("td", [128, 16], F32, kind="ExternalInput").ap()
    hm_in = nc.dram_tensor("hm", [128, SCANW], F32, kind="ExternalInput").ap()
    wa_d = nc.dram_tensor("waph", [48, 128, DC * 128], BF16, kind="ExternalInput").ap()
    wp_d = nc.dram_tensor("wph", [16, 128, 16 * 128], BF16, kind="ExternalInput").ap()
    ws_in = nc.dram_tensor("wstd", [1, 16], F32, kind="ExternalInput").ap()
    wr_in = nc.dram_tensor("wrec", [1, 16], F32, kind="ExternalInput").ap()
    tm_in = nc.dram_tensor("tmask", [128, 128], F32, kind="ExternalInput").ap()
    hsel_in = nc.dram_tensor("hsel", [128, 8], F32, kind="ExternalInput").ap()
    y_out = nc.dram_tensor("y", [D, TLOC], F32, kind="ExternalOutput").ap()

    xA = nc.dram_tensor("xA", [D, EXTA], F32, kind="Internal").ap()
    xB = nc.dram_tensor("xB", [D, TLOC], F32, kind="Internal").ap()
    xC = nc.dram_tensor("xC", [D, TLOC], F32, kind="Internal").ap()
    xD = nc.dram_tensor("xD", [D, TLOC], F32, kind="Internal").ap()
    xE = nc.dram_tensor("xE", [D, TLOC], F32, kind="Internal").ap()
    xh = nc.dram_tensor("xh", [D, RH], F32, kind="Internal").ap()
    pb_in_t = nc.dram_tensor("pb_in", [128, 32768], BF16, kind="Internal")
    pb_out_t = nc.dram_tensor("pb_out", [1024, 32768], BF16, kind="Internal")
    nr_in_t = nc.dram_tensor("nr_in", [1, 2], F32, kind="Internal")
    nr_out_t = nc.dram_tensor("nr_out", [8, 2], F32, kind="Internal")
    rh_in_t = nc.dram_tensor("rh_in", [128, DC * RH], F32, kind="Internal")
    rh_out_t = nc.dram_tensor("rh_out", [1024, DC * RH], F32, kind="Internal")
    qfA = nc.dram_tensor("qfA", [128, 16 * TLOC], BF16, kind="Internal").ap()
    pb_in, pb_out = pb_in_t.ap(), pb_out_t.ap()
    nr_in, nr_out = nr_in_t.ap(), nr_out_t.ap()
    rh_in, rh_out = rh_in_t.ap(), rh_out_t.ap()

    def dsl(ap, dc, t0, tsz):
        return ap[dc * 128:(dc + 1) * 128, t0:t0 + tsz]

    with ExitStack() as stack:
        c = Ctx(nc, stack)
        p = c.p
        pool = Pool(c)
        pool.emit_consts()
        idents = emit_ident(c, "id")

        mown_sb = c.sbuf("mown_sb", [128, 128], F32)
        mprev_sb = c.sbuf("mprev_sb", [128, NT * 128], F32)
        mpers_sb = c.sbuf("mpers_sb", [128, NT * 16], F32)
        td_sb = c.sbuf("td_sb", [128, 16], F32)
        tmask_sb = c.sbuf("tmask_sb", [128, 128], F32)
        hsel_sb = c.sbuf("hsel_sb", [128, 8], F32)
        wstd_row = c.sbuf("wstd_row", [1, 16], F32)
        wrec_row = c.sbuf("wrec_row", [1, 16], F32)
        colscale = c.sbuf("colscale", [128, 48], F32)
        swi = c.sbuf("swi", [128, 16], I32)
        swf = c.sbuf("swf", [128, 16], F32)
        sgb = c.sbuf("sgb", [128, 16], F32)
        l1tmp = c.sbuf("l1tmp", [1, 64], F32)

        s_cld = c.sem("cld")
        for (dst, srcap) in ((mown_sb, mown_in),
                             (mprev_sb, mprev_in), (mpers_sb, mpers_in),
                             (td_sb, td_in), (tmask_sb, tm_in),
                             (hsel_sb, hsel_in), (wstd_row, ws_in),
                             (wrec_row, wr_in)):
            c.dma("sync", dst[:], srcap[:, :], sem=s_cld)

        def barrier(*sems):
            for e in ENGINES:
                for s in sems:
                    c.wait(e, s)

        barrier(pool.s_cst, idents["s_msk"], s_cld)

        wdr_ffn = dict(w0h=[w0h_d[i] for i in range(64)],
                       w1q=[w1q_d[i] for i in range(64)])
        with ExitStack() as st1:
            e1 = st1.enter_context
            bufs1 = dict(
                x_tb=e1(nc.sbuf_tensor("p1_x", [128, DC * 512], F32)),
                xn_sb=e1(nc.sbuf_tensor("p1_xn", [128, DC * 512], BF16)),
                w0b=[e1(nc.sbuf_tensor(f"p1_w0{i}", [128, DC * 128], BF16)) for i in range(2)],
                w1b=[e1(nc.sbuf_tensor(f"p1_w1{i}", [128, 16 * 128], BF16)) for i in range(4)],
                h_sb=e1(nc.sbuf_tensor("p1_h", [128, 64 * 512], BF16)),
                bias0c=e1(nc.sbuf_tensor("p1_b0", [128, 64], F32)),
                bias1r=e1(nc.sbuf_tensor("p1_b1", [1, D], F32)))
            s_bld = c.sem("fbld")
            c.dma("sync", bufs1["bias0c"][:], b0_d[:, :], sem=s_bld)
            c.dma("sync", bufs1["bias1r"][:], b1_d[:, :], sem=s_bld)
            for e in ("scalar", "tensor"):
                c.wait(e, s_bld)
            s_st1 = emit_ffn0_dram(c, pool, "f",
                                   lambda dc, t0, tsz: dsl(xe0, dc, t0, tsz),
                                   lambda dc, t0, tsz: dsl(xA, dc, t0, tsz),
                                   EXTA, bufs1, wdr_ffn)

        wdr_att = dict(wqh=[wq_d[i] for i in range(16)],
                       wkvh=[wkv_d[i] for i in range(4)],
                       wkh=[wk_d[i] for i in range(16)],
                       wvh=[wv_d[i] for i in range(16)],
                       woh=[wo_d[i] for i in range(16)])
        with ExitStack() as st2:
            e2 = st2.enter_context
            bufs2 = dict(
                u1=e2(nc.sbuf_tensor("p2_u1", [128, 16 * EXTA], BF16)),
                qy_sb=e2(nc.sbuf_tensor("p2_qy", [128, 16 * TLOC], BF16)),
                k_sb=e2(nc.sbuf_tensor("p2_k", [128, 16 * EXTA], BF16)),
                lat_sb=e2(nc.sbuf_tensor("p2_lat", [128, 4 * EXTA], BF16)),
                x_tb=e2(nc.sbuf_tensor("p2_x", [128, DC * 512], F32)),
                vstg=[e2(nc.sbuf_tensor(f"p2_vs{i}", [128, 512], F32)) for i in range(2)],
                p_sb=e2(nc.sbuf_tensor("p2_p", [128, 272], F32)),
                sm_sb=e2(nc.sbuf_tensor("p2_sm", [128, 272], F32)),
                pn_sb=e2(nc.sbuf_tensor("p2_pn", [128, 272], F32)),
                pt_sb=e2(nc.sbuf_tensor("p2_pt", [128, 3 * 128], BF16)),
                wpb=[e2(nc.sbuf_tensor(f"p2_wp{i}", [128, 16 * 128], BF16)) for i in range(2)],
                bcol=e2(nc.sbuf_tensor("p2_bc", [128, 1], F32)),
                den=e2(nc.sbuf_tensor("p2_dn", [128, 1], F32)),
                rcol=e2(nc.sbuf_tensor("p2_rc", [128, 1], F32)),
                small=e2(nc.sbuf_tensor("p2_sml", [1, 64], F32)),
                masks=idents, mown_sb=mown_sb, mprev_sb=mprev_sb,
                mpers_sb=mpers_sb)
            s_st2 = emit_att0(c, pool, "a",
                              lambda dc, t0, tsz: dsl(xA, dc, t0, tsz),
                              lambda dc, t0, tsz: dsl(xA, dc, 128 + t0, tsz),
                              lambda dc, t0, tsz: dsl(xB, dc, t0, tsz),
                              TLOC, wdr_att, bufs2)
        c.wait("sync", s_st2)

        with ExitStack() as st3:
            e3 = st3.enter_context
            bufs3 = dict(
                x_tb=e3(nc.sbuf_tensor("p3_x", [128, DC * 512], F32)),
                xn_sb=e3(nc.sbuf_tensor("p3_xn", [128, DC * 512], BF16)),
                w0b=[e3(nc.sbuf_tensor(f"p3_w0{i}", [128, DC * 128], BF16)) for i in range(2)],
                w1b=[e3(nc.sbuf_tensor(f"p3_w1{i}", [128, 16 * 128], BF16)) for i in range(4)],
                h_sb=e3(nc.sbuf_tensor("p3_h", [128, 64 * 512], BF16)),
                bias0c=e3(nc.sbuf_tensor("p3_b0", [128, 64], F32)),
                bias1r=e3(nc.sbuf_tensor("p3_b1", [1, D], F32)))
            s_bld = c.sem("fbld")
            c.dma("sync", bufs3["bias0c"][:], b0_d[:, :], sem=s_bld)
            c.dma("sync", bufs3["bias1r"][:], b1_d[:, :], sem=s_bld)
            for e in ("scalar", "tensor"):
                c.wait(e, s_bld)
            s_st3 = emit_ffn0_dram(c, pool, "f",
                                   lambda dc, t0, tsz: dsl(xB, dc, t0, tsz),
                                   lambda dc, t0, tsz: dsl(xC, dc, t0, tsz),
                                   TLOC, bufs3, wdr_ffn)

        wdr_wa = dict(waph=[wa_d[i] for i in range(48)])
        wdr_fl = dict(kg=[pb_out[r * 128:(r + 1) * 128, 0:16384] for r in range(8)],
                      vg=[pb_out[r * 128:(r + 1) * 128, 16384:32768] for r in range(8)],
                      kfl=pb_in[:, 0:16384], vtl=pb_in[:, 16384:32768],
                      wph=[wp_d[i] for i in range(16)])

        def l1_call(x_in, y_dst_ap, tag):
            with ExitStack() as sp:
                ep = sp.enter_context
                bufsp = dict(
                    fused_sb=ep(nc.sbuf_tensor(f"{tag}_fused", [128, 48 * TLOC], BF16)),
                    xn_sb=ep(nc.sbuf_tensor(f"{tag}_xn", [128, DC * TLOC], BF16)),
                    x_tb=ep(nc.sbuf_tensor(f"{tag}_x", [128, DC * 256], F32)),
                    stage=[ep(nc.sbuf_tensor(f"{tag}_st{i}", [128, 2048], BF16)) for i in range(2)],
                    wpb=[ep(nc.sbuf_tensor(f"{tag}_wp{i}", [128, DC * 128], BF16)) for i in range(2)],
                    vt32=[ep(nc.sbuf_tensor(f"{tag}_v32{i}", [128, 128], F32)) for i in range(2)],
                    colscale=colscale, swi=swi, swf=swf, sg=sgb, l1tmp=l1tmp,
                    wstd_row=wstd_row, wrec_row=wrec_row, s_wsin=s_cld,
                    masks=idents, nsq=pool.sq, xtb_cols=256)
                outs = dict(qf=qfA, kf=pb_in[:, 0:16384],
                            vtm=pb_in[:, 16384:32768], nrm=nr_in[:, :])
                s_pv, s_pq = emit_l1ffn_pub(c, pool, "pub",
                                            lambda dc, t0, tsz: dsl(x_in, dc, t0, tsz),
                                            TLOC, wdr_wa, bufsp, outs)
            c.wait("gpsimd", s_pv)
            c.wait("gpsimd", s_pq)
            s_cc = c.sem(f"cc{tag}g")
            s_cn = c.sem(f"cc{tag}n")
            p.gpsimd.collective_compute(
                "AllGather", ALU.bypass, replica_groups=[list(range(8))],
                ins=[pb_in_t.ap().opt()], outs=[pb_out_t.ap().opt()]).then_inc(s_cc.h)
            s_cc.n += 1
            p.gpsimd.collective_compute(
                "AllGather", ALU.bypass, replica_groups=[list(range(8))],
                ins=[nr_in_t.ap().opt()], outs=[nr_out_t.ap().opt()]).then_inc(s_cn.h)
            s_cn.n += 1
            with ExitStack() as sf:
                ef = sf.enter_context
                bufsf = dict(
                    qf_sb=ef(nc.sbuf_tensor(f"{tag}_qf", [128, 16 * 512], BF16)),
                    oacc=ef(nc.sbuf_tensor(f"{tag}_oa", [128, 16 * 512], F32)),
                    dacc=ef(nc.sbuf_tensor(f"{tag}_da", [128, 6 * 512], F32)),
                    kband=[ef(nc.sbuf_tensor(f"{tag}_kb{i}", [128, 16 * 512], BF16)) for i in range(2)],
                    vband=[ef(nc.sbuf_tensor(f"{tag}_vb{i}", [128, 4 * 2048], BF16)) for i in range(2)],
                    pt_sb=ef(nc.sbuf_tensor(f"{tag}_pt", [128, 8 * 512], BF16)),
                    sm_sb=ef(nc.sbuf_tensor(f"{tag}_sm", [128, 512], F32)),
                    bmat=ef(nc.sbuf_tensor(f"{tag}_bm", [128, 130], F32)),
                    bcol=ef(nc.sbuf_tensor(f"{tag}_bc", [128, 1], F32)),
                    nrm_sb=ef(nc.sbuf_tensor(f"{tag}_nr", [1, 20], F32)),
                    rec_row=ef(nc.sbuf_tensor(f"{tag}_rr", [1, 512], F32)),
                    x_tb=ef(nc.sbuf_tensor(f"{tag}_xt", [128, DC * 512], F32)),
                    wpb=[ef(nc.sbuf_tensor(f"{tag}_fw{i}", [128, DC * 128], BF16)) for i in range(2)],
                    tmask_sb=tmask_sb,
                    s_tm_in=s_cld)
                dm_t = ef(nc.sbuf_tensor(f"{tag}_dm", [128, 4 * 512], BF16))
                dmasks, s_dm = emit_l1ffn_diagmasks(c, "dm", dm_t)
                bufsf["dmasks"] = (dmasks, s_dm)
                bufsf["y_sb"] = bufsf["qf_sb"]
                s_nin = c.sem("flnr")
                c.wait("sync", s_cn)
                c.dma("sync", bufsf["nrm_sb"][0:1, 0:16],
                      nr_out.rearrange("a b -> (a b)"), sem=s_nin)
                c.dma("sync", bufsf["nrm_sb"][0:1, 16:18], nr_in[0:1, 0:2], sem=s_nin)
                bufsf["s_nrm_in"] = s_nin
                wo_gate = None
                for qs in range(TLOC // 512):
                    s_qf = c.sem("flqf")
                    if wo_gate is not None:
                        c.p.sync.wait_ge(wo_gate[0].h, wo_gate[1])
                    for h in range(HEADS):
                        c.dma("sync", bufsf["qf_sb"][:, h * 512:(h + 1) * 512],
                              qfA[:, h * TLOC + qs * 512: h * TLOC + qs * 512 + 512],
                              sem=s_qf)
                    bufsf["s_qf"] = s_qf
                    if qs == 0:
                        c.wait("sync", s_cc)
                    stt = emit_l1ffn_flash(c, pool, "fl", TLOC, wdr_fl, bufsf,
                                           qs, 8)
                    s_stf, wo_gate = emit_l1ffn_norm_wp(
                        c, pool, "wp",
                        lambda dc, t0, tsz: dsl(x_in, dc, t0, tsz),
                        lambda dc, t0, tsz: dsl(y_dst_ap, dc, t0, tsz),
                        TLOC, wdr_fl, bufsf, stt, qs, wo_gate)
            return s_stf

        s_stfA = l1_call(xC, xD, "A")

        s_rp = c.sem("rhpub")
        for dc in range(DC):
            c.dma("gpsimd", rh_in[:, dc * RH:(dc + 1) * RH],
                  dsl(xD, dc, TLOC - RH, RH), sem=s_rp)
        c.wait("gpsimd", s_rp)
        s_cr = c.sem("ccr")
        p.gpsimd.collective_compute(
            "AllGather", ALU.bypass, replica_groups=[list(range(8))],
            ins=[rh_in_t.ap().opt()], outs=[rh_out_t.ap().opt()]).then_inc(s_cr.h)
        s_cr.n += 1
        c.wait("sync", s_cr)
        with ExitStack() as sh:
            eh = sh.enter_context
            hacc = eh(nc.sbuf_tensor("hacc", [128, DC * RH], F32))
            hstg = eh(nc.sbuf_tensor("hstg", [128, DC * RH], F32))
            s_hl = c.sem("rhld")
            s_ha = c.sem("rhacc")
            p.vector.memset(hacc[:], 0.0).then_inc(s_ha.h, 1)
            s_ha.n += 1
            for r in range(8):
                c.wait("sync", s_ha)
                c.dma("sync", hstg[:], rh_out[r * 128:(r + 1) * 128, :], sem=s_hl)
                c.wait("vector", s_hl)
                p.vector.scalar_tensor_tensor(
                    out=hacc[:], in0=hstg[:], scalar=hsel_sb[:, r:r + 1],
                    in1=hacc[:], op0=ALU.mult, op1=ALU.add).then_inc(s_ha.h, 1)
                s_ha.n += 1
            c.wait("sync", s_ha)
            s_hw = c.sem("rhwr")
            for dc in range(DC):
                c.dma("sync", xh[dc * 128:(dc + 1) * 128, 0:RH],
                      hacc[:, dc * RH:(dc + 1) * RH], sem=s_hw)
            c.wait("sync", s_stfA)

        wdr_r = dict(wkh=[rk_d[i] for i in range(16)],
                     wvh=[rv_d[i] for i in range(16)],
                     wrh=[rr_d[i] for i in range(16)])

        def x_src_rwkv(dc, t0, tsz):
            out = []
            if t0 < RH:
                w = min(RH - t0, tsz)
                out.append((dsl(xh, dc, t0, w), 0, w))
            if t0 + tsz > RH:
                lo = max(t0, RH)
                out.append((dsl(xD, dc, lo - RH, t0 + tsz - lo), lo - t0,
                            t0 + tsz - lo))
            return out

        with ExitStack() as s5:
            e5 = s5.enter_context
            bufs5 = dict(
                ln_sb=e5(nc.sbuf_tensor("p5_ln", [128, DC * EXTR], BF16)),
                x_tb=e5(nc.sbuf_tensor("p5_x", [128, DC * 512], F32)),
                wpb3=[e5(nc.sbuf_tensor(f"p5_w{i}", [128, DC * 128], BF16)) for i in range(3)],
                ek=e5(nc.sbuf_tensor("p5_ek", [128, SCANW], F32)),
                hmk=e5(nc.sbuf_tensor("p5_hk", [128, SCANW], F32)),
                v=e5(nc.sbuf_tensor("p5_v", [128, SCANW], F32)),
                a=e5(nc.sbuf_tensor("p5_a", [128, SCANW], F32)),
                num=e5(nc.sbuf_tensor("p5_n", [128, SCANW], F32)),
                den=e5(nc.sbuf_tensor("p5_d", [128, SCANW], F32)),
                rec=e5(nc.sbuf_tensor("p5_r", [128, TLOC], F32)),
                wkv=e5(nc.sbuf_tensor("p5_wv", [128, TLOC], F32)),
                outb=e5(nc.sbuf_tensor("p5_o", [128, TLOC], F32)),
                r_sb=e5(nc.sbuf_tensor("p5_rs", [128, TLOC], F32)),
                hm_sb=e5(nc.sbuf_tensor("p5_hm", [128, SCANW], F32)),
                td_sb=td_sb,
                e1_sb=e5(nc.sbuf_tensor("p5_e1", [128, 16], F32)),
                w_sb=e5(nc.sbuf_tensor("p5_wd", [128, 16], F32)))
            s_hm = c.sem("hmld")
            c.dma("sync", bufs5["hm_sb"][:], hm_in[:, :], sem=s_hm)
            c.wait("scalar", s_cld)
            c.wait("vector", s_cld)
            c.wait("vector", s_hm)
            s_st5 = emit_rwkv(c, pool, "rw", x_src_rwkv,
                              lambda dc, t0, tsz: dsl(xD, dc, t0, tsz),
                              lambda dc, t0, tsz: dsl(xE, dc, t0, tsz),
                              TLOC, wdr_r, bufs5)
        c.wait("sync", s_st5)

        s_stfB = l1_call(xE, y_out, "B")
        barrier(s_stfB)

        with nc.Block() as block:
            c.p.replay(block)
    return nc


_NC_CACHE = {}
_USE_FUSED = True


def _get_nc(key, builder, *a, **kw):
    if key not in _NC_CACHE:
        _NC_CACHE[key] = builder(*a, **kw)
    return _NC_CACHE[key]


def _run(nc, in_maps):
    return run_bass_kernel_spmd(nc, in_maps, core_ids=list(range(NCORES)))


def kernel(L0_attn_x, L0_attn_Wq, L0_attn_Wkv, L0_attn_Wk, L0_attn_Wv,
           L0_attn_Wo, L0_ffn_W0, L0_ffn_bias0, L0_ffn_W1, L0_ffn_bias1,
           L1_attn_Wk, L1_attn_Wv, L1_attn_Wr, L1_attn_time_decay,
           L1_ffn_Wa, L1_ffn_Wp, L1_ffn_w_std, L1_ffn_w_rec):
    TLOC = S // NCORES
    x0 = np.asarray(L0_attn_x, dtype=np.float32).reshape(S, D)
    xT = np.ascontiguousarray(x0.T)

    w0h, w1q, b0c, b1r = prep_ffn0_weights(L0_ffn_W0, L0_ffn_bias0,
                                           L0_ffn_W1, L0_ffn_bias1)
    wqh, wkvh, wkh, wvh, woh = prep_att0_weights(L0_attn_Wq, L0_attn_Wkv,
                                                 L0_attn_Wk, L0_attn_Wv, L0_attn_Wo)
    rkh, rvh, rrh, tdc = prep_rwkv_weights(L1_attn_Wk, L1_attn_Wv, L1_attn_Wr,
                                           L1_attn_time_decay)
    waph, wph = prep_l1ffn_weights(L1_ffn_Wa, L1_ffn_Wp)
    wstd = f32c(np.asarray(L1_ffn_w_std).reshape(1, 16))
    wrec = f32c(np.asarray(L1_ffn_w_rec).reshape(1, 16))

    def split(xt):
        return [np.ascontiguousarray(xt[:, c * TLOC:(c + 1) * TLOC])
                for c in range(NCORES)]

    def cat(res, key="y"):
        return np.concatenate([res.results[c][key] for c in range(NCORES)], axis=1)

    def run_ffn0(xt):
        nc = _get_nc("ffn0", build_ffn0_launch, TLOC)
        maps = [{"x": s, "w0h": w0h, "w1q": w1q, "b0c": b0c, "b1r": b1r}
                for s in split(xt)]
        return cat(_run(nc, maps))

    def run_att0(xt):
        nc = _get_nc("att0", build_att0_launch, TLOC)
        maps = []
        for c in range(NCORES):
            t0 = c * TLOC
            xe = np.zeros((D, 128 + TLOC + 128), np.float32)
            if c > 0:
                xe[:, 0:128] = xt[:, t0 - 128:t0]
            xe[:, 128:128 + TLOC] = xt[:, t0:t0 + TLOC]
            xe[:, 128 + TLOC:128 + TLOC + 16] = xt[:, 0:16]
            mown, mprev, mpers = make_att0_masks_host(8 * c, TLOC // 128)
            maps.append({"xe": xe, "wqh": wqh, "wkvh": wkvh, "wkh": wkh,
                         "wvh": wvh, "woh": woh, "mown": mown, "mprev": mprev,
                         "mpers": mpers})
        return cat(_run(nc, maps))

    def run_rwkv(xt):
        nc = _get_nc("rwkv", build_rwkv_launch, TLOC)
        SCANW = 256 + TLOC
        maps = []
        for c in range(NCORES):
            t0 = c * TLOC
            xe = np.zeros((D, 264 + TLOC), np.float32)
            if c > 0:
                xe[:, 0:264] = xt[:, t0 - 264:t0]
            xe[:, 264:] = xt[:, t0:t0 + TLOC]
            hm = np.ones((128, SCANW), np.float32)
            if c == 0:
                hm[:, 0:256] = 0.0
            maps.append({"xe": xe, "wkh": rkh, "wvh": rvh, "wrh": rrh,
                         "td": tdc, "hm": hm})
        return cat(_run(nc, maps))

    def run_l1ffn(xt):
        nc_a = _get_nc("l1pub", build_l1pub_launch, TLOC)
        maps = [{"xe": s, "waph": waph, "wstd": wstd, "wrec": wrec}
                for s in split(xt)]
        ra = _run(nc_a, maps)
        kg = np.ascontiguousarray(np.stack([ra.results[c]["kf"] for c in range(NCORES)]))
        vg = np.ascontiguousarray(np.stack([ra.results[c]["vtm"] for c in range(NCORES)]))
        nc_b = _get_nc("l1flash", build_l1flash_launch, TLOC, (8, 8))
        maps_b = []
        xs = split(xt)
        for c in range(NCORES):
            nrmv = np.zeros((1, 20), np.float32)
            for j in range(NCORES):
                nrmv[0, 2 * j] = ra.results[j]["nrm"][0, 0]
                nrmv[0, 2 * j + 1] = ra.results[j]["nrm"][0, 1]
            nrmv[0, 16] = ra.results[c]["nrm"][0, 0]
            nrmv[0, 17] = ra.results[c]["nrm"][0, 1]
            tm = make_l1_tmask_host((2 * c, 2 * c + 1))
            maps_b.append({"xe": xs[c], "qf": ra.results[c]["qf"], "kg": kg,
                           "vg": vg, "kfl": ra.results[c]["kf"],
                           "vtl": ra.results[c]["vtm"], "nrmv": nrmv,
                           "tmask": tm, "wph": wph})
        return cat(_run(nc_b, maps_b))

    if _USE_FUSED:
        EXTA = 128 + TLOC + 128
        SCANW = 256 + TLOC
        nc = _get_nc("fused", build_fused_launch, TLOC)
        maps = []
        for c in range(NCORES):
            t0 = c * TLOC
            xe0 = np.zeros((D, EXTA), np.float32)
            if c > 0:
                xe0[:, 0:128] = xT[:, t0 - 128:t0]
            xe0[:, 128:128 + TLOC] = xT[:, t0:t0 + TLOC]
            xe0[:, 128 + TLOC:128 + TLOC + 16] = xT[:, 0:16]
            mown, mprev, mpers = make_att0_masks_host(8 * c, TLOC // 128)
            hm = np.ones((128, SCANW), np.float32)
            if c == 0:
                hm[:, 0:256] = 0.0
            hsel = np.zeros((128, 8), np.float32)
            if c > 0:
                hsel[:, c - 1] = 1.0
            tm = make_l1_tmask_host((2 * c, 2 * c + 1))
            maps.append({"xe0": xe0, "w0h": w0h, "w1q": w1q, "b0c": b0c,
                         "b1r": b1r, "wqh": wqh, "wkvh": wkvh, "wkh": wkh,
                         "wvh": wvh, "woh": woh, "mown": mown, "mprev": mprev,
                         "mpers": mpers, "rkh": rkh, "rvh": rvh, "rrh": rrh,
                         "td": tdc, "hm": hm, "waph": waph, "wph": wph,
                         "wstd": wstd, "wrec": wrec, "tmask": tm, "hsel": hsel})
        res = _run(nc, maps)
        x6 = cat(res)
        return np.ascontiguousarray(x6.T).reshape(1, S, D).astype(np.float32)

    x1 = run_ffn0(xT)
    x2 = run_att0(x1)
    x3 = run_ffn0(x2)
    x4 = run_l1ffn(x3)
    x5 = run_rwkv(x4)
    x6 = run_l1ffn(x5)
    return np.ascontiguousarray(x6.T).reshape(1, S, D).astype(np.float32)


# version 10
# speedup vs baseline: 1.7651x; 1.5554x over previous
import numpy as np
import ml_dtypes
from contextlib import ExitStack

import concourse.bass as bass
import concourse.mybir as mybir
from concourse.bass_utils import run_bass_kernel_spmd

F32 = mybir.dt.float32
BF16 = mybir.dt.bfloat16
I32 = mybir.dt.int32
AF = mybir.ActivationFunctionType
ALU = mybir.AluOpType
BF16NP = ml_dtypes.bfloat16

D = 2048
DC = D // 128
S = 8192
NCORES = 8
HEADS = 16
HD = 128
NEG = -1e30

ENGINES = ("sync", "scalar", "vector", "tensor", "gpsimd")


class _Holder:
    __slots__ = ("incs",)

    def __init__(self):
        self.incs = []

    def then_inc(self, sem, amt=None):
        self.incs.append((sem, amt))
        return self


class _EngRec:

    def __init__(self, prog, name):
        self._prog = prog
        self._name = name

    def __getattr__(self, attr):
        prog, name = self._prog, self._name

        def call(*args, **kwargs):
            h = _Holder()

            def fn(e, _attr=attr, _args=args, _kwargs=kwargs, _h=h):
                inst = getattr(e, _attr)(*_args, **_kwargs)
                for (sem, amt) in _h.incs:
                    if amt is None:
                        inst.then_inc(sem)
                    else:
                        inst.then_inc(sem, amt)

            prog.fns[name].append(fn)
            return h

        return call


class Prog:
    def __init__(self):
        self.fns = {e: [] for e in ENGINES}
        self.sync = _EngRec(self, "sync")
        self.scalar = _EngRec(self, "scalar")
        self.vector = _EngRec(self, "vector")
        self.tensor = _EngRec(self, "tensor")
        self.gpsimd = _EngRec(self, "gpsimd")

    def eng(self, name):
        return getattr(self, name)

    def replay(self, block):
        @block.sync
        def _(e):
            for fn in self.fns["sync"]:
                fn(e)

        @block.scalar
        def _(e):
            for fn in self.fns["scalar"]:
                fn(e)

        @block.vector
        def _(e):
            for fn in self.fns["vector"]:
                fn(e)

        @block.tensor
        def _(e):
            for fn in self.fns["tensor"]:
                fn(e)

        @block.gpsimd
        def _(e):
            for fn in self.fns["gpsimd"]:
                fn(e)


class Sem:

    __slots__ = ("h", "name", "_cnt", "base")

    def __init__(self, handle, cnt, name=""):
        self.h = handle
        self._cnt = cnt
        self.base = cnt[0]
        self.name = name

    @property
    def n(self):
        return self._cnt[0]

    @n.setter
    def n(self, v):
        self._cnt[0] = v


import re as _re


class Ctx:

    def __init__(self, nc, stack):
        self.nc = nc
        self.stack = stack
        self.p = Prog()
        self._semcnt = 0
        self._sempool = {}

    def sem(self, name=None):
        name = name or f"s{self._semcnt}"
        self._semcnt += 1
        key = _re.sub(r"\d+", "", name)
        if key not in self._sempool:
            self._sempool[key] = (
                self.stack.enter_context(self.nc.semaphore(key)), [0])
        h, cnt = self._sempool[key]
        return Sem(h, cnt, key)

    def sbuf(self, name, shape, dtype):
        return self.stack.enter_context(self.nc.sbuf_tensor(name, shape, dtype))

    def psum(self, name, shape, dtype=F32):
        return self.stack.enter_context(self.nc.psum_tensor(name, shape, dtype))

    def dma(self, qname, out, in_, sem=None, **kw):
        i = self.p.eng(qname).dma_start(out=out, in_=in_, **kw)
        if sem is not None:
            i.then_inc(sem.h, 16)
            sem.n += 16
        return i

    def wait(self, qname, sem, val=None):
        self.p.eng(qname).wait_ge(sem.h, sem.n if val is None else val)


def bcast_col(ap_col, nfree):
    return bass.AP(ap_col.tensor, ap_col.offset, [list(ap_col.ap[0]), [0, nfree]])


def chunks512(t0, n):
    out = []
    while n > 0:
        sz = min(512, n)
        out.append((t0, sz))
        t0 += sz
        n -= sz
    return out


def host_tile_lhsT(A):
    K, M = A.shape
    assert K % 128 == 0 and M % 128 == 0
    H = np.ascontiguousarray(
        A.reshape(K // 128, 128, M // 128, 128).transpose(2, 1, 0, 3)
        .reshape(M // 128, 128, (K // 128) * 128)
    ).astype(BF16NP)
    return H


def f32c(x):
    return np.ascontiguousarray(x, dtype=np.float32)


class Pool:

    def __init__(self, c: Ctx, ln=True):
        self.c = c
        self.ps_a = [c.psum("ps_a0", [128, 512], F32), c.psum("ps_a1", [128, 512], F32)]
        self.ps_b = [c.psum("ps_b0", [128, 512], F32), c.psum("ps_b1", [128, 512], F32)]
        self.ps_s1 = c.psum("ps_s1", [128, 512], F32)
        self.ps_s2 = c.psum("ps_s2", [128, 512], F32)
        self.ps_c = [c.psum("ps_c0", [128, 512], F32), c.psum("ps_c1", [128, 512], F32)]
        self.ones_col = c.sbuf("ones_col", [128, 1], F32)
        self.ones_col_bf = c.sbuf("ones_col_bf", [128, 1], BF16)
        self.ones128 = c.sbuf("ones128", [128, 128], F32)
        self.ones128_bf = c.sbuf("ones128_bf", [128, 128], BF16)
        self.ones_row = c.sbuf("ones_row", [1, 128], F32)
        self.eps1 = c.sbuf("eps1", [1, 1], F32)
        self.ln = ln
        if ln:
            self.ones_row_t = c.sbuf("ones_row_t", [1, 512], F32)
            self.stat = c.sbuf("stat", [1, 4 * 512], F32)
            self.sq = c.sbuf("sq", [128, 2 * 512], F32)
            self.ab = c.sbuf("ab", [128, 2 * 512], F32)
        self.s_cst = c.sem("cst")

    def emit_consts(self):
        c = self.c
        p = c.p
        p.gpsimd.memset(self.ones_col[:], 1.0)
        p.gpsimd.memset(self.ones_col_bf[:], 1.0)
        p.gpsimd.memset(self.ones128[:], 1.0)
        p.gpsimd.memset(self.ones128_bf[:], 1.0)
        p.gpsimd.memset(self.eps1[:], 1e-5)
        i = p.gpsimd.memset(self.ones_row[:], 1.0)
        if self.ln:
            i = p.gpsimd.memset(self.ones_row_t[:], 1.0)
        i.then_inc(self.s_cst.h, 1)
        self.s_cst.n += 1
        for e in ("scalar", "vector", "tensor"):
            c.wait(e, self.s_cst)


def emit_ln(c: Ctx, pool: Pool, name, x_sb, xT, xt0, xn_sb, xnT, xnt0, nT,
            s_x=None, eps=1e-5, out_dtype_note="bf16"):
    p = c.p
    s_sq = c.sem("LNsq")
    s_mm = c.sem("LNmm")
    s_st = c.sem("LNst")
    s_bc = c.sem("LNbc")
    s_ab = c.sem("LNab")
    s_ap = c.sem("LNap")

    if s_x is not None:
        for e in ("scalar", "tensor", "vector"):
            c.wait(e, s_x)

    for (t0, tsz) in chunks512(0, nT):
        mm_base = s_mm.n
        for dc in range(DC):
            slot = dc % 2
            xsl = x_sb[:, dc * xT + xt0 + t0: dc * xT + xt0 + t0 + tsz]
            if dc >= 2:
                p.scalar.wait_ge(s_mm.h, mm_base + dc - 1)
            p.scalar.activation(out=pool.sq[:, slot * 512: slot * 512 + tsz],
                                in_=xsl, func=AF.Square).then_inc(s_sq.h, 1)
            s_sq.n += 1
            p.tensor.wait_ge(s_sq.h, s_sq.n)
            p.tensor.matmul(pool.ps_s1[:, 0:tsz], pool.ones128[:, :], xsl,
                            start=(dc == 0), stop=(dc == DC - 1))
            p.tensor.matmul(pool.ps_s2[:, 0:tsz], pool.ones128[:, :],
                            pool.sq[:, slot * 512: slot * 512 + tsz],
                            start=(dc == 0), stop=(dc == DC - 1)).then_inc(s_mm.h, 1)
            s_mm.n += 1
        mean = pool.stat[0:1, 0:tsz]
        ex2 = pool.stat[0:1, 512:512 + tsz]
        a_v = pool.stat[0:1, 1024:1024 + tsz]
        b_v = pool.stat[0:1, 1536:1536 + tsz]
        p.vector.wait_ge(s_mm.h, s_mm.n)
        p.vector.tensor_scalar_mul(out=mean, in0=pool.ps_s1[0:1, 0:tsz], scalar1=1.0 / D)
        p.vector.tensor_scalar_mul(out=ex2, in0=pool.ps_s2[0:1, 0:tsz],
                                   scalar1=1.0 / D).then_inc(s_st.h, 1)
        s_st.n += 1
        p.vector.wait_ge(s_st.h, s_st.n)
        p.vector.tensor_tensor(out=a_v, in0=mean, in1=mean, op=ALU.mult).then_inc(s_st.h, 1)
        s_st.n += 1
        p.vector.wait_ge(s_st.h, s_st.n)
        p.vector.tensor_tensor(out=ex2, in0=ex2, in1=a_v, op=ALU.subtract).then_inc(s_st.h, 1)
        s_st.n += 1
        p.scalar.wait_ge(s_st.h, s_st.n)
        p.scalar.activation(out=a_v, in_=ex2, func=AF.Ln,
                            bias=pool.eps1[0:1, 0:1]).then_inc(s_st.h, 1)
        s_st.n += 1
        p.scalar.wait_ge(s_st.h, s_st.n)
        p.scalar.activation(out=a_v, in_=a_v, func=AF.Exp, scale=-0.5).then_inc(s_st.h, 1)
        s_st.n += 1
        p.vector.wait_ge(s_st.h, s_st.n)
        p.vector.tensor_tensor(out=b_v, in0=mean, in1=a_v, op=ALU.mult).then_inc(s_st.h, 1)
        s_st.n += 1
        p.vector.wait_ge(s_st.h, s_st.n)
        p.vector.tensor_scalar_mul(out=b_v, in0=b_v, scalar1=-1.0).then_inc(s_st.h, 1)
        s_st.n += 1
        p.tensor.wait_ge(s_st.h, s_st.n)
        p.tensor.matmul(pool.ps_c[0][:, 0:tsz], pool.ones_row[:, :], a_v, start=True, stop=True)
        p.tensor.matmul(pool.ps_c[1][:, 0:tsz], pool.ones_row[:, :], b_v, start=True, stop=True).then_inc(s_bc.h, 1)
        s_bc.n += 1
        p.vector.wait_ge(s_bc.h, s_bc.n)
        p.vector.tensor_copy(out=pool.ab[:, 0:tsz], in_=pool.ps_c[0][:, 0:tsz])
        p.vector.tensor_copy(out=pool.ab[:, 512:512 + tsz], in_=pool.ps_c[1][:, 0:tsz]).then_inc(s_ab.h, 1)
        s_ab.n += 1
        p.vector.wait_ge(s_ab.h, s_ab.n)
        for dc in range(DC):
            slot = dc % 2
            xsl = x_sb[:, dc * xT + xt0 + t0: dc * xT + xt0 + t0 + tsz]
            osl = xn_sb[:, dc * xnT + xnt0 + t0: dc * xnT + xnt0 + t0 + tsz]
            p.vector.tensor_tensor(out=pool.sq[:, slot * 512: slot * 512 + tsz],
                                   in0=xsl, in1=pool.ab[:, 0:tsz], op=ALU.mult)
            p.vector.tensor_tensor(out=osl, in0=pool.sq[:, slot * 512: slot * 512 + tsz],
                                   in1=pool.ab[:, 512:512 + tsz], op=ALU.add).then_inc(s_ap.h, 1)
            s_ap.n += 1
    return s_ap


def emit_ffn0_tb(c: Ctx, pool: Pool, name, x_sb, xT, xt0, xn_sb, xnT, xnt0, nT,
                 bufs, wdr, s_xn, half=0.5):
    assert nT <= 512
    p = c.p
    NCH = 64
    HCH = 16
    w0b, w1b, h_sb = bufs["w0b"], bufs["w1b"], bufs["h_sb"]
    bias0c, bias1r = bufs["bias0c"], bufs["bias1r"]

    s_w0 = c.sem(f"{name}_w0")
    s_w1 = c.sem(f"{name}_w1")
    s_hmm = c.sem(f"{name}_hmm")
    s_gel = c.sem(f"{name}_gel")
    s_q = c.sem(f"{name}_q")
    s_emm = c.sem(f"{name}_emm")
    s_ev = c.sem(f"{name}_ev")

    xn_gate = s_xn.n
    p.tensor.wait_ge(s_xn.h, xn_gate)

    for ncb in range(NCH):
        buf = ncb % 2
        if ncb >= 2:
            c.p.sync.wait_ge(s_hmm.h, s_hmm.base + ncb - 1)
        c.dma("sync", w0b[buf][:, :], wdr["w0h"][ncb], sem=s_w0)
        pb = ncb % 2
        p.tensor.wait_ge(s_w0.h, s_w0.base + (ncb + 1) * 16)
        if ncb >= 2:
            p.tensor.wait_ge(s_gel.h, s_gel.base + ncb - 1)
        for dc in range(DC):
            i = p.tensor.matmul(pool.ps_a[pb][:, 0:nT],
                                w0b[buf][:, dc * 128:(dc + 1) * 128],
                                xn_sb[:, dc * xnT + xnt0: dc * xnT + xnt0 + nT],
                                start=(dc == 0), stop=(dc == DC - 1))
        i.then_inc(s_hmm.h, 1)
        s_hmm.n += 1
        p.scalar.wait_ge(s_hmm.h, s_hmm.n)
        p.scalar.activation(out=h_sb[:, ncb * nT: ncb * nT + nT],
                            in_=pool.ps_a[pb][:, 0:nT], func=AF.Gelu,
                            bias=bias0c[:, ncb:ncb + 1], scale=1.0).then_inc(s_gel.h, 1)
        s_gel.n += 1

    p.tensor.wait_ge(s_gel.h, s_gel.n)
    for k in range(4):
        c.dma("sync", w1b[k % 4][:, :], wdr["w1q"][k], sem=s_w1)
    for hcb in range(HCH):
        pb = hcb % 2
        if hcb >= 2:
            p.tensor.wait_ge(s_ev.h, s_ev.base + hcb - 1)
        for qi in range(4):
            k = 4 * hcb + qi
            buf = k % 4
            p.tensor.wait_ge(s_w1.h, s_w1.base + (k + 1) * 16)
            for j in range(16):
                ncb = qi * 16 + j
                i = p.tensor.matmul(pool.ps_b[pb][:, 0:nT],
                                    w1b[buf][:, j * 128:(j + 1) * 128],
                                    h_sb[:, ncb * nT: ncb * nT + nT],
                                    start=(ncb == 0), stop=False)
            i.then_inc(s_q.h, 1)
            s_q.n += 1
            if k + 4 < 4 * HCH:
                c.p.sync.wait_ge(s_q.h, s_q.base + k + 1)
                c.dma("sync", w1b[buf][:, :], wdr["w1q"][k + 4], sem=s_w1)
        i = p.tensor.matmul(pool.ps_b[pb][:, 0:nT],
                            bias1r[0:1, hcb * 128:(hcb + 1) * 128],
                            pool.ones_row_t[0:1, 0:nT],
                            start=False, stop=True)
        i.then_inc(s_emm.h, 1)
        s_emm.n += 1
        p.vector.wait_ge(s_emm.h, s_emm.n)
        p.vector.scalar_tensor_tensor(
            out=x_sb[:, hcb * xT + xt0: hcb * xT + xt0 + nT],
            in0=pool.ps_b[pb][:, 0:nT], scalar=half,
            in1=x_sb[:, hcb * xT + xt0: hcb * xT + xt0 + nT],
            op0=ALU.mult, op1=ALU.add).then_inc(s_ev.h, 1)
        s_ev.n += 1
    return s_ev


def _blocks(T, TB):
    out = []
    t0 = 0
    while t0 < T:
        out.append((t0, min(TB, T - t0)))
        t0 += min(TB, T - t0)
    return out


def emit_ffn0(c: Ctx, pool: Pool, name, x_sb, T, wdr, bufs, s_x, TB=512, half=0.5):
    sevs = []
    for (i, (t0, tb)) in enumerate(_blocks(T, TB)):
        s_ap = emit_ln(c, pool, f"{name}_ln{i}", x_sb, T, t0, bufs["xn_sb"], TB, 0, tb, s_x=s_x)
        s_ev = emit_ffn0_tb(c, pool, f"{name}_tb{i}", x_sb, T, t0,
                            bufs["xn_sb"], TB, 0, tb, bufs, wdr, s_ap, half=half)
        sevs.append(s_ev)
        s_x = s_ev
    return sevs



def emit_ffn0_dram(c: Ctx, pool: Pool, name, x_src, y_dst, T, bufs, wdr,
                   s_pre=None, half=0.5, TB=512):
    p = c.p
    s_ld = c.sem(f"{name}_ld")
    s_st = c.sem(f"{name}_st")
    if s_pre is not None:
        c.wait("sync", s_pre)
    nst_prev = 0
    for (i, (t0, tsz)) in enumerate(_blocks(T, TB)):
        if i > 0:
            c.wait("sync", s_st)
        for dc in range(DC):
            c.dma("sync", bufs["x_tb"][:, dc * TB: dc * TB + tsz],
                  x_src(dc, t0, tsz), sem=s_ld)
        for e in ("scalar", "tensor", "vector"):
            c.wait(e, s_ld)
        s_ap = emit_ln(c, pool, f"{name}_ln{i}", bufs["x_tb"], TB, 0,
                       bufs["xn_sb"], TB, 0, tsz)
        s_ev = emit_ffn0_tb(c, pool, f"{name}_tb{i}", bufs["x_tb"], TB, 0,
                            bufs["xn_sb"], TB, 0, tsz, bufs, wdr, s_ap, half=half)
        c.wait("sync", s_ev)
        for dc in range(DC):
            c.dma("sync", y_dst(dc, t0, tsz),
                  bufs["x_tb"][:, dc * TB: dc * TB + tsz], sem=s_st)
    return s_st


def emit_proj(c: Ctx, pool: Pool, name, whd, MC, KC, src_sb, srcT, src_t0, nT,
              evac_fn, s_src, wpb, psum_pair=None, wgate=None, extra_mm=None,
              evac_eng="vector"):
    p = c.p
    s_w = c.sem("PJw")
    s_mm = c.sem("PJmm")
    s_ev = c.sem("PJev" + evac_eng[0])
    ps = psum_pair or pool.ps_a
    src_gate = s_src.n if s_src is not None else None
    TCH = chunks512(0, nT)
    ntch = len(TCH)
    g = 0
    for mc in range(MC):
        buf = mc % 2
        if mc >= 2:
            c.p.sync.wait_ge(s_mm.h, s_mm.base + (mc - 1) * ntch)
        elif wgate is not None:
            c.p.sync.wait_ge(wgate[0].h, wgate[1])
        c.dma("sync", wpb[buf][:, 0:KC * 128], whd[mc], sem=s_w)
        for (t0, tsz) in TCH:
            pb = g % 2
            p.tensor.wait_ge(s_w.h, s_w.base + (mc + 1) * 16)
            if g == 0 and src_gate is not None:
                p.tensor.wait_ge(s_src.h, src_gate)
            if g >= 2:
                p.tensor.wait_ge(s_ev.h, s_ev.base + g - 1)
            for kc in range(KC):
                i = p.tensor.matmul(ps[pb][:, 0:tsz],
                                    wpb[buf][:, kc * 128:(kc + 1) * 128],
                                    src_sb[:, kc * srcT + src_t0 + t0: kc * srcT + src_t0 + t0 + tsz],
                                    start=(kc == 0),
                                    stop=(kc == KC - 1 and extra_mm is None))
            if extra_mm is not None:
                i = extra_mm(mc, t0, tsz, ps[pb])
            i.then_inc(s_mm.h, 1)
            s_mm.n += 1
            evac_fn(mc, t0, tsz, ps[pb], s_mm, s_ev)
            g += 1
    return s_mm, s_ev


ATT0_SCALE = float(HD) ** -0.5


def emit_ident(c: Ctx, name):
    p = c.p
    ident = c.sbuf(f"{name}_ident", [128, 128], F32)
    s_msk = c.sem(f"{name}_identc")
    p.gpsimd.memset(ident[:], 0.0)
    p.gpsimd.affine_select(out=ident[:], in_=ident[:], compare_op=ALU.not_equal,
                           fill=1.0, base=0, pattern=[[-1, 128]], channel_multiplier=1).then_inc(s_msk.h, 1)
    s_msk.n += 1
    return dict(ident=ident, s_msk=s_msk)


def make_att0_masks_host(gq0, NQT):
    i = np.arange(128)[:, None]
    j = np.arange(128)[None, :]
    mown = np.where(j <= i, 0.0, NEG).astype(np.float32)
    mprev = np.zeros((128, NQT * 128), np.float32)
    mpers = np.zeros((128, NQT * 16), np.float32)
    for qt in range(NQT):
        gq = gq0 + qt
        if gq == 0:
            mprev[:, qt * 128:(qt + 1) * 128] = NEG
            mpers[:, qt * 16:(qt + 1) * 16] = NEG
        elif gq == 1:
            mprev[:, qt * 128:(qt + 1) * 128] = np.where((j > i) | (j < 16), 0.0, NEG)
            mpers[:, qt * 16:(qt + 1) * 16] = NEG
        else:
            mprev[:, qt * 128:(qt + 1) * 128] = np.where(j > i, 0.0, NEG)
    return mown, mprev, mpers


def emit_att0(c: Ctx, pool: Pool, name, x_src, y_src, y_dst, TLOCAL, wdr, bufs):
    p = c.p
    EXTW = 128 + TLOCAL + 128
    NQT = TLOCAL // 128
    u1, qy, k_sb, lat = bufs["u1"], bufs["qy_sb"], bufs["k_sb"], bufs["lat_sb"]
    masks = bufs["masks"]

    s_ld = c.sem(f"{name}_ld")

    s_ap = None
    for (i, (t0, tsz)) in enumerate(chunks512(0, EXTW)):
        for dc in range(DC):
            c.dma("sync", bufs["x_tb"][:, dc * 512: dc * 512 + tsz],
                  x_src(dc, t0, tsz), sem=s_ld)
        for e in ("scalar", "tensor", "vector"):
            c.wait(e, s_ld)
        s_ap = emit_ln(c, pool, f"{name}_ln{i}", bufs["x_tb"], 512, 0,
                       u1, EXTW, t0, tsz)
        c.wait("sync", s_ap)
    s_lnap = s_ap

    def mk_evac_bf16(dst, dstT, scale=None):
        def evac(mc, t0, tsz, ps_ap, s_mm, s_ev):
            p.scalar.wait_ge(s_mm.h, s_mm.n)
            if scale is None:
                i = p.scalar.copy(out=dst[:, mc * dstT + t0: mc * dstT + t0 + tsz],
                                  in_=ps_ap[:, 0:tsz])
            else:
                i = p.scalar.mul(out=dst[:, mc * dstT + t0: mc * dstT + t0 + tsz],
                                 in_=ps_ap[:, 0:tsz], mul=scale)
            i.then_inc(s_ev.h, 1)
            s_ev.n += 1
        return evac

    mm_lat, ev_lat = emit_proj(c, pool, f"{name}_lat", wdr["wkvh"], 4, DC,
                               u1, EXTW, 0, EXTW, mk_evac_bf16(lat, EXTW),
                               s_ap, bufs["wpb"], evac_eng="scalar")
    mm_q, ev_q = emit_proj(c, pool, f"{name}_q", wdr["wqh"], 16, DC,
                           u1, EXTW, 128, TLOCAL, mk_evac_bf16(qy, TLOCAL, ATT0_SCALE),
                           s_ap, bufs["wpb"], wgate=(mm_lat, mm_lat.n),
                           evac_eng="scalar")
    mm_k, ev_k = emit_proj(c, pool, f"{name}_k", wdr["wkh"], 16, 4,
                           lat, EXTW, 0, EXTW, mk_evac_bf16(k_sb, EXTW),
                           ev_lat, bufs["wpb"], wgate=(mm_q, mm_q.n),
                           evac_eng="scalar")

    s_vstg = c.sem(f"{name}_vstg")
    s_trn = c.sem(f"{name}_trn")
    s_vtm = c.sem(f"{name}_vtm")
    NTILE = EXTW // 128
    vstate = {"vg": 0, "trn_after": []}

    def evac_v(mc, t0, tsz, ps_ap, s_mm, s_ev):
        vg = vstate["vg"]
        stg = bufs["vstg"][vg % 2]
        p.scalar.wait_ge(s_mm.h, s_mm.n)
        if vg >= 2:
            p.scalar.wait_ge(s_trn.h, vstate["trn_after"][vg - 2])
        p.scalar.copy(out=stg[:, 0:tsz], in_=ps_ap[:, 0:tsz]).then_inc(s_vstg.h, 1)
        s_vstg.n += 1
        p.tensor.wait_ge(s_vstg.h, s_vstg.n)
        nsub = (tsz + 127) // 128
        for j in range(nsub):
            w = min(128, tsz - j * 128)
            tau = s_trn.n - s_trn.base
            bank = pool.ps_b[tau % 2]
            if tau >= 2:
                p.tensor.wait_ge(s_vtm.h, s_vtm.base + tau - 1)
            i = p.tensor.transpose(bank[0:w, 0:128],
                                   stg[:, j * 128: j * 128 + w], masks["ident"][:, :])
            i.then_inc(s_trn.h, 1)
            s_trn.n += 1
            p.vector.wait_ge(s_trn.h, s_trn.n)
            jt = (t0 + j * 128) // 128
            p.vector.tensor_copy(
                out=u1[0:w, mc * EXTW + jt * 128: mc * EXTW + jt * 128 + 128],
                in_=bank[0:w, 0:128]).then_inc(s_vtm.h, 1)
            s_vtm.n += 1
        p.vector.engine_nop().then_inc(s_ev.h, 1)
        s_ev.n += 1
        vstate["trn_after"].append(s_trn.n)
        vstate["vg"] = vg + 1

    mm_v, ev_v = emit_proj(c, pool, f"{name}_v", wdr["wvh"], 16, 4,
                           lat, EXTW, 0, EXTW, evac_v,
                           ev_lat, bufs["wpb"], wgate=(mm_k, mm_k.n))

    small = bufs["small"]
    bcol = bufs["bcol"]
    s_bnd = c.sem(f"{name}_bnd")
    s_bmm = c.sem(f"{name}_bmm")
    s_ch = c.sem(f"{name}_ch")

    def vedge():
        s_ch.n += 1
        return (s_ch.h, s_ch.n)

    def emit_norm_max(src, ncols, out_idx, gate):
        p.vector.memset(small[0:1, out_idx:out_idx + 1], 0.0)
        for (t0, tsz) in chunks512(0, ncols):
            p.scalar.wait_ge(gate.h, gate.n)
            if s_bmm.n > 0:
                p.scalar.wait_ge(s_bmm.h, s_bmm.n)
            p.scalar.activation(out=pool.sq[:, 0:tsz], in_=src[:, t0:t0 + tsz],
                                func=AF.Square).then_inc(s_bnd.h, 1)
            s_bnd.n += 1
            p.tensor.wait_ge(s_bnd.h, s_bnd.n)
            p.tensor.matmul(pool.ps_s2[:, 0:tsz], pool.ones128[:, :],
                            pool.sq[:, 0:tsz], start=True, stop=True).then_inc(s_bmm.h, 1)
            s_bmm.n += 1
            p.vector.wait_ge(s_bmm.h, s_bmm.n)
            h1, v1 = s_ch.h, None
            p.vector.reduce_max(out=small[0:1, 3:4], in_=pool.ps_s2[0:1, 0:tsz],
                                axis=mybir.AxisListType.X).then_inc(*vedge())
            p.vector.wait_ge(s_ch.h, s_ch.n)
            p.vector.tensor_tensor(out=small[0:1, out_idx:out_idx + 1],
                                   in0=small[0:1, out_idx:out_idx + 1],
                                   in1=small[0:1, 3:4], op=ALU.max).then_inc(s_bnd.h, 1)
            s_bnd.n += 1

    emit_norm_max(qy, 16 * TLOCAL, 0, ev_q)
    emit_norm_max(k_sb, 16 * EXTW, 1, ev_k)
    p.vector.tensor_tensor(out=small[0:1, 2:3], in0=small[0:1, 0:1],
                           in1=small[0:1, 1:2], op=ALU.mult).then_inc(s_bnd.h, 1)
    s_bnd.n += 1
    p.scalar.wait_ge(s_bnd.h, s_bnd.n)
    p.scalar.activation(out=small[0:1, 2:3], in_=small[0:1, 2:3],
                        func=AF.Sqrt).then_inc(*vedge())
    p.scalar.wait_ge(s_ch.h, s_ch.n)
    p.scalar.mul(out=small[0:1, 2:3], in_=small[0:1, 2:3], mul=-1.0).then_inc(s_bnd.h, 1)
    s_bnd.n += 1
    p.tensor.wait_ge(s_bnd.h, s_bnd.n)
    p.tensor.matmul(pool.ps_s2[:, 0:1], pool.ones_row[:, :], small[0:1, 2:3],
                    start=True, stop=True).then_inc(s_bmm.h, 1)
    s_bmm.n += 1
    p.vector.wait_ge(s_bmm.h, s_bmm.n)
    p.vector.tensor_copy(out=bcol[:, 0:1], in_=pool.ps_s2[:, 0:1]).then_inc(s_bnd.h, 1)
    s_bnd.n += 1

    if "dbg_preattn" in bufs:
        bufs["dbg_preattn"](c, s_bnd)

    p_sb, pn_sb, pt_sb = bufs["p_sb"], bufs["pn_sb"], bufs["pt_sb"]
    sm_sb = bufs["sm_sb"]
    den = bufs["den"]
    rcol = bufs["rcol"]
    s_smm = c.sem(f"{name}_smm")
    s_msk2 = c.sem(f"{name}_msk2")
    s_exp = c.sem(f"{name}_exp")
    s_pn = c.sem(f"{name}_pn")
    s_rc = c.sem(f"{name}_rc")
    s_ptr = c.sem(f"{name}_ptr")
    s_ptv = c.sem(f"{name}_ptv")
    s_pv = c.sem(f"{name}_pv")
    s_oev = c.sem(f"{name}_oev")
    b_msk = s_msk2.n
    b_exp = s_exp.n
    b_pn = s_pn.n
    b_ptr = s_ptr.n
    b_ptv = s_ptv.n
    b_pv = s_pv.n
    b_oev = s_oev.n

    for e in ("tensor", "vector", "scalar"):
        c.wait(e, s_bnd)
        c.wait(e, s_vtm)

    units = [(h, qt) for h in range(HEADS) for qt in range(NQT)]
    NU = len(units)

    def att_blocks(qt):
        prevoff = (128 + (qt - 1) * 128) if qt > 0 else 0
        return [(0, 128, prevoff, bufs["mprev_sb"][:, qt * 128:(qt + 1) * 128], qt),
                (128, 128, 128 + qt * 128, bufs["mown_sb"][:, 0:128], qt + 1),
                (256, 16, 128 + TLOCAL, bufs["mpers_sb"][:, qt * 16:(qt + 1) * 16], NTILE - 1)]

    def stage1(u):
        h, qt = units[u]
        pb = u % 2
        d = pb * 272
        qtile = qy[:, h * TLOCAL + qt * 128: h * TLOCAL + (qt + 1) * 128]
        blocks = att_blocks(qt)
        if u >= 2:
            p.tensor.wait_ge(s_msk2.h, b_msk + 3 * (u - 1))
        for (so, sw, ko, msk, jt) in blocks:
            i = p.tensor.matmul(pool.ps_a[pb][:, so:so + sw], qtile,
                                k_sb[:, h * EXTW + ko: h * EXTW + ko + sw],
                                start=True, stop=True)
        i.then_inc(s_smm.h, 1)
        s_smm.n += 1
        p.vector.wait_ge(s_smm.h, s_smm.n)
        if u >= 2:
            p.vector.wait_ge(s_exp.h, b_exp + u - 1)
        for (so, sw, ko, msk, jt) in blocks:
            p.vector.tensor_tensor(out=sm_sb[:, d + so: d + so + sw],
                                   in0=pool.ps_a[pb][:, so:so + sw],
                                   in1=msk if msk.shape[1] == sw else msk[:, 0:sw],
                                   op=ALU.add).then_inc(s_msk2.h, 1)
            s_msk2.n += 1
        p.scalar.wait_ge(s_msk2.h, s_msk2.n)
        if u >= 2:
            p.scalar.wait_ge(s_pn.h, b_pn + u - 1)
        p.scalar.activation(out=p_sb[:, d: d + 272], in_=sm_sb[:, d: d + 272],
                            func=AF.Exp, bias=bcol[:, 0:1],
                            accum_out=den[pb][:, 0:1]).then_inc(s_exp.h, 1)
        s_exp.n += 1
        p.vector.wait_ge(s_exp.h, s_exp.n)
        if u >= 2:
            p.vector.wait_ge(s_ptr.h, b_ptr + 3 * (u - 1))
        p.vector.reciprocal(out=rcol[pb][:, 0:1], in_=den[pb][:, 0:1]).then_inc(s_rc.h, 1)
        s_rc.n += 1
        p.vector.wait_ge(s_rc.h, s_rc.n)
        p.vector.tensor_scalar(out=pn_sb[:, d: d + 272], in0=p_sb[:, d: d + 272],
                               scalar1=rcol[pb][:, 0:1], scalar2=None,
                               op0=ALU.mult).then_inc(s_pn.h, 1)
        s_pn.n += 1

    def stage2(u):
        h, qt = units[u]
        pb = u % 2
        d = pb * 272
        dp = pb * 384
        blocks = att_blocks(qt)
        p.tensor.wait_ge(s_pn.h, b_pn + u + 1)
        for (bi, (so, sw, ko, msk, jt)) in enumerate(blocks):
            tau = s_ptr.n - b_ptr
            bank = (pool.ps_b[1], pool.ps_c[0])[tau % 2]
            if tau >= 2:
                p.tensor.wait_ge(s_ptv.h, b_ptv + tau - 1)
            i = p.tensor.transpose(bank[0:sw, 0:128],
                                   pn_sb[:, d + so: d + so + sw], masks["ident"][:, :])
            i.then_inc(s_ptr.h, 1)
            s_ptr.n += 1
            p.vector.wait_ge(s_ptr.h, s_ptr.n)
            if bi == 0 and u >= 2:
                p.vector.wait_ge(s_pv.h, b_pv + u - 1)
            p.vector.tensor_copy(out=pt_sb[0:sw, dp + bi * 128: dp + bi * 128 + 128],
                                 in_=bank[0:sw, 0:128]).then_inc(s_ptv.h, 1)
            s_ptv.n += 1
        obank = (pool.ps_b[0], pool.ps_c[1])[pb]
        p.tensor.wait_ge(s_ptv.h, s_ptv.n)
        if u >= 2:
            p.tensor.wait_ge(s_oev.h, b_oev + u - 1)
        for (bi, (so, sw, ko, msk, jt)) in enumerate(blocks):
            i = p.tensor.matmul(obank[:, 0:128],
                                u1[0:sw, h * EXTW + jt * 128: h * EXTW + jt * 128 + 128],
                                pt_sb[0:sw, dp + bi * 128: dp + bi * 128 + 128],
                                start=(bi == 0), stop=(bi == len(blocks) - 1))
        i.then_inc(s_pv.h, 1)
        s_pv.n += 1
        p.scalar.wait_ge(s_pv.h, s_pv.n)
        p.scalar.copy(out=qy[:, h * TLOCAL + qt * 128: h * TLOCAL + (qt + 1) * 128],
                      in_=obank[:, 0:128]).then_inc(s_oev.h, 1)
        s_oev.n += 1

    for u in range(NU):
        stage1(u)
        if u >= 1:
            stage2(u - 1)
    stage2(NU - 1)
    u = NU

    s_xld = c.sem(f"{name}_xld")
    s_st = c.sem(f"{name}_st")

    def evac_wo(mc, t0, tsz, ps_ap, s_mm, s_ev):
        p.vector.wait_ge(s_mm.h, s_mm.n)
        p.vector.wait_ge(s_xld.h, s_xld.n)
        p.vector.scalar_tensor_tensor(
            out=bufs["x_tb"][:, mc * 512: mc * 512 + tsz],
            in0=ps_ap[:, 0:tsz], scalar=1.0,
            in1=bufs["x_tb"][:, mc * 512: mc * 512 + tsz],
            op0=ALU.mult, op1=ALU.add).then_inc(s_ev.h, 1)
        s_ev.n += 1

    wo_gate = (mm_v, mm_v.n)
    for (ti, (t0, tsz)) in enumerate(chunks512(0, TLOCAL)):
        if ti > 0:
            c.wait("gpsimd", s_st)
        else:
            c.wait("gpsimd", s_lnap)
        for dc in range(DC):
            c.dma("gpsimd", bufs["x_tb"][:, dc * 512: dc * 512 + tsz],
                  y_src(dc, t0, tsz), sem=s_xld)
        mm_wo, ev_wo = emit_proj(c, pool, f"{name}_wo{ti}", wdr["woh"], 16, DC,
                                 qy, TLOCAL, t0, tsz, evac_wo,
                                 s_oev, bufs["wpb"],
                                 wgate=wo_gate, psum_pair=pool.ps_c)
        wo_gate = (mm_wo, mm_wo.n)
        c.wait("gpsimd", ev_wo)
        for dc in range(DC):
            c.dma("gpsimd", y_dst(dc, t0, tsz),
                  bufs["x_tb"][:, dc * 512: dc * 512 + tsz], sem=s_st)
    return s_st



def build_ffn0_launch(TLOC=1024, TB=512, half=0.5):
    nc = bass.Bass("TRN2", target_bir_lowering=False, debug=False)
    x_in = nc.dram_tensor("x", [D, TLOC], F32, kind="ExternalInput").ap()
    w0h_d = nc.dram_tensor("w0h", [64, 128, DC * 128], BF16, kind="ExternalInput").ap()
    w1h_d = nc.dram_tensor("w1q", [64, 128, 16 * 128], BF16, kind="ExternalInput").ap()
    b0_d = nc.dram_tensor("b0c", [128, 64], F32, kind="ExternalInput").ap()
    b1_d = nc.dram_tensor("b1r", [1, D], F32, kind="ExternalInput").ap()
    y_out = nc.dram_tensor("y", [D, TLOC], F32, kind="ExternalOutput").ap()

    with ExitStack() as stack:
        c = Ctx(nc, stack)
        p = c.p
        pool = Pool(c)
        x_sb = c.sbuf("x_sb", [128, DC * TLOC], F32)
        xn_sb = c.sbuf("xn_sb", [128, DC * TB], BF16)
        w0b = [c.sbuf(f"w0b{i}", [128, DC * 128], BF16) for i in range(2)]
        w1b = [c.sbuf(f"w1b{i}", [128, 16 * 128], BF16) for i in range(4)]
        h_sb = c.sbuf("h_sb", [128, 64 * TB], BF16)
        bias0c = c.sbuf("bias0c", [128, 64], F32)
        bias1r = c.sbuf("bias1r", [1, D], F32)

        s_ld = c.sem("ld")
        s_out = c.sem("out")

        pool.emit_consts()
        c.dma("sync", x_sb[:], x_in.rearrange("(dc p) t -> p dc t", p=128), sem=s_ld)
        c.dma("sync", bias0c[:], b0_d[:, :], sem=s_ld)
        c.dma("sync", bias1r[:], b1_d[:, :], sem=s_ld)
        for e in ("scalar", "vector", "tensor"):
            c.wait(e, s_ld)

        wdr = dict(w0h=[w0h_d[i] for i in range(64)],
                   w1q=[w1h_d[i] for i in range(64)])
        bufs = dict(w0b=w0b, w1b=w1b, h_sb=h_sb, bias0c=bias0c, bias1r=bias1r,
                    xn_sb=xn_sb)

        sevs = emit_ffn0(c, pool, "f", x_sb, TLOC, wdr, bufs, s_ld, TB=TB, half=half)
        for s in sevs:
            c.wait("sync", s)
        c.dma("sync", y_out.rearrange("(dc p) t -> p dc t", p=128), x_sb[:], sem=s_out)
        c.wait("sync", s_out)

        with nc.Block() as block:
            c.p.replay(block)
    return nc


def prep_ffn0_weights(W0, b0, W1, b1):
    w0h = host_tile_lhsT(np.asarray(W0, dtype=np.float32).T)
    w1h = host_tile_lhsT(np.asarray(W1, dtype=np.float32).T)
    w1q = np.ascontiguousarray(
        w1h.reshape(16, 128, 4, 16 * 128).transpose(0, 2, 1, 3).reshape(64, 128, 16 * 128))
    b0c = f32c(np.asarray(b0).reshape(64, 128).T)
    b1r = f32c(np.asarray(b1).reshape(1, D))
    return w0h, w1q, b0c, b1r


def emit_rwkv(c: Ctx, pool: Pool, name, x_src, y_src, y_dst, TLOCAL, wdr, bufs):
    p = c.p
    EXTW = 264 + TLOCAL
    SCANW = 256 + TLOCAL
    ln_sb = bufs["ln_sb"]
    s_ld = c.sem(f"{name}_ld")
    s_ch = c.sem(f"{name}_ch")

    s_ap = None
    for (i, (t0, tsz)) in enumerate(chunks512(0, EXTW)):
        for dc in range(DC):
            srcs = x_src(dc, t0, tsz)
            if not isinstance(srcs, list):
                srcs = [(srcs, 0, tsz)]
            for (ap, ro, w) in srcs:
                c.dma("sync", bufs["x_tb"][:, dc * 512 + ro: dc * 512 + ro + w],
                      ap, sem=s_ld)
        for e in ("scalar", "tensor", "vector"):
            c.wait(e, s_ld)
        s_ap = emit_ln(c, pool, f"{name}_ln{i}", bufs["x_tb"], 512, 0,
                       ln_sb, EXTW, t0, tsz)
        c.wait("sync", s_ap)
    s_lnap = s_ap

    s_w2 = c.sem(f"{name}_w2")
    p.scalar.wait_ge(s_ld.h, s_ld.n)
    p.scalar.activation(out=bufs["e1_sb"][:], in_=bufs["td_sb"][:],
                        func=AF.Exp).then_inc(s_ch.h, 1)
    s_ch.n += 1
    p.scalar.wait_ge(s_ch.h, s_ch.n)
    p.scalar.activation(out=bufs["w_sb"][:], in_=bufs["e1_sb"][:],
                        func=AF.Exp, scale=-1.0).then_inc(s_w2.h, 1)
    s_w2.n += 1
    c.wait("vector", s_w2)

    s_wld = c.sem(f"{name}_wld")
    s_mm = c.sem(f"{name}_mm")
    s_ev = c.sem(f"{name}_ev")
    s_dve = c.sem(f"{name}_dve")
    s_xld = c.sem(f"{name}_xld")
    s_st = c.sem(f"{name}_st")
    ek, hmk, v_sb, a_sb = bufs["ek"], bufs["hmk"], bufs["v"], bufs["a"]
    num, den = bufs["num"], bufs["den"]
    rec, wkv, outb, r_sb = bufs["rec"], bufs["wkv"], bufs["outb"], bufs["r_sb"]
    hm_sb = bufs["hm_sb"]
    SCH = chunks512(0, SCANW)
    LCH = chunks512(0, TLOCAL)
    g = 0
    for dc in range(DC):
        if dc >= 1:
            c.p.sync.wait_ge(s_mm.h, s_mm.base + dc * 8)
        else:
            c.p.sync.wait_ge(s_lnap.h, s_lnap.n)
        c.dma("sync", bufs["wpb3"][0][:, :], wdr["wkh"][dc], sem=s_wld)
        c.dma("sync", bufs["wpb3"][1][:, :], wdr["wvh"][dc], sem=s_wld)
        c.dma("sync", bufs["wpb3"][2][:, :], wdr["wrh"][dc], sem=s_wld)
        specs = [(0, SCH, 7, ek, "exp"), (1, SCH, 7, v_sb, "copy"),
                 (2, LCH, 264, r_sb, "copy")]
        if dc == 0:
            p.tensor.wait_ge(s_lnap.h, s_lnap.n)
        for (wi, CH, coff, dst, mode) in specs:
            p.tensor.wait_ge(s_wld.h, s_wld.base + (dc * 3 + wi + 1) * 16)
            for (t0, tsz) in CH:
                pb = g % 2
                if g >= 2:
                    p.tensor.wait_ge(s_ev.h, s_ev.base + g - 1)
                for kc in range(DC):
                    i = p.tensor.matmul(pool.ps_a[pb][:, 0:tsz],
                                        bufs["wpb3"][wi][:, kc * 128:(kc + 1) * 128],
                                        ln_sb[:, kc * EXTW + coff + t0: kc * EXTW + coff + t0 + tsz],
                                        start=(kc == 0), stop=(kc == DC - 1))
                i.then_inc(s_mm.h, 1)
                s_mm.n += 1
                p.scalar.wait_ge(s_mm.h, s_mm.n)
                if dc >= 1:
                    p.scalar.wait_ge(s_dve.h, s_dve.base + 2 * dc - 1)
                if mode == "exp":
                    i = p.scalar.activation(out=dst[:, t0:t0 + tsz],
                                            in_=pool.ps_a[pb][:, 0:tsz], func=AF.Exp)
                else:
                    i = p.scalar.copy(out=dst[:, t0:t0 + tsz], in_=pool.ps_a[pb][:, 0:tsz])
                i.then_inc(s_ev.h, 1)
                s_ev.n += 1
                g += 1
        p.vector.wait_ge(s_ev.h, s_ev.n)
        if dc >= 1:
            pass
        w_bcast = bass.AP(bufs["w_sb"], dc, [[16, 128], [0, SCANW]])
        p.vector.tensor_tensor(out=hmk[:, 0:SCANW], in0=ek[:, 0:SCANW],
                               in1=hm_sb[:, 0:SCANW], op=ALU.mult)
        p.vector.tensor_tensor(out=a_sb[:, 0:SCANW], in0=hmk[:, 0:SCANW],
                               in1=v_sb[:, 0:SCANW], op=ALU.mult)
        p.vector.tensor_tensor_scan(out=num[:, 0:SCANW], data0=w_bcast,
                                    data1=a_sb[:, 0:SCANW], initial=0.0,
                                    op0=ALU.mult, op1=ALU.add)
        p.vector.tensor_tensor_scan(out=den[:, 0:SCANW], data0=w_bcast,
                                    data1=hmk[:, 0:SCANW], initial=0.0,
                                    op0=ALU.mult, op1=ALU.add)
        p.vector.tensor_scalar_add(out=den[:, 0:SCANW], in0=den[:, 0:SCANW],
                                   scalar1=1e-8).then_inc(s_ch.h, 1)
        s_ch.n += 1
        p.vector.wait_ge(s_ch.h, s_ch.n)
        p.vector.reciprocal(out=rec[:, 0:TLOCAL], in_=den[:, 256:SCANW]).then_inc(s_ch.h, 1)
        s_ch.n += 1
        p.vector.wait_ge(s_ch.h, s_ch.n)
        p.vector.tensor_tensor(out=wkv[:, 0:TLOCAL], in0=num[:, 256:SCANW],
                               in1=rec[:, 0:TLOCAL], op=ALU.mult)
        p.vector.tensor_tensor(out=outb[:, 0:TLOCAL], in0=r_sb[:, 0:TLOCAL],
                               in1=wkv[:, 0:TLOCAL], op=ALU.mult).then_inc(s_dve.h, 1)
        s_dve.n += 1
        if dc == 0:
            c.wait("gpsimd", s_lnap)
        for (t0, tsz) in LCH:
            c.dma("gpsimd", bufs["x_tb"][:, t0:t0 + tsz], y_src(dc, t0, tsz), sem=s_xld)
        p.vector.wait_ge(s_xld.h, s_xld.n)
        p.vector.tensor_tensor(out=outb[:, 0:TLOCAL], in0=outb[:, 0:TLOCAL],
                               in1=bufs["x_tb"][:, 0:TLOCAL], op=ALU.add).then_inc(s_dve.h, 1)
        s_dve.n += 1
        c.wait("gpsimd", s_dve)
        for (t0, tsz) in LCH:
            c.dma("gpsimd", y_dst(dc, t0, tsz), outb[:, t0:t0 + tsz], sem=s_st)
        c.p.vector.wait_ge(s_st.h, s_st.n)
    return s_st


L1_SCALE = float(HD) ** -0.5


def emit_l1_colscale(c: Ctx, pool: Pool, name, bufs, gate):
    p = c.p
    ws, wr = bufs["wstd_row"], bufs["wrec_row"]
    s_cs = c.sem(f"{name}_cs")
    s_ch = c.sem(f"{name}_ch")
    tmp = bufs["l1tmp"]
    colscale = bufs["colscale"]
    swi = bufs["swi"]
    swf = bufs["swf"]
    sg = bufs["sg"]
    s_in = bufs["s_wsin"]

    def edge(i):
        i.then_inc(s_ch.h, 1)
        s_ch.n += 1

    c.wait("vector", s_in)
    c.wait("scalar", s_in)
    edge(p.vector.tensor_scalar_max(out=tmp[0:1, 0:16], in0=ws[0:1, 0:16], scalar1=1e-8))
    edge(p.vector.tensor_scalar_max(out=tmp[0:1, 16:32], in0=wr[0:1, 0:16], scalar1=1e-8))
    p.scalar.wait_ge(s_ch.h, s_ch.n)
    edge(p.scalar.activation(out=tmp[0:1, 0:16], in_=tmp[0:1, 0:16], func=AF.Sqrt))
    p.scalar.wait_ge(s_ch.h, s_ch.n)
    edge(p.scalar.activation(out=tmp[0:1, 16:32], in_=tmp[0:1, 16:32], func=AF.Sqrt))
    edge(p.vector.tensor_scalar(out=tmp[0:1, 32:48], in0=wr[0:1, 0:16],
                                scalar1=0.1, scalar2=None, op0=ALU.is_gt))
    p.tensor.wait_ge(s_ch.h, s_ch.n)
    p.tensor.wait_ge(gate[0].h, gate[1])
    p.tensor.matmul(pool.ps_c[0][:, 0:16], pool.ones_row[:, :], tmp[0:1, 0:16],
                    start=True, stop=True)
    p.tensor.matmul(pool.ps_c[0][:, 16:32], pool.ones_row[:, :], tmp[0:1, 16:32],
                    start=True, stop=True)
    i = p.tensor.matmul(pool.ps_c[0][:, 32:48], pool.ones_row[:, :], tmp[0:1, 32:48],
                        start=True, stop=True)
    i.then_inc(s_cs.h, 1)
    s_cs.n += 1
    p.vector.wait_ge(s_cs.h, s_cs.n)
    edge(p.vector.tensor_copy(out=swf[:, 0:16], in_=pool.ps_c[0][:, 0:16]))
    edge(p.vector.tensor_copy(out=sg[:, 0:16], in_=pool.ps_c[0][:, 16:32]))
    edge(p.vector.tensor_copy(out=colscale[:, 32:48], in_=pool.ps_c[0][:, 32:48]))
    c.wait("gpsimd", s_ch)
    p.gpsimd.affine_select(out=swf[:, 0:16], in_=swf[:, 0:16], compare_op=ALU.is_ge,
                           fill=0.0, base=119, pattern=[[0, 16]], channel_multiplier=-1)
    p.gpsimd.affine_select(out=sg[:, 0:16], in_=sg[:, 0:16], compare_op=ALU.is_ge,
                           fill=0.0, base=-120, pattern=[[0, 16]], channel_multiplier=1)
    i = p.gpsimd.affine_select(out=colscale[:, 32:48], in_=colscale[:, 32:48],
                               compare_op=ALU.is_ge, fill=0.0, base=-120,
                               pattern=[[0, 16]], channel_multiplier=1)
    i.then_inc(s_cs.h, 1)
    s_cs.n += 1
    p.vector.wait_ge(s_cs.h, s_cs.n)
    p.vector.tensor_copy(out=swi[:, 0:16], in_=colscale[:, 32:48])
    edge(p.vector.tensor_tensor(out=colscale[:, 0:16], in0=swf[:, 0:16],
                                in1=sg[:, 0:16], op=ALU.add))
    p.vector.wait_ge(s_ch.h, s_ch.n)
    edge(p.vector.tensor_copy(out=colscale[:, 16:32], in_=colscale[:, 0:16]))
    p.vector.wait_ge(s_ch.h, s_ch.n)
    i = p.vector.memset(colscale[:, 32:48], 1.0)
    i.then_inc(s_cs.h, 1)
    s_cs.n += 1
    return s_cs


def emit_l1ffn_pub(c: Ctx, pool: Pool, name, x_src, TLOCAL, wdr, bufs, outs):
    p = c.p
    NT = TLOCAL // 128
    XTB = bufs.get("xtb_cols", 512)
    s_ld = c.sem(f"{name}_ld")

    s_ap = None
    t0 = 0
    i = 0
    while t0 < TLOCAL:
        tsz = min(XTB, TLOCAL - t0)
        for dc in range(DC):
            c.dma("sync", bufs["x_tb"][:, dc * XTB: dc * XTB + tsz],
                  x_src(dc, t0, tsz), sem=s_ld)
        for e in ("scalar", "tensor", "vector"):
            c.wait(e, s_ld)
        s_ap = emit_ln(c, pool, f"{name}_ln{i}", bufs["x_tb"], XTB, 0,
                       bufs["xn_sb"], TLOCAL, t0, tsz)
        c.wait("sync", s_ap)
        t0 += tsz
        i += 1

    s_cs = emit_l1_colscale(c, pool, name + "cs", bufs, (s_ap, s_ap.n))

    fused = bufs["fused_sb"]
    c.wait("vector", s_cs)

    def evac_fused(mc, t0, tsz, ps_ap, s_mm, s_ev):
        p.vector.wait_ge(s_mm.h, s_mm.n)
        p.vector.tensor_scalar(out=fused[:, mc * TLOCAL + t0: mc * TLOCAL + t0 + tsz],
                               in0=ps_ap[:, 0:tsz], scalar1=bufs["colscale"][:, mc:mc + 1],
                               scalar2=None, op0=ALU.mult).then_inc(s_ev.h, 1)
        s_ev.n += 1

    mm_f, ev_f = emit_proj(c, pool, f"{name}_fp", wdr["waph"], 48, DC,
                           bufs["xn_sb"], TLOCAL, 0, TLOCAL, evac_fused,
                           s_ap, bufs["wpb"])

    s_sel = c.sem(f"{name}_sel")
    s_pq = c.sem(f"{name}_pq")
    s_nrm = c.sem(f"{name}_nrm")
    s_nmm = c.sem(f"{name}_nmm")
    s_ch2 = c.sem(f"{name}_ch2")
    stage = bufs["stage"]
    nsq = bufs["nsq"]
    small = bufs["l1tmp"]
    c.wait("vector", ev_f)
    p.vector.memset(small[0:1, 48:49], 0.0)
    p.vector.memset(small[0:1, 49:50], 0.0)
    pi = 0
    for (sect, base_self, base_oth, out_dram, nidx) in (
            ("q", 0, 16, outs["qf"], 48), ("k", 16, 0, outs["kf"], 49)):
        for h in range(HEADS):
            st_buf = stage[pi % 2]
            if pi >= 2:
                c.wait("vector", s_pq, s_pq.base + 16 * (pi - 1))
            mycol = (base_self + h) * TLOCAL
            othcol = (base_oth + h) * TLOCAL
            swm = bcast_col(bufs["swi"][:, h:h + 1], TLOCAL)
            p.vector.select(out=st_buf[:, 0:TLOCAL], mask=swm,
                            on_true=fused[:, othcol:othcol + TLOCAL],
                            on_false=fused[:, mycol:mycol + TLOCAL]).then_inc(s_sel.h, 1)
            s_sel.n += 1
            for (t0, tsz) in chunks512(0, TLOCAL):
                p.scalar.wait_ge(s_sel.h, s_sel.n)
                if s_nmm.n > 0:
                    p.scalar.wait_ge(s_nmm.h, s_nmm.n)
                p.scalar.activation(out=nsq[:, 0:tsz], in_=st_buf[:, t0:t0 + tsz],
                                    func=AF.Square).then_inc(s_nrm.h, 1)
                s_nrm.n += 1
                p.tensor.wait_ge(s_nrm.h, s_nrm.n)
                p.tensor.matmul(pool.ps_s2[:, 0:tsz], pool.ones128[:, :],
                                nsq[:, 0:tsz], start=True, stop=True).then_inc(s_nmm.h, 1)
                s_nmm.n += 1
                p.vector.wait_ge(s_nmm.h, s_nmm.n)
                p.vector.reduce_max(out=small[0:1, 50:51], in_=pool.ps_s2[0:1, 0:tsz],
                                    axis=mybir.AxisListType.X).then_inc(s_ch2.h, 1)
                s_ch2.n += 1
                p.vector.wait_ge(s_ch2.h, s_ch2.n)
                p.vector.tensor_tensor(out=small[0:1, nidx:nidx + 1],
                                       in0=small[0:1, nidx:nidx + 1],
                                       in1=small[0:1, 50:51], op=ALU.max).then_inc(s_ch2.h, 1)
                s_ch2.n += 1
            c.wait("sync", s_sel)
            c.wait("sync", s_ch2)
            c.dma("sync", out_dram[:, h * TLOCAL:(h + 1) * TLOCAL],
                  st_buf[:, 0:TLOCAL], sem=s_pq)
            pi += 1
    c.wait("sync", s_ch2)
    c.dma("sync", outs["nrm"][0:1, 0:2], small[0:1, 48:50], sem=s_pq)

    s_v32 = c.sem(f"{name}_v32")
    s_tr = c.sem(f"{name}_tr")
    s_tv = c.sem(f"{name}_tv")
    s_pv = c.sem(f"{name}_pv")
    vt32 = bufs["vt32"]
    c.wait("vector", s_pq)
    for jt in range(NT):
        st_buf = stage[jt % 2]
        if jt >= 2:
            c.wait("vector", s_pv, s_pv.base + 16 * (jt - 1))
        for h in range(HEADS):
            tau = s_tr.n - s_tr.base
            p.scalar.wait_ge(ev_f.h, ev_f.n)
            if tau >= 2:
                p.scalar.wait_ge(s_tr.h, s_tr.base + tau - 1)
            p.scalar.copy(out=vt32[tau % 2][:, :],
                          in_=fused[:, (32 + h) * TLOCAL + jt * 128: (32 + h) * TLOCAL + jt * 128 + 128]).then_inc(s_v32.h, 1)
            s_v32.n += 1
            bank = pool.ps_b[tau % 2]
            p.tensor.wait_ge(s_v32.h, s_v32.n)
            if tau >= 2:
                p.tensor.wait_ge(s_tv.h, s_tv.base + tau - 1)
            i = p.tensor.transpose(bank[:, 0:128], vt32[tau % 2][:, :],
                                   bufs["masks"]["ident"][:, :])
            i.then_inc(s_tr.h, 1)
            s_tr.n += 1
            p.vector.wait_ge(s_tr.h, s_tr.n)
            p.vector.tensor_copy(out=st_buf[:, h * 128:(h + 1) * 128],
                                 in_=bank[:, 0:128]).then_inc(s_tv.h, 1)
            s_tv.n += 1
        c.wait("sync", s_tv)
        c.dma("sync", outs["vtm"][:, jt * 2048:(jt + 1) * 2048],
              st_buf[:, 0:2048], sem=s_pv)
    return s_pv, s_pq


def emit_l1ffn_diagmasks(c: Ctx, name, dm=None):
    p = c.p
    if dm is None:
        dm = c.sbuf(f"{name}_dm", [128, 4 * 512], BF16)
    s_m = c.sem(f"{name}_dmc")
    p.gpsimd.memset(dm[:], 0.0)
    for pa in range(4):
        i = p.gpsimd.affine_select(
            out=dm[:, pa * 512:(pa + 1) * 512], in_=dm[:, pa * 512:(pa + 1) * 512],
            compare_op=ALU.is_ge, fill=NEG, base=-128 * pa,
            pattern=[[1, 512]], channel_multiplier=-1)
    i.then_inc(s_m.h, 1)
    s_m.n += 1
    return dm, s_m


def emit_l1ffn_flash(c: Ctx, pool: Pool, name, TLOCAL, wdr, bufs, qs, nbands):
    p = c.p
    s_ch = c.sem(f"{name}_ch")

    def vedge(i):
        i.then_inc(s_ch.h, 1)
        s_ch.n += 1
        return s_ch.n

    qf = bufs["qf_sb"]

    def dacc_sl(idx, dacc=bufs["dacc"]):
        b = 32 * (idx // 6)
        sl = idx % 6
        return dacc[b:b + 1, sl * 512:(sl + 1) * 512]
    bufs["dacc_sl"] = dacc_sl
    kband, vband = bufs["kband"], bufs["vband"]
    pt_sb = bufs["pt_sb"]
    sm_sb = bufs["sm_sb"]
    oacc, dacc = bufs["oacc"], bufs["dacc"]
    dmasks, s_dm = bufs["dmasks"]
    bmat = bufs["bmat"]
    s_kv = c.sem(f"{name}_kv")
    s_smm = c.sem(f"{name}_smm")
    s_msk = c.sem(f"{name}_msk")
    s_exp = c.sem(f"{name}_exp")
    s_ptv = c.sem(f"{name}_ptv")
    s_oev = c.sem(f"{name}_oev")
    s_dev = c.sem(f"{name}_dev")

    if not bufs.get("bias_built", False):
        bufs["bias_built"] = True
        nrm = bufs["nrm_sb"]
        s_b = c.sem(f"{name}_b")
        c.wait("vector", bufs["s_nrm_in"])
        c.wait("scalar", bufs["s_nrm_in"])
        kstrided = bass.AP(nrm.tensor if isinstance(nrm, bass.AP) else nrm, 1,
                           [[20, 1], [2, 8]])
        vedge(p.vector.reduce_max(out=nrm[0:1, 18:19], in_=kstrided,
                                  axis=mybir.AxisListType.X))
        p.vector.wait_ge(s_ch.h, s_ch.n)
        vedge(p.vector.tensor_tensor(out=nrm[0:1, 10:11], in0=nrm[0:1, 18:19],
                                     in1=nrm[0:1, 16:17], op=ALU.mult))
        p.scalar.wait_ge(s_ch.h, s_ch.n)
        vedge(p.scalar.activation(out=nrm[0:1, 10:11], in_=nrm[0:1, 10:11], func=AF.Sqrt))
        p.scalar.wait_ge(s_ch.h, s_ch.n)
        vedge(p.scalar.mul(out=nrm[0:1, 10:11], in_=nrm[0:1, 10:11], mul=-L1_SCALE))
        p.tensor.wait_ge(s_ch.h, s_ch.n)
        i = p.tensor.matmul(pool.ps_s2[:, 0:1], pool.ones_row[:, :], nrm[0:1, 10:11],
                            start=True, stop=True)
        i.then_inc(s_b.h, 1)
        s_b.n += 1
        p.vector.wait_ge(s_b.h, s_b.n)
        c.wait("vector", bufs["s_tm_in"])
        vedge(p.vector.tensor_copy(out=bufs["bcol"][:, 0:1], in_=pool.ps_s2[:, 0:1]))
        p.vector.wait_ge(s_ch.h, s_ch.n)
        vedge(p.vector.tensor_scalar(out=bmat[:, 0:128], in0=bufs["tmask_sb"][:, 0:128],
                                     scalar1=bufs["bcol"][:, 0:1], scalar2=None,
                                     op0=ALU.add))
        bufs["bias_done"] = s_ch.n
    sbd = bufs["bias_done"]

    for e in ("tensor", "vector", "scalar"):
        c.wait(e, bufs["s_qf"])
        p.eng(e).wait_ge(s_ch.h, sbd)
        c.wait(e, s_dm)
    p.vector.memset(oacc[:, :], 0.0)
    i = p.vector.memset(dacc[:, :], 0.0)
    vedge(i)
    s_zero = s_ch.n
    for e in ("tensor", "scalar"):
        p.eng(e).wait_ge(s_ch.h, s_zero)

    seq = [("l", 0, qs)] + [("g", b // 2, b % 2) for b in range(2 * nbands)]
    nsb = len(seq)
    exp_base = s_exp.n
    ptv_after_sb = {}
    oev_after_gi = {}
    groups = []

    def sub_k(base_ap, half):
        return bass.AP(base_ap.tensor, base_ap.offset + half * 512,
                       [list(base_ap.ap[0]), [1024, 16], [1, 512]])

    SB_BANKS = (pool.ps_a[0], pool.ps_a[1], pool.ps_c[0], pool.ps_c[1])

    def emit_S_batch(gi):
        sb, h, local, buf, kv_gate = groups[gi]
        for jt in range(4):
            u = gi * 4 + jt
            sbank = SB_BANKS[jt]
            ktile = kband[buf][:, h * 512 + jt * 128: h * 512 + jt * 128 + 128]
            p.tensor.wait_ge(s_kv.h, kv_gate)
            if gi >= 1:
                p.tensor.wait_ge(s_exp.h, exp_base + (gi - 1) * 4 + jt + 1)
            elif jt == 3 and "psc1_gate" in bufs:
                p.tensor.wait_ge(*bufs["psc1_gate"])
            i = p.tensor.matmul(sbank[:, 0:512], ktile,
                                qf[:, h * 512: (h + 1) * 512],
                                start=True, stop=True)
            i.then_inc(s_smm.h, 1)
            s_smm.n += 1

    def emit_exp_batch(gi):
        sb, h, local, buf, kv_gate = groups[gi]
        ptg = gi % 2
        for jt in range(4):
            u = gi * 4 + jt
            sbank = SB_BANKS[jt]
            ptslot = pt_sb[:, (ptg * 4 + jt) * 512: (ptg * 4 + jt) * 512 + 512]
            if local:
                p.vector.wait_ge(s_smm.h, s_smm.n)
                p.vector.wait_ge(s_exp.h, s_exp.n)
                p.vector.tensor_tensor(out=sm_sb[:, 0:512],
                                       in0=sbank[:, 0:512],
                                       in1=dmasks[:, jt * 512:(jt + 1) * 512],
                                       op=ALU.add).then_inc(s_msk.h, 1)
                s_msk.n += 1
                exp_in = sm_sb[:, 0:512]
                bias_ap = bufs["bcol"][:, 0:1]
                p.scalar.wait_ge(s_msk.h, s_msk.n)
            else:
                g = (sb - 1) * 4 + jt
                exp_in = sbank[:, 0:512]
                bias_ap = bmat[:, qs * 64 + g: qs * 64 + g + 1]
                p.scalar.wait_ge(s_smm.h, exp_base_smm + u + 1)
            if jt == 0 and gi >= 2:
                p.scalar.wait_ge(s_ptv.h, ptv_after_gi[gi - 2])
            i = p.scalar.activation(out=ptslot, in_=exp_in,
                                    func=AF.Exp, bias=bias_ap, scale=L1_SCALE)
            i.then_inc(s_exp.h, 1)
            s_exp.n += 1

    ptv_after_gi = {}

    def emit_denpv_batch(gi):
        sb, h, local, buf, kv_gate = groups[gi]
        ptg = gi % 2
        obank = pool.ps_b[gi % 2]
        dbank = (pool.ps_s1, pool.ps_s2)[gi % 2]
        if gi >= 2:
            p.tensor.wait_ge(s_oev.h, oev_after_gi[gi - 2])
        elif "psb_gate" in bufs:
            p.tensor.wait_ge(*bufs["psb_gate"])
        for jt in range(4):
            u = gi * 4 + jt
            vtile = vband[buf][:, jt * 2048 + h * 128: jt * 2048 + h * 128 + 128]
            ptslot = pt_sb[:, (ptg * 4 + jt) * 512: (ptg * 4 + jt) * 512 + 512]
            p.tensor.wait_ge(s_exp.h, exp_base + u + 1)
            p.tensor.matmul(dbank[:, 0:512], pool.ones128_bf[:, :],
                            ptslot, start=(jt == 0), stop=(jt == 3))
            i = p.tensor.matmul(obank[:, 0:512], vtile,
                                ptslot, start=(jt == 0), stop=(jt == 3))
            i.then_inc(s_ptv.h, 1)
            s_ptv.n += 1
        ptv_after_gi[gi] = s_ptv.n
        p.vector.wait_ge(s_ptv.h, s_ptv.n)
        p.vector.scalar_tensor_tensor(
            out=oacc[:, h * 512: (h + 1) * 512],
            in0=obank[:, 0:512], scalar=1.0,
            in1=oacc[:, h * 512: (h + 1) * 512],
            op0=ALU.mult, op1=ALU.add).then_inc(s_oev.h, 1)
        s_oev.n += 1
        p.vector.scalar_tensor_tensor(
            out=dacc_sl(h), in0=dbank[0:1, 0:512], scalar=1.0,
            in1=dacc_sl(h), op0=ALU.mult, op1=ALU.add).then_inc(s_oev.h, 1)
        s_oev.n += 1
        oev_after_gi[gi] = s_oev.n

    exp_base_smm = s_smm.n
    gidx = 0
    for sb, (kind, blk, half) in enumerate(seq):
        buf = sb % 2
        if sb >= 2:
            c.p.sync.wait_ge(s_ptv.h, ptv_after_sb[sb - 2])
        if kind == "g":
            c.dma("sync", kband[buf][:, :], sub_k(wdr["kg"][blk], half), sem=s_kv)
            c.dma("sync", vband[buf][:, :],
                  bass.AP(wdr["vg"][blk].tensor, wdr["vg"][blk].offset + half * 8192,
                          [list(wdr["vg"][blk].ap[0]), [1, 8192]]), sem=s_kv)
        else:
            c.dma("sync", kband[buf][:, :], sub_k(wdr["kfl"], half), sem=s_kv)
            c.dma("sync", vband[buf][:, :],
                  bass.AP(wdr["vtl"].tensor, wdr["vtl"].offset + half * 8192,
                          [list(wdr["vtl"].ap[0]), [1, 8192]]), sem=s_kv)
        kv_gate = s_kv.n
        local = (kind == "l")
        for h in range(HEADS):
            groups.append((sb, h, local, buf, kv_gate))
            gi = gidx
            emit_S_batch(gi)
            emit_exp_batch(gi)
            if gi >= 1:
                emit_denpv_batch(gi - 1)
            gidx += 1
        ptv_after_sb[sb] = s_ptv.n + 4
    emit_denpv_batch(gidx - 1)
    
    st = dict(s_oev=s_oev, s_dev=s_oev, s_ptv=s_ptv, s_ch=s_ch, vedge=vedge)
    return st


def emit_l1ffn_norm_wp(c: Ctx, pool: Pool, name, y_src, y_dst, TLOCAL, wdr,
                       bufs, st, qs, wo_gate=None, half=0.5):
    p = c.p
    s_oev, s_dev, s_ptv = st["s_oev"], st["s_dev"], st["s_ptv"]
    s_ch, vedge = st["s_ch"], st["vedge"]
    oacc = bufs["oacc"]
    dacc_sl = bufs["dacc_sl"]
    s_nb = c.sem(f"{name}_nb")
    s_ymul = c.sem(f"{name}_ymul")
    s_xld = c.sem(f"{name}_xld")
    s_st = c.sem(f"{name}_st")
    y_sb = bufs["y_sb"]
    rec_row = bufs["rec_row"]
    q0 = qs * 512
    for h in range(HEADS):
        p.vector.wait_ge(s_oev.h, s_oev.n)
        p.vector.wait_ge(s_dev.h, s_dev.n)
        if h > 0:
            p.vector.wait_ge(s_ymul.h, s_ymul.n)
        i = p.vector.reciprocal(out=rec_row[0:1, 0:512], in_=dacc_sl(h))
        vedge(i)
        p.tensor.wait_ge(s_ch.h, s_ch.n)
        if h > 0:
            p.tensor.wait_ge(s_ymul.h, s_ymul.n)
        i = p.tensor.matmul(pool.ps_c[1][:, 0:512], pool.ones_row[:, :],
                            rec_row[0:1, 0:512], start=True, stop=True)
        i.then_inc(s_nb.h, 1)
        s_nb.n += 1
        p.vector.wait_ge(s_nb.h, s_nb.n)
        p.vector.tensor_tensor(out=y_sb[:, h * 512: (h + 1) * 512],
                               in0=oacc[:, h * 512: (h + 1) * 512],
                               in1=pool.ps_c[1][:, 0:512],
                               op=ALU.mult).then_inc(s_ymul.h, 1)
        s_ymul.n += 1
    c.wait("gpsimd", s_ptv)
    for dc in range(DC):
        c.dma("gpsimd", bufs["x_tb"][:, dc * 512: dc * 512 + 512],
              y_src(dc, q0, 512), sem=s_xld)

    def evac_wp(mc, t0, tsz, ps_ap, s_mm, s_ev):
        p.vector.wait_ge(s_mm.h, s_mm.n)
        p.vector.wait_ge(s_xld.h, s_xld.n)
        p.vector.scalar_tensor_tensor(
            out=bufs["x_tb"][:, mc * 512: mc * 512 + tsz],
            in0=ps_ap[:, 0:tsz], scalar=half,
            in1=bufs["x_tb"][:, mc * 512: mc * 512 + tsz],
            op0=ALU.mult, op1=ALU.add).then_inc(s_ev.h, 1)
        s_ev.n += 1

    mm_wp, ev_wp = emit_proj(c, pool, f"{name}_wpj", wdr["wph"], 16, 16,
                             y_sb, 512, 0, 512, evac_wp,
                             s_ymul, bufs["wpb"], wgate=wo_gate,
                             psum_pair=pool.ps_b)
    bufs["psb_gate"] = (ev_wp.h, ev_wp.n)
    bufs["psc1_gate"] = (s_ymul.h, s_ymul.n)
    c.wait("gpsimd", ev_wp)
    for dc in range(DC):
        c.dma("gpsimd", y_dst(dc, q0, 512),
              bufs["x_tb"][:, dc * 512: dc * 512 + 512], sem=s_st)
    c.p.vector.wait_ge(mm_wp.h, mm_wp.n)
    return s_st, (mm_wp, mm_wp.n)


def build_att0_launch(TLOCAL=1024):
    EXTW = 128 + TLOCAL + 128
    NQT = TLOCAL // 128
    nc = bass.Bass("TRN2", target_bir_lowering=False, debug=False)
    xe_in = nc.dram_tensor("xe", [D, EXTW], F32, kind="ExternalInput").ap()
    mown_in = nc.dram_tensor("mown", [128, 128], F32, kind="ExternalInput").ap()
    mprev_in = nc.dram_tensor("mprev", [128, NQT * 128], F32, kind="ExternalInput").ap()
    mpers_in = nc.dram_tensor("mpers", [128, NQT * 16], F32, kind="ExternalInput").ap()
    wq_d = nc.dram_tensor("wqh", [16, 128, DC * 128], BF16, kind="ExternalInput").ap()
    wkv_d = nc.dram_tensor("wkvh", [4, 128, DC * 128], BF16, kind="ExternalInput").ap()
    wk_d = nc.dram_tensor("wkh", [16, 128, 4 * 128], BF16, kind="ExternalInput").ap()
    wv_d = nc.dram_tensor("wvh", [16, 128, 4 * 128], BF16, kind="ExternalInput").ap()
    wo_d = nc.dram_tensor("woh", [16, 128, DC * 128], BF16, kind="ExternalInput").ap()
    y_out = nc.dram_tensor("y", [D, TLOCAL], F32, kind="ExternalOutput").ap()

    with ExitStack() as stack:
        c = Ctx(nc, stack)
        pool = Pool(c)
        u1 = c.sbuf("u1", [128, 16 * EXTW], BF16)
        qy = c.sbuf("qy", [128, 16 * TLOCAL], BF16)
        k_sb = c.sbuf("k_sb", [128, 16 * EXTW], BF16)
        lat = c.sbuf("lat", [128, 4 * EXTW], BF16)
        x_tb = c.sbuf("x_tb", [128, 16 * 512], F32)
        vstg = [c.sbuf(f"vstg{i}", [128, 512], F32) for i in range(2)]
        p_sb = c.sbuf("p_sb", [128, 544], F32)
        sm_sb = c.sbuf("sm_sb", [128, 544], F32)
        pn_sb = c.sbuf("pn_sb", [128, 544], F32)
        pt_sb = c.sbuf("pt_sb", [128, 768], BF16)
        wpb = [c.sbuf(f"wpb{i}", [128, 16 * 128], BF16) for i in range(2)]
        bcol = c.sbuf("bcol", [128, 1], F32)
        den = [c.sbuf(f"den{i}", [128, 1], F32) for i in range(2)]
        rcol = [c.sbuf(f"rcol{i}", [128, 1], F32) for i in range(2)]
        small = c.sbuf("small", [1, 64], F32)
        mown_sb = c.sbuf("mown_sb", [128, 128], F32)
        mprev_sb = c.sbuf("mprev_sb", [128, NQT * 128], F32)
        mpers_sb = c.sbuf("mpers_sb", [128, NQT * 16], F32)

        pool.emit_consts()
        masks = emit_ident(c, "msk")
        s_mld = c.sem("mld")
        c.dma("sync", mown_sb[:], mown_in[:, :], sem=s_mld)
        c.dma("sync", mprev_sb[:], mprev_in[:, :], sem=s_mld)
        c.dma("sync", mpers_sb[:], mpers_in[:, :], sem=s_mld)
        for e in ("tensor", "vector", "scalar"):
            c.wait(e, masks["s_msk"])
            c.wait(e, s_mld)

        bufs = dict(u1=u1, qy_sb=qy, k_sb=k_sb, lat_sb=lat, x_tb=x_tb,
                    vstg=vstg, p_sb=p_sb, pn_sb=pn_sb, pt_sb=pt_sb, wpb=wpb,
                    masks=masks, bcol=bcol, den=den, rcol=rcol, small=small,
                    sm_sb=sm_sb,
                    mown_sb=mown_sb, mprev_sb=mprev_sb, mpers_sb=mpers_sb)
        wdr = dict(wqh=[wq_d[i] for i in range(16)],
                   wkvh=[wkv_d[i] for i in range(4)],
                   wkh=[wk_d[i] for i in range(16)],
                   wvh=[wv_d[i] for i in range(16)],
                   woh=[wo_d[i] for i in range(16)])

        def x_src(dc, t0, tsz):
            return xe_in[dc * 128:(dc + 1) * 128, t0:t0 + tsz]

        def y_src(dc, t0, tsz):
            return xe_in[dc * 128:(dc + 1) * 128, 128 + t0:128 + t0 + tsz]

        def y_dst(dc, t0, tsz):
            return y_out[dc * 128:(dc + 1) * 128, t0:t0 + tsz]

        s_st = emit_att0(c, pool, "a0", x_src, y_src, y_dst, TLOCAL, wdr, bufs)
        c.wait("gpsimd", s_st)
        with nc.Block() as block:
            c.p.replay(block)
    return nc


def prep_att0_weights(Wq, Wkv, Wk, Wv, Wo):
    return (host_tile_lhsT(np.asarray(Wq, np.float32)),
            host_tile_lhsT(np.asarray(Wkv, np.float32)),
            host_tile_lhsT(np.asarray(Wk, np.float32)),
            host_tile_lhsT(np.asarray(Wv, np.float32)),
            host_tile_lhsT(np.asarray(Wo, np.float32)))


def build_rwkv_launch(TLOCAL=1024):
    EXTW = 264 + TLOCAL
    SCANW = 256 + TLOCAL
    nc = bass.Bass("TRN2", target_bir_lowering=False, debug=False)
    xe_in = nc.dram_tensor("xe", [D, EXTW], F32, kind="ExternalInput").ap()
    wk_d = nc.dram_tensor("wkh", [16, 128, DC * 128], BF16, kind="ExternalInput").ap()
    wv_d = nc.dram_tensor("wvh", [16, 128, DC * 128], BF16, kind="ExternalInput").ap()
    wr_d = nc.dram_tensor("wrh", [16, 128, DC * 128], BF16, kind="ExternalInput").ap()
    td_in = nc.dram_tensor("td", [128, 16], F32, kind="ExternalInput").ap()
    hm_in = nc.dram_tensor("hm", [128, SCANW], F32, kind="ExternalInput").ap()
    y_out = nc.dram_tensor("y", [D, TLOCAL], F32, kind="ExternalOutput").ap()

    with ExitStack() as stack:
        c = Ctx(nc, stack)
        pool = Pool(c)
        ln_sb = c.sbuf("ln_sb", [128, DC * EXTW], BF16)
        x_tb = c.sbuf("x_tb", [128, DC * 512], F32)
        wpb3 = [c.sbuf(f"wpb3_{i}", [128, DC * 128], BF16) for i in range(3)]
        ek = c.sbuf("ek", [128, SCANW], F32)
        hmk = c.sbuf("hmk", [128, SCANW], F32)
        v_b = c.sbuf("v_b", [128, SCANW], F32)
        a_b = c.sbuf("a_b", [128, SCANW], F32)
        num = c.sbuf("num", [128, SCANW], F32)
        den = c.sbuf("den", [128, SCANW], F32)
        rec = c.sbuf("rec", [128, TLOCAL], F32)
        wkv = c.sbuf("wkv", [128, TLOCAL], F32)
        outb = c.sbuf("outb", [128, TLOCAL], F32)
        r_sb = c.sbuf("r_sb", [128, TLOCAL], F32)
        hm_sb = c.sbuf("hm_sb", [128, SCANW], F32)
        td_sb = c.sbuf("td_sb", [128, 16], F32)
        e1_sb = c.sbuf("e1_sb", [128, 16], F32)
        w_sb = c.sbuf("w_sb", [128, 16], F32)

        pool.emit_consts()
        s_ld0 = c.sem("ld0")
        c.dma("sync", td_sb[:], td_in[:, :], sem=s_ld0)
        c.dma("sync", hm_sb[:], hm_in[:, :], sem=s_ld0)
        for e in ("scalar", "vector"):
            c.wait(e, s_ld0)

        bufs = dict(ln_sb=ln_sb, x_tb=x_tb, wpb3=wpb3, ek=ek, hmk=hmk, v=v_b,
                    a=a_b, num=num, den=den, rec=rec, wkv=wkv, outb=outb,
                    r_sb=r_sb, hm_sb=hm_sb, td_sb=td_sb, e1_sb=e1_sb, w_sb=w_sb)
        wdr = dict(wkh=[wk_d[i] for i in range(16)],
                   wvh=[wv_d[i] for i in range(16)],
                   wrh=[wr_d[i] for i in range(16)])

        def x_src(dc, t0, tsz):
            return xe_in[dc * 128:(dc + 1) * 128, t0:t0 + tsz]

        def y_src(dc, t0, tsz):
            return xe_in[dc * 128:(dc + 1) * 128, 264 + t0:264 + t0 + tsz]

        def y_dst(dc, t0, tsz):
            return y_out[dc * 128:(dc + 1) * 128, t0:t0 + tsz]

        s_st = emit_rwkv(c, pool, "rw", x_src, y_src, y_dst, TLOCAL, wdr, bufs)
        c.wait("gpsimd", s_st)
        with nc.Block() as block:
            c.p.replay(block)
    return nc


def prep_rwkv_weights(Wk, Wv, Wr, td):
    return (host_tile_lhsT(np.asarray(Wk, np.float32).T),
            host_tile_lhsT(np.asarray(Wv, np.float32).T),
            host_tile_lhsT(np.asarray(Wr, np.float32).T),
            f32c(np.asarray(td).reshape(16, 128).T))


def build_l1pub_launch(TLOCAL=1024, dbg=False):
    NT = TLOCAL // 128
    nc = bass.Bass("TRN2", target_bir_lowering=False, debug=False)
    xe_in = nc.dram_tensor("xe", [D, TLOCAL], F32, kind="ExternalInput").ap()
    wa_d = nc.dram_tensor("waph", [48, 128, DC * 128], BF16, kind="ExternalInput").ap()
    ws_in = nc.dram_tensor("wstd", [1, 16], F32, kind="ExternalInput").ap()
    wr_in = nc.dram_tensor("wrec", [1, 16], F32, kind="ExternalInput").ap()
    qf_o = nc.dram_tensor("qf", [128, 16 * TLOCAL], BF16, kind="ExternalOutput").ap()
    kf_o = nc.dram_tensor("kf", [128, 16 * TLOCAL], BF16, kind="ExternalOutput").ap()
    vtm_o = nc.dram_tensor("vtm", [128, NT * 2048], BF16, kind="ExternalOutput").ap()
    nrm_o = nc.dram_tensor("nrm", [1, 2], F32, kind="ExternalOutput").ap()

    with ExitStack() as stack:
        c = Ctx(nc, stack)
        pool = Pool(c)
        fused = c.sbuf("fused", [128, 48 * TLOCAL], BF16)
        xn_sb = c.sbuf("xn_sb", [128, DC * TLOCAL], BF16)
        x_tb = c.sbuf("x_tb", [128, DC * 256], F32)
        stage = [c.sbuf(f"stage{i}", [128, 2048], BF16) for i in range(2)]
        wpb = [c.sbuf(f"wpb{i}", [128, DC * 128], BF16) for i in range(2)]
        vt32 = [c.sbuf(f"vt32_{i}", [128, 128], F32) for i in range(2)]
        colscale = c.sbuf("colscale", [128, 48], F32)
        swi = c.sbuf("swi", [128, 16], I32)
        swf = c.sbuf("swf", [128, 16], F32)
        sg = c.sbuf("sg", [128, 16], F32)
        l1tmp = c.sbuf("l1tmp", [1, 64], F32)
        wstd_row = c.sbuf("wstd_row", [1, 16], F32)
        wrec_row = c.sbuf("wrec_row", [1, 16], F32)

        pool.emit_consts()
        masks = emit_ident(c, "id")
        for e in ("tensor", "vector", "scalar"):
            c.wait(e, masks["s_msk"])
        s_ws = c.sem("wsld")
        c.dma("sync", wstd_row[:], ws_in[:, :], sem=s_ws)
        c.dma("sync", wrec_row[:], wr_in[:, :], sem=s_ws)

        bufs = dict(fused_sb=fused, xn_sb=xn_sb, x_tb=x_tb, stage=stage,
                    wpb=wpb, vt32=vt32, colscale=colscale, swi=swi, swf=swf,
                    sg=sg, l1tmp=l1tmp, wstd_row=wstd_row, wrec_row=wrec_row,
                    s_wsin=s_ws, masks=masks, nsq=pool.sq, xtb_cols=256)
        wdr = dict(waph=[wa_d[i] for i in range(48)])
        outs = dict(qf=qf_o, kf=kf_o, vtm=vtm_o, nrm=nrm_o)

        def x_src(dc, t0, tsz):
            return xe_in[dc * 128:(dc + 1) * 128, t0:t0 + tsz]

        s_pv, s_pq = emit_l1ffn_pub(c, pool, "pub", x_src, TLOCAL, wdr, bufs, outs)
        c.wait("sync", s_pv)
        c.wait("sync", s_pq)
        if dbg:
            d_cs = nc.dram_tensor("d_cs", [128, 48], F32, kind="ExternalOutput").ap()
            d_swi = nc.dram_tensor("d_swi", [128, 16], I32, kind="ExternalOutput").ap()
            s_dbg = c.sem("dbg")
            c.dma("sync", d_cs[:, :], colscale[:], sem=s_dbg)
            c.dma("sync", d_swi[:, :], swi[:], sem=s_dbg)
            c.wait("sync", s_dbg)
        with nc.Block() as block:
            c.p.replay(block)
    return nc


def build_l1flash_launch(TLOCAL=1024, slot_bands=(8, 8)):
    nc = bass.Bass("TRN2", target_bir_lowering=False, debug=False)
    xe_in = nc.dram_tensor("xe", [D, TLOCAL], F32, kind="ExternalInput").ap()
    qf_in = nc.dram_tensor("qf", [128, 16 * TLOCAL], BF16, kind="ExternalInput").ap()
    kg_in = nc.dram_tensor("kg", [8, 128, 16 * 1024], BF16, kind="ExternalInput").ap()
    vg_in = nc.dram_tensor("vg", [8, 128, 8 * 2048], BF16, kind="ExternalInput").ap()
    kfl_in = nc.dram_tensor("kfl", [128, 16 * 1024], BF16, kind="ExternalInput").ap()
    vtl_in = nc.dram_tensor("vtl", [128, 8 * 2048], BF16, kind="ExternalInput").ap()
    nrm_in = nc.dram_tensor("nrmv", [1, 20], F32, kind="ExternalInput").ap()
    tm_in = nc.dram_tensor("tmask", [128, 128], F32, kind="ExternalInput").ap()
    wp_d = nc.dram_tensor("wph", [16, 128, 16 * 128], BF16, kind="ExternalInput").ap()
    y_out = nc.dram_tensor("y", [D, TLOCAL], F32, kind="ExternalOutput").ap()

    with ExitStack() as stack:
        c = Ctx(nc, stack)
        pool = Pool(c, ln=False)
        qf_sb = c.sbuf("qf_sb", [128, 16 * 512], BF16)
        oacc = c.sbuf("oacc", [128, 16 * 512], F32)
        dacc = c.sbuf("dacc", [128, 6 * 512], F32)
        kband = [c.sbuf(f"kband{i}", [128, 16 * 512], BF16) for i in range(2)]
        vband = [c.sbuf(f"vband{i}", [128, 4 * 2048], BF16) for i in range(2)]
        pt_sb = c.sbuf("pt_sb", [128, 8 * 512], BF16)
        sm_sb = c.sbuf("sm_sb", [128, 512], F32)
        bmat = c.sbuf("bmat", [128, 130], F32)
        tmask_sb = c.sbuf("tmask_sb", [128, 128], F32)
        bcol = c.sbuf("bcolf", [128, 1], F32)
        nrm_sb = c.sbuf("nrm_sb", [1, 20], F32)
        rec_row = c.sbuf("rec_row", [1, 512], F32)
        x_tb = c.sbuf("x_tbf", [128, DC * 512], F32)
        wpb = [c.sbuf(f"wpbf{i}", [128, DC * 128], BF16) for i in range(2)]

        pool.emit_consts()
        dmasks, s_dm = emit_l1ffn_diagmasks(c, "dm")
        s_in = c.sem("flin")
        c.dma("sync", nrm_sb[:], nrm_in[:, :], sem=s_in)
        c.dma("sync", tmask_sb[:], tm_in[:, :], sem=s_in)

        bufs = dict(qf_sb=qf_sb, kband=kband, vband=vband, pt_sb=pt_sb,
                    bmat=bmat, tmask_sb=tmask_sb, bcol=bcol, nrm_sb=nrm_sb,
                    sm_sb=sm_sb, oacc=oacc, dacc=dacc, dmasks=(dmasks, s_dm),
                    y_sb=qf_sb, x_tb=x_tb, wpb=wpb, rec_row=rec_row,
                    s_nrm_in=s_in, s_tm_in=s_in)
        wdr = dict(kg=[kg_in[i][:, :] for i in range(8)],
                   vg=[vg_in[i][:, :] for i in range(8)],
                   kfl=kfl_in[:, :], vtl=vtl_in[:, :],
                   wph=[wp_d[i] for i in range(16)])

        def y_src(dc, t0, tsz):
            return xe_in[dc * 128:(dc + 1) * 128, t0:t0 + tsz]

        def y_dst(dc, t0, tsz):
            return y_out[dc * 128:(dc + 1) * 128, t0:t0 + tsz]

        wo_gate = None
        s_st = None
        for qs in range(TLOCAL // 512):
            s_qf = c.sem("qfld")
            if wo_gate is not None:
                c.p.sync.wait_ge(wo_gate[0].h, wo_gate[1])
            for h in range(HEADS):
                c.dma("sync", qf_sb[:, h * 512:(h + 1) * 512],
                      qf_in[:, h * TLOCAL + qs * 512: h * TLOCAL + qs * 512 + 512],
                      sem=s_qf)
            bufs["s_qf"] = s_qf
            st = emit_l1ffn_flash(c, pool, "fl", TLOCAL, wdr, bufs, qs,
                                  slot_bands[qs])
            s_st, wo_gate = emit_l1ffn_norm_wp(c, pool, "wp", y_src, y_dst,
                                               TLOCAL, wdr, bufs, st, qs, wo_gate)
        c.wait("gpsimd", s_st)
        with nc.Block() as block:
            c.p.replay(block)
    return nc


def prep_l1ffn_weights(Wa, Wp):
    Ds, R = 120, 8
    q_std_end = HEADS * Ds
    k_std_end = 2 * q_std_end
    v_end = k_std_end + D
    q_low_end = v_end + HEADS * R
    perm = []
    for h in range(HEADS):
        perm += list(range(h * Ds, (h + 1) * Ds))
        perm += list(range(v_end + h * R, v_end + (h + 1) * R))
    for h in range(HEADS):
        perm += list(range(q_std_end + h * Ds, q_std_end + (h + 1) * Ds))
        perm += list(range(q_low_end + h * R, q_low_end + (h + 1) * R))
    perm += list(range(k_std_end, v_end))
    WaP = np.ascontiguousarray(np.asarray(Wa, np.float32)[:, perm])
    return host_tile_lhsT(WaP), host_tile_lhsT(np.asarray(Wp, np.float32))


def make_l1_tmask_host(my_chunks):
    m = np.zeros((128, 128), np.float32)
    for (qs, qc) in enumerate(my_chunks):
        for g in range(64):
            if not (g < 4 * qc):
                m[:, qs * 64 + g] = NEG
    return m


def build_fused_launch(TLOC=1024):
    EXTA = 128 + TLOC + 128
    RH = 264
    EXTR = RH + TLOC
    SCANW = 256 + TLOC
    NT = TLOC // 128
    nc = bass.Bass("TRN2", target_bir_lowering=False, debug=False)

    xe0 = nc.dram_tensor("xe0", [D, EXTA], F32, kind="ExternalInput").ap()
    w0h_d = nc.dram_tensor("w0h", [64, 128, DC * 128], BF16, kind="ExternalInput").ap()
    w1q_d = nc.dram_tensor("w1q", [64, 128, 16 * 128], BF16, kind="ExternalInput").ap()
    b0_d = nc.dram_tensor("b0c", [128, 64], F32, kind="ExternalInput").ap()
    b1_d = nc.dram_tensor("b1r", [1, D], F32, kind="ExternalInput").ap()
    wq_d = nc.dram_tensor("wqh", [16, 128, DC * 128], BF16, kind="ExternalInput").ap()
    wkv_d = nc.dram_tensor("wkvh", [4, 128, DC * 128], BF16, kind="ExternalInput").ap()
    wk_d = nc.dram_tensor("wkh", [16, 128, 4 * 128], BF16, kind="ExternalInput").ap()
    wv_d = nc.dram_tensor("wvh", [16, 128, 4 * 128], BF16, kind="ExternalInput").ap()
    wo_d = nc.dram_tensor("woh", [16, 128, DC * 128], BF16, kind="ExternalInput").ap()
    mown_in = nc.dram_tensor("mown", [128, 128], F32, kind="ExternalInput").ap()
    mprev_in = nc.dram_tensor("mprev", [128, NT * 128], F32, kind="ExternalInput").ap()
    mpers_in = nc.dram_tensor("mpers", [128, NT * 16], F32, kind="ExternalInput").ap()
    rk_d = nc.dram_tensor("rkh", [16, 128, DC * 128], BF16, kind="ExternalInput").ap()
    rv_d = nc.dram_tensor("rvh", [16, 128, DC * 128], BF16, kind="ExternalInput").ap()
    rr_d = nc.dram_tensor("rrh", [16, 128, DC * 128], BF16, kind="ExternalInput").ap()
    td_in = nc.dram_tensor("td", [128, 16], F32, kind="ExternalInput").ap()
    hm_in = nc.dram_tensor("hm", [128, SCANW], F32, kind="ExternalInput").ap()
    wa_d = nc.dram_tensor("waph", [48, 128, DC * 128], BF16, kind="ExternalInput").ap()
    wp_d = nc.dram_tensor("wph", [16, 128, 16 * 128], BF16, kind="ExternalInput").ap()
    ws_in = nc.dram_tensor("wstd", [1, 16], F32, kind="ExternalInput").ap()
    wr_in = nc.dram_tensor("wrec", [1, 16], F32, kind="ExternalInput").ap()
    tm_in = nc.dram_tensor("tmask", [128, 128], F32, kind="ExternalInput").ap()
    hsel_in = nc.dram_tensor("hsel", [128, 8], F32, kind="ExternalInput").ap()
    y_out = nc.dram_tensor("y", [D, TLOC], F32, kind="ExternalOutput").ap()

    xA = nc.dram_tensor("xA", [D, EXTA], F32, kind="Internal").ap()
    xB = nc.dram_tensor("xB", [D, TLOC], F32, kind="Internal").ap()
    xC = nc.dram_tensor("xC", [D, TLOC], F32, kind="Internal").ap()
    xD = nc.dram_tensor("xD", [D, TLOC], F32, kind="Internal").ap()
    xE = nc.dram_tensor("xE", [D, TLOC], F32, kind="Internal").ap()
    xh = nc.dram_tensor("xh", [D, RH], F32, kind="Internal").ap()
    pb_in_t = nc.dram_tensor("pb_in", [128, 32768], BF16, kind="Internal")
    pb_out_t = nc.dram_tensor("pb_out", [1024, 32768], BF16, kind="Internal")
    nr_in_t = nc.dram_tensor("nr_in", [1, 2], F32, kind="Internal")
    nr_out_t = nc.dram_tensor("nr_out", [8, 2], F32, kind="Internal")
    rh_in_t = nc.dram_tensor("rh_in", [128, DC * RH], F32, kind="Internal")
    rh_out_t = nc.dram_tensor("rh_out", [1024, DC * RH], F32, kind="Internal")
    qfA = nc.dram_tensor("qfA", [128, 16 * TLOC], BF16, kind="Internal").ap()
    pb_in, pb_out = pb_in_t.ap(), pb_out_t.ap()
    nr_in, nr_out = nr_in_t.ap(), nr_out_t.ap()
    rh_in, rh_out = rh_in_t.ap(), rh_out_t.ap()

    def dsl(ap, dc, t0, tsz):
        return ap[dc * 128:(dc + 1) * 128, t0:t0 + tsz]

    with ExitStack() as stack:
        c = Ctx(nc, stack)
        p = c.p
        pool = Pool(c)
        pool.emit_consts()
        idents = emit_ident(c, "id")

        mown_sb = c.sbuf("mown_sb", [128, 128], F32)
        mprev_sb = c.sbuf("mprev_sb", [128, NT * 128], F32)
        mpers_sb = c.sbuf("mpers_sb", [128, NT * 16], F32)
        td_sb = c.sbuf("td_sb", [128, 16], F32)
        tmask_sb = c.sbuf("tmask_sb", [128, 128], F32)
        hsel_sb = c.sbuf("hsel_sb", [128, 8], F32)
        wstd_row = c.sbuf("wstd_row", [1, 16], F32)
        wrec_row = c.sbuf("wrec_row", [1, 16], F32)
        colscale = c.sbuf("colscale", [128, 48], F32)
        swi = c.sbuf("swi", [128, 16], I32)
        swf = c.sbuf("swf", [128, 16], F32)
        sgb = c.sbuf("sgb", [128, 16], F32)
        l1tmp = c.sbuf("l1tmp", [1, 64], F32)

        s_cld = c.sem("cld")
        for (dst, srcap) in ((mown_sb, mown_in),
                             (mprev_sb, mprev_in), (mpers_sb, mpers_in),
                             (td_sb, td_in), (tmask_sb, tm_in),
                             (hsel_sb, hsel_in), (wstd_row, ws_in),
                             (wrec_row, wr_in)):
            c.dma("sync", dst[:], srcap[:, :], sem=s_cld)

        def barrier(*sems):
            for e in ENGINES:
                for s in sems:
                    c.wait(e, s)

        barrier(pool.s_cst, idents["s_msk"], s_cld)

        wdr_ffn = dict(w0h=[w0h_d[i] for i in range(64)],
                       w1q=[w1q_d[i] for i in range(64)])
        with ExitStack() as st1:
            e1 = st1.enter_context
            bufs1 = dict(
                x_tb=e1(nc.sbuf_tensor("p1_x", [128, DC * 512], F32)),
                xn_sb=e1(nc.sbuf_tensor("p1_xn", [128, DC * 512], BF16)),
                w0b=[e1(nc.sbuf_tensor(f"p1_w0{i}", [128, DC * 128], BF16)) for i in range(2)],
                w1b=[e1(nc.sbuf_tensor(f"p1_w1{i}", [128, 16 * 128], BF16)) for i in range(4)],
                h_sb=e1(nc.sbuf_tensor("p1_h", [128, 64 * 512], BF16)),
                bias0c=e1(nc.sbuf_tensor("p1_b0", [128, 64], F32)),
                bias1r=e1(nc.sbuf_tensor("p1_b1", [1, D], F32)))
            s_bld = c.sem("fbld")
            c.dma("sync", bufs1["bias0c"][:], b0_d[:, :], sem=s_bld)
            c.dma("sync", bufs1["bias1r"][:], b1_d[:, :], sem=s_bld)
            for e in ("scalar", "tensor"):
                c.wait(e, s_bld)
            s_st1 = emit_ffn0_dram(c, pool, "f",
                                   lambda dc, t0, tsz: dsl(xe0, dc, t0, tsz),
                                   lambda dc, t0, tsz: dsl(xA, dc, t0, tsz),
                                   EXTA, bufs1, wdr_ffn)

        wdr_att = dict(wqh=[wq_d[i] for i in range(16)],
                       wkvh=[wkv_d[i] for i in range(4)],
                       wkh=[wk_d[i] for i in range(16)],
                       wvh=[wv_d[i] for i in range(16)],
                       woh=[wo_d[i] for i in range(16)])
        with ExitStack() as st2:
            e2 = st2.enter_context
            bufs2 = dict(
                u1=e2(nc.sbuf_tensor("p2_u1", [128, 16 * EXTA], BF16)),
                qy_sb=e2(nc.sbuf_tensor("p2_qy", [128, 16 * TLOC], BF16)),
                k_sb=e2(nc.sbuf_tensor("p2_k", [128, 16 * EXTA], BF16)),
                lat_sb=e2(nc.sbuf_tensor("p2_lat", [128, 4 * EXTA], BF16)),
                x_tb=e2(nc.sbuf_tensor("p2_x", [128, DC * 512], F32)),
                vstg=[e2(nc.sbuf_tensor(f"p2_vs{i}", [128, 512], F32)) for i in range(2)],
                p_sb=e2(nc.sbuf_tensor("p2_p", [128, 544], F32)),
                sm_sb=e2(nc.sbuf_tensor("p2_sm", [128, 544], F32)),
                pn_sb=e2(nc.sbuf_tensor("p2_pn", [128, 544], F32)),
                pt_sb=e2(nc.sbuf_tensor("p2_pt", [128, 768], BF16)),
                wpb=[e2(nc.sbuf_tensor(f"p2_wp{i}", [128, 16 * 128], BF16)) for i in range(2)],
                bcol=e2(nc.sbuf_tensor("p2_bc", [128, 1], F32)),
                den=[e2(nc.sbuf_tensor(f"p2_dn{i}", [128, 1], F32)) for i in range(2)],
                rcol=[e2(nc.sbuf_tensor(f"p2_rc{i}", [128, 1], F32)) for i in range(2)],
                small=e2(nc.sbuf_tensor("p2_sml", [1, 64], F32)),
                masks=idents, mown_sb=mown_sb, mprev_sb=mprev_sb,
                mpers_sb=mpers_sb)
            s_st2 = emit_att0(c, pool, "a",
                              lambda dc, t0, tsz: dsl(xA, dc, t0, tsz),
                              lambda dc, t0, tsz: dsl(xA, dc, 128 + t0, tsz),
                              lambda dc, t0, tsz: dsl(xB, dc, t0, tsz),
                              TLOC, wdr_att, bufs2)
        c.wait("sync", s_st2)

        with ExitStack() as st3:
            e3 = st3.enter_context
            bufs3 = dict(
                x_tb=e3(nc.sbuf_tensor("p3_x", [128, DC * 512], F32)),
                xn_sb=e3(nc.sbuf_tensor("p3_xn", [128, DC * 512], BF16)),
                w0b=[e3(nc.sbuf_tensor(f"p3_w0{i}", [128, DC * 128], BF16)) for i in range(2)],
                w1b=[e3(nc.sbuf_tensor(f"p3_w1{i}", [128, 16 * 128], BF16)) for i in range(4)],
                h_sb=e3(nc.sbuf_tensor("p3_h", [128, 64 * 512], BF16)),
                bias0c=e3(nc.sbuf_tensor("p3_b0", [128, 64], F32)),
                bias1r=e3(nc.sbuf_tensor("p3_b1", [1, D], F32)))
            s_bld = c.sem("fbld")
            c.dma("sync", bufs3["bias0c"][:], b0_d[:, :], sem=s_bld)
            c.dma("sync", bufs3["bias1r"][:], b1_d[:, :], sem=s_bld)
            for e in ("scalar", "tensor"):
                c.wait(e, s_bld)
            s_st3 = emit_ffn0_dram(c, pool, "f",
                                   lambda dc, t0, tsz: dsl(xB, dc, t0, tsz),
                                   lambda dc, t0, tsz: dsl(xC, dc, t0, tsz),
                                   TLOC, bufs3, wdr_ffn)

        wdr_wa = dict(waph=[wa_d[i] for i in range(48)])
        wdr_fl = dict(kg=[pb_out[r * 128:(r + 1) * 128, 0:16384] for r in range(8)],
                      vg=[pb_out[r * 128:(r + 1) * 128, 16384:32768] for r in range(8)],
                      kfl=pb_in[:, 0:16384], vtl=pb_in[:, 16384:32768],
                      wph=[wp_d[i] for i in range(16)])

        def l1_call(x_in, y_dst_ap, tag):
            with ExitStack() as sp:
                ep = sp.enter_context
                bufsp = dict(
                    fused_sb=ep(nc.sbuf_tensor(f"{tag}_fused", [128, 48 * TLOC], BF16)),
                    xn_sb=ep(nc.sbuf_tensor(f"{tag}_xn", [128, DC * TLOC], BF16)),
                    x_tb=ep(nc.sbuf_tensor(f"{tag}_x", [128, DC * 256], F32)),
                    stage=[ep(nc.sbuf_tensor(f"{tag}_st{i}", [128, 2048], BF16)) for i in range(2)],
                    wpb=[ep(nc.sbuf_tensor(f"{tag}_wp{i}", [128, DC * 128], BF16)) for i in range(2)],
                    vt32=[ep(nc.sbuf_tensor(f"{tag}_v32{i}", [128, 128], F32)) for i in range(2)],
                    colscale=colscale, swi=swi, swf=swf, sg=sgb, l1tmp=l1tmp,
                    wstd_row=wstd_row, wrec_row=wrec_row, s_wsin=s_cld,
                    masks=idents, nsq=pool.sq, xtb_cols=256)
                outs = dict(qf=qfA, kf=pb_in[:, 0:16384],
                            vtm=pb_in[:, 16384:32768], nrm=nr_in[:, :])
                s_pv, s_pq = emit_l1ffn_pub(c, pool, "pub",
                                            lambda dc, t0, tsz: dsl(x_in, dc, t0, tsz),
                                            TLOC, wdr_wa, bufsp, outs)
            c.wait("gpsimd", s_pv)
            c.wait("gpsimd", s_pq)
            s_cc = c.sem(f"cc{tag}g")
            s_cn = c.sem(f"cc{tag}n")
            p.gpsimd.collective_compute(
                "AllGather", ALU.bypass, replica_groups=[list(range(8))],
                ins=[pb_in_t.ap().opt()], outs=[pb_out_t.ap().opt()]).then_inc(s_cc.h)
            s_cc.n += 1
            p.gpsimd.collective_compute(
                "AllGather", ALU.bypass, replica_groups=[list(range(8))],
                ins=[nr_in_t.ap().opt()], outs=[nr_out_t.ap().opt()]).then_inc(s_cn.h)
            s_cn.n += 1
            with ExitStack() as sf:
                ef = sf.enter_context
                bufsf = dict(
                    qf_sb=ef(nc.sbuf_tensor(f"{tag}_qf", [128, 16 * 512], BF16)),
                    oacc=ef(nc.sbuf_tensor(f"{tag}_oa", [128, 16 * 512], F32)),
                    dacc=ef(nc.sbuf_tensor(f"{tag}_da", [128, 6 * 512], F32)),
                    kband=[ef(nc.sbuf_tensor(f"{tag}_kb{i}", [128, 16 * 512], BF16)) for i in range(2)],
                    vband=[ef(nc.sbuf_tensor(f"{tag}_vb{i}", [128, 4 * 2048], BF16)) for i in range(2)],
                    pt_sb=ef(nc.sbuf_tensor(f"{tag}_pt", [128, 8 * 512], BF16)),
                    sm_sb=ef(nc.sbuf_tensor(f"{tag}_sm", [128, 512], F32)),
                    bmat=ef(nc.sbuf_tensor(f"{tag}_bm", [128, 130], F32)),
                    bcol=ef(nc.sbuf_tensor(f"{tag}_bc", [128, 1], F32)),
                    nrm_sb=ef(nc.sbuf_tensor(f"{tag}_nr", [1, 20], F32)),
                    rec_row=ef(nc.sbuf_tensor(f"{tag}_rr", [1, 512], F32)),
                    x_tb=ef(nc.sbuf_tensor(f"{tag}_xt", [128, DC * 512], F32)),
                    wpb=[ef(nc.sbuf_tensor(f"{tag}_fw{i}", [128, DC * 128], BF16)) for i in range(2)],
                    tmask_sb=tmask_sb,
                    s_tm_in=s_cld)
                dm_t = ef(nc.sbuf_tensor(f"{tag}_dm", [128, 4 * 512], BF16))
                dmasks, s_dm = emit_l1ffn_diagmasks(c, "dm", dm_t)
                bufsf["dmasks"] = (dmasks, s_dm)
                bufsf["y_sb"] = bufsf["qf_sb"]
                s_nin = c.sem("flnr")
                c.wait("sync", s_cn)
                c.dma("sync", bufsf["nrm_sb"][0:1, 0:16],
                      nr_out.rearrange("a b -> (a b)"), sem=s_nin)
                c.dma("sync", bufsf["nrm_sb"][0:1, 16:18], nr_in[0:1, 0:2], sem=s_nin)
                bufsf["s_nrm_in"] = s_nin
                wo_gate = None
                for qs in range(TLOC // 512):
                    s_qf = c.sem("flqf")
                    if wo_gate is not None:
                        c.p.sync.wait_ge(wo_gate[0].h, wo_gate[1])
                    for h in range(HEADS):
                        c.dma("sync", bufsf["qf_sb"][:, h * 512:(h + 1) * 512],
                              qfA[:, h * TLOC + qs * 512: h * TLOC + qs * 512 + 512],
                              sem=s_qf)
                    bufsf["s_qf"] = s_qf
                    if qs == 0:
                        c.wait("sync", s_cc)
                    stt = emit_l1ffn_flash(c, pool, "fl", TLOC, wdr_fl, bufsf,
                                           qs, 8)
                    s_stf, wo_gate = emit_l1ffn_norm_wp(
                        c, pool, "wp",
                        lambda dc, t0, tsz: dsl(x_in, dc, t0, tsz),
                        lambda dc, t0, tsz: dsl(y_dst_ap, dc, t0, tsz),
                        TLOC, wdr_fl, bufsf, stt, qs, wo_gate)
            return s_stf

        s_stfA = l1_call(xC, xD, "A")

        s_rp = c.sem("rhpub")
        for dc in range(DC):
            c.dma("gpsimd", rh_in[:, dc * RH:(dc + 1) * RH],
                  dsl(xD, dc, TLOC - RH, RH), sem=s_rp)
        c.wait("gpsimd", s_rp)
        s_cr = c.sem("ccr")
        p.gpsimd.collective_compute(
            "AllGather", ALU.bypass, replica_groups=[list(range(8))],
            ins=[rh_in_t.ap().opt()], outs=[rh_out_t.ap().opt()]).then_inc(s_cr.h)
        s_cr.n += 1
        c.wait("sync", s_cr)
        with ExitStack() as sh:
            eh = sh.enter_context
            hacc = eh(nc.sbuf_tensor("hacc", [128, DC * RH], F32))
            hstg = eh(nc.sbuf_tensor("hstg", [128, DC * RH], F32))
            s_hl = c.sem("rhld")
            s_ha = c.sem("rhacc")
            p.vector.memset(hacc[:], 0.0).then_inc(s_ha.h, 1)
            s_ha.n += 1
            for r in range(8):
                c.wait("sync", s_ha)
                c.dma("sync", hstg[:], rh_out[r * 128:(r + 1) * 128, :], sem=s_hl)
                c.wait("vector", s_hl)
                p.vector.scalar_tensor_tensor(
                    out=hacc[:], in0=hstg[:], scalar=hsel_sb[:, r:r + 1],
                    in1=hacc[:], op0=ALU.mult, op1=ALU.add).then_inc(s_ha.h, 1)
                s_ha.n += 1
            c.wait("sync", s_ha)
            s_hw = c.sem("rhwr")
            for dc in range(DC):
                c.dma("sync", xh[dc * 128:(dc + 1) * 128, 0:RH],
                      hacc[:, dc * RH:(dc + 1) * RH], sem=s_hw)
            c.wait("sync", s_stfA)

        wdr_r = dict(wkh=[rk_d[i] for i in range(16)],
                     wvh=[rv_d[i] for i in range(16)],
                     wrh=[rr_d[i] for i in range(16)])

        def x_src_rwkv(dc, t0, tsz):
            out = []
            if t0 < RH:
                w = min(RH - t0, tsz)
                out.append((dsl(xh, dc, t0, w), 0, w))
            if t0 + tsz > RH:
                lo = max(t0, RH)
                out.append((dsl(xD, dc, lo - RH, t0 + tsz - lo), lo - t0,
                            t0 + tsz - lo))
            return out

        with ExitStack() as s5:
            e5 = s5.enter_context
            bufs5 = dict(
                ln_sb=e5(nc.sbuf_tensor("p5_ln", [128, DC * EXTR], BF16)),
                x_tb=e5(nc.sbuf_tensor("p5_x", [128, DC * 512], F32)),
                wpb3=[e5(nc.sbuf_tensor(f"p5_w{i}", [128, DC * 128], BF16)) for i in range(3)],
                ek=e5(nc.sbuf_tensor("p5_ek", [128, SCANW], F32)),
                hmk=e5(nc.sbuf_tensor("p5_hk", [128, SCANW], F32)),
                v=e5(nc.sbuf_tensor("p5_v", [128, SCANW], F32)),
                a=e5(nc.sbuf_tensor("p5_a", [128, SCANW], F32)),
                num=e5(nc.sbuf_tensor("p5_n", [128, SCANW], F32)),
                den=e5(nc.sbuf_tensor("p5_d", [128, SCANW], F32)),
                rec=e5(nc.sbuf_tensor("p5_r", [128, TLOC], F32)),
                wkv=e5(nc.sbuf_tensor("p5_wv", [128, TLOC], F32)),
                outb=e5(nc.sbuf_tensor("p5_o", [128, TLOC], F32)),
                r_sb=e5(nc.sbuf_tensor("p5_rs", [128, TLOC], F32)),
                hm_sb=e5(nc.sbuf_tensor("p5_hm", [128, SCANW], F32)),
                td_sb=td_sb,
                e1_sb=e5(nc.sbuf_tensor("p5_e1", [128, 16], F32)),
                w_sb=e5(nc.sbuf_tensor("p5_wd", [128, 16], F32)))
            s_hm = c.sem("hmld")
            c.dma("sync", bufs5["hm_sb"][:], hm_in[:, :], sem=s_hm)
            c.wait("scalar", s_cld)
            c.wait("vector", s_cld)
            c.wait("vector", s_hm)
            s_st5 = emit_rwkv(c, pool, "rw", x_src_rwkv,
                              lambda dc, t0, tsz: dsl(xD, dc, t0, tsz),
                              lambda dc, t0, tsz: dsl(xE, dc, t0, tsz),
                              TLOC, wdr_r, bufs5)
        c.wait("sync", s_st5)

        s_stfB = l1_call(xE, y_out, "B")
        barrier(s_stfB)

        with nc.Block() as block:
            c.p.replay(block)
    return nc


_NC_CACHE = {}
_USE_FUSED = True


def _get_nc(key, builder, *a, **kw):
    if key not in _NC_CACHE:
        _NC_CACHE[key] = builder(*a, **kw)
    return _NC_CACHE[key]


def _run(nc, in_maps):
    return run_bass_kernel_spmd(nc, in_maps, core_ids=list(range(NCORES)))


def kernel(L0_attn_x, L0_attn_Wq, L0_attn_Wkv, L0_attn_Wk, L0_attn_Wv,
           L0_attn_Wo, L0_ffn_W0, L0_ffn_bias0, L0_ffn_W1, L0_ffn_bias1,
           L1_attn_Wk, L1_attn_Wv, L1_attn_Wr, L1_attn_time_decay,
           L1_ffn_Wa, L1_ffn_Wp, L1_ffn_w_std, L1_ffn_w_rec):
    TLOC = S // NCORES
    x0 = np.asarray(L0_attn_x, dtype=np.float32).reshape(S, D)
    xT = np.ascontiguousarray(x0.T)

    w0h, w1q, b0c, b1r = prep_ffn0_weights(L0_ffn_W0, L0_ffn_bias0,
                                           L0_ffn_W1, L0_ffn_bias1)
    wqh, wkvh, wkh, wvh, woh = prep_att0_weights(L0_attn_Wq, L0_attn_Wkv,
                                                 L0_attn_Wk, L0_attn_Wv, L0_attn_Wo)
    rkh, rvh, rrh, tdc = prep_rwkv_weights(L1_attn_Wk, L1_attn_Wv, L1_attn_Wr,
                                           L1_attn_time_decay)
    waph, wph = prep_l1ffn_weights(L1_ffn_Wa, L1_ffn_Wp)
    wstd = f32c(np.asarray(L1_ffn_w_std).reshape(1, 16))
    wrec = f32c(np.asarray(L1_ffn_w_rec).reshape(1, 16))

    def split(xt):
        return [np.ascontiguousarray(xt[:, c * TLOC:(c + 1) * TLOC])
                for c in range(NCORES)]

    def cat(res, key="y"):
        return np.concatenate([res.results[c][key] for c in range(NCORES)], axis=1)

    def run_ffn0(xt):
        nc = _get_nc("ffn0", build_ffn0_launch, TLOC)
        maps = [{"x": s, "w0h": w0h, "w1q": w1q, "b0c": b0c, "b1r": b1r}
                for s in split(xt)]
        return cat(_run(nc, maps))

    def run_att0(xt):
        nc = _get_nc("att0", build_att0_launch, TLOC)
        maps = []
        for c in range(NCORES):
            t0 = c * TLOC
            xe = np.zeros((D, 128 + TLOC + 128), np.float32)
            if c > 0:
                xe[:, 0:128] = xt[:, t0 - 128:t0]
            xe[:, 128:128 + TLOC] = xt[:, t0:t0 + TLOC]
            xe[:, 128 + TLOC:128 + TLOC + 16] = xt[:, 0:16]
            mown, mprev, mpers = make_att0_masks_host(8 * c, TLOC // 128)
            maps.append({"xe": xe, "wqh": wqh, "wkvh": wkvh, "wkh": wkh,
                         "wvh": wvh, "woh": woh, "mown": mown, "mprev": mprev,
                         "mpers": mpers})
        return cat(_run(nc, maps))

    def run_rwkv(xt):
        nc = _get_nc("rwkv", build_rwkv_launch, TLOC)
        SCANW = 256 + TLOC
        maps = []
        for c in range(NCORES):
            t0 = c * TLOC
            xe = np.zeros((D, 264 + TLOC), np.float32)
            if c > 0:
                xe[:, 0:264] = xt[:, t0 - 264:t0]
            xe[:, 264:] = xt[:, t0:t0 + TLOC]
            hm = np.ones((128, SCANW), np.float32)
            if c == 0:
                hm[:, 0:256] = 0.0
            maps.append({"xe": xe, "wkh": rkh, "wvh": rvh, "wrh": rrh,
                         "td": tdc, "hm": hm})
        return cat(_run(nc, maps))

    def run_l1ffn(xt):
        nc_a = _get_nc("l1pub", build_l1pub_launch, TLOC)
        maps = [{"xe": s, "waph": waph, "wstd": wstd, "wrec": wrec}
                for s in split(xt)]
        ra = _run(nc_a, maps)
        kg = np.ascontiguousarray(np.stack([ra.results[c]["kf"] for c in range(NCORES)]))
        vg = np.ascontiguousarray(np.stack([ra.results[c]["vtm"] for c in range(NCORES)]))
        nc_b = _get_nc("l1flash", build_l1flash_launch, TLOC, (8, 8))
        maps_b = []
        xs = split(xt)
        for c in range(NCORES):
            nrmv = np.zeros((1, 20), np.float32)
            for j in range(NCORES):
                nrmv[0, 2 * j] = ra.results[j]["nrm"][0, 0]
                nrmv[0, 2 * j + 1] = ra.results[j]["nrm"][0, 1]
            nrmv[0, 16] = ra.results[c]["nrm"][0, 0]
            nrmv[0, 17] = ra.results[c]["nrm"][0, 1]
            tm = make_l1_tmask_host((2 * c, 2 * c + 1))
            maps_b.append({"xe": xs[c], "qf": ra.results[c]["qf"], "kg": kg,
                           "vg": vg, "kfl": ra.results[c]["kf"],
                           "vtl": ra.results[c]["vtm"], "nrmv": nrmv,
                           "tmask": tm, "wph": wph})
        return cat(_run(nc_b, maps_b))

    if _USE_FUSED:
        EXTA = 128 + TLOC + 128
        SCANW = 256 + TLOC
        nc = _get_nc("fused", build_fused_launch, TLOC)
        maps = []
        for c in range(NCORES):
            t0 = c * TLOC
            xe0 = np.zeros((D, EXTA), np.float32)
            if c > 0:
                xe0[:, 0:128] = xT[:, t0 - 128:t0]
            xe0[:, 128:128 + TLOC] = xT[:, t0:t0 + TLOC]
            xe0[:, 128 + TLOC:128 + TLOC + 16] = xT[:, 0:16]
            mown, mprev, mpers = make_att0_masks_host(8 * c, TLOC // 128)
            hm = np.ones((128, SCANW), np.float32)
            if c == 0:
                hm[:, 0:256] = 0.0
            hsel = np.zeros((128, 8), np.float32)
            if c > 0:
                hsel[:, c - 1] = 1.0
            tm = make_l1_tmask_host((2 * c, 2 * c + 1))
            maps.append({"xe0": xe0, "w0h": w0h, "w1q": w1q, "b0c": b0c,
                         "b1r": b1r, "wqh": wqh, "wkvh": wkvh, "wkh": wkh,
                         "wvh": wvh, "woh": woh, "mown": mown, "mprev": mprev,
                         "mpers": mpers, "rkh": rkh, "rvh": rvh, "rrh": rrh,
                         "td": tdc, "hm": hm, "waph": waph, "wph": wph,
                         "wstd": wstd, "wrec": wrec, "tmask": tm, "hsel": hsel})
        res = _run(nc, maps)
        x6 = cat(res)
        return np.ascontiguousarray(x6.T).reshape(1, S, D).astype(np.float32)

    x1 = run_ffn0(xT)
    x2 = run_att0(x1)
    x3 = run_ffn0(x2)
    x4 = run_l1ffn(x3)
    x5 = run_rwkv(x4)
    x6 = run_l1ffn(x5)
    return np.ascontiguousarray(x6.T).reshape(1, S, D).astype(np.float32)
